# version 23
# baseline (speedup 1.0000x reference)
import contextlib
import os as _os
import numpy as np
import concourse.bass as bass
import concourse.mybir as mybir
from concourse.bass_utils import run_bass_kernel_spmd
from concourse.alu_op_type import AluOpType

F32 = mybir.dt.float32
AF = mybir.ActivationFunctionType
OP = AluOpType
AX = mybir.AxisListType

NCORE = 8
T_TOT, D, M = 16384, 2048, 64
if _os.environ.get("KERNEL_SMALL") == "1":
    T_TOT = 8192
SEG = T_TOT // NCORE
NT = SEG // 128
NB = SEG // 512
ESL = D // NCORE
SCALE = float(1.0 / np.sqrt(M))
C1 = float(0.5 / np.sqrt(8.0))
EPS = 1e-5
HFS = 260


class Ctr:
    def __init__(self, handle):
        self.h = handle
        self.cnt = 0


class _FakeIns:
    def then_inc(self, h, k):
        return self


class _Prox:

    def __init__(self, name):
        self.name = name
        self.real = None

    def wait(self, ctr, val=None):
        v = ctr.cnt if val is None else val
        if self.real is not None and v > 0:
            self.real.wait_ge(ctr.h, v)

    def __getattr__(self, attr):
        def call(*a, _attr=attr, inc=None, k=1, **kw):
            if self.real is not None:
                ins = getattr(self.real, _attr)(*a, **kw)
            else:
                ins = _FakeIns()
            if inc is not None:
                ins.then_inc(inc.h, k)
                inc.cnt += k
            return ins
        return call


def build_nc():
    nc = bass.Bass("TRN2", target_bir_lowering=False)

    xin = nc.dram_tensor("xin", [(NT + 1) * 128, D], F32, kind="ExternalInput").ap()
    fsl = nc.dram_tensor("fsl", [8, D, ESL], F32, kind="ExternalInput").ap()
    ipT = nc.dram_tensor("ipT", [D, 128], F32, kind="ExternalInput").ap()
    mgT = nc.dram_tensor("mgT", [D, 128], F32, kind="ExternalInput").ap()
    ch0T = nc.dram_tensor("ch0T", [D, 128], F32, kind="ExternalInput").ap()
    ch1rT = nc.dram_tensor("ch1rT", [128, D], F32, kind="ExternalInput").ap()
    convw = nc.dram_tensor("convw", [D, 4], F32, kind="ExternalInput").ap()
    convb = nc.dram_tensor("convb", [D], F32, kind="ExternalInput").ap()
    mgb = nc.dram_tensor("mgb", [64, 2], F32, kind="ExternalInput").ap()
    resw = nc.dram_tensor("resw", [D], F32, kind="ExternalInput").ap()
    ch1b = nc.dram_tensor("ch1b", [D], F32, kind="ExternalInput").ap()
    fbias = nc.dram_tensor("fbias", [D], F32, kind="ExternalInput").ap()
    ident_i = nc.dram_tensor("identi", [128, 128], F32, kind="ExternalInput").ap()
    mask32_i = nc.dram_tensor("mask32i", [128, 32], F32, kind="ExternalInput").ap()
    ones_i = nc.dram_tensor("onesi", [128, 512], F32, kind="ExternalInput").ap()
    mj_i = nc.dram_tensor("mji", [8, 8], F32, kind="ExternalInput").ap()
    sel_i = nc.dram_tensor("seli", [8, 1], F32, kind="ExternalInput").ap()

    yout = nc.dram_tensor("yout", [SEG, D], F32, kind="ExternalOutput").ap()

    cc_in = nc.dram_tensor("cc_in", [D, ESL], F32).ap()
    cc_out = nc.dram_tensor("cc_out", [NCORE * D, ESL], F32).ap()
    w_hbm = nc.dram_tensor("w_hbm", [64 * 128, 512], F32).ap()
    xmid_hbm = nc.dram_tensor("xmid_hbm", [(NT + 1) * 128, D], F32).ap()
    ypart_hbm = nc.dram_tensor("ypart_hbm", [SEG, D], F32).ap()
    u_hbm = nc.dram_tensor("u_hbm", [SEG, D], F32).ap()
    z_hbm = nc.dram_tensor("z_hbm", [SEG, D], F32).ap()
    r_hbm = nc.dram_tensor("r_hbm", [M, SEG], F32).ap()
    k_hbm = nc.dram_tensor("k_hbm", [M, SEG], F32).ap()
    g_hbm = nc.dram_tensor("g_hbm", [M, SEG], F32).ap()
    sc_in = nc.dram_tensor("sc_in", [M, 2049], F32).ap()
    sc_out = nc.dram_tensor("sc_out", [NCORE * M, 2049], F32).ap()

    es = contextlib.ExitStack()
    with es:
        AR = 47616
        arena = es.enter_context(nc.sbuf_tensor("arena", [128, AR], F32)).ap()

        pos = [0]

        def A_(n):
            p = pos[0]
            pos[0] += n
            assert pos[0] <= AR, f"arena overflow {pos[0]}"
            return p

        p_ipT = A_(2048); p_mgT = A_(2048); p_ch0T = A_(2048); p_ch1rT = A_(2048)
        p_convw = A_(64); p_convb = A_(16); p_resw = A_(16); p_mgb = A_(2)
        p_ident = A_(128); p_mask = A_(32); p_ones = A_(512)
        p_acar = A_(4); p_zero = A_(1); p_eps = A_(1)
        p_stat = A_(64); p_small = A_(64)

        stage0 = pos[0]

        def stage_alloc(sizes):
            out, q = [], stage0
            for n in sizes:
                out.append(q)
                q += n
            assert q <= AR, f"stage arena overflow {q}"
            return out

        (s1_Xa, s1_Xb, s1_LFa, s1_LFb, s1_LT, s1_W, s1_hbb,
         s1_HF, s1_SH, s1_U,
         s1_k0, s1_sel, s1_tau, s1_it, s1_dif, s1_an, s1_ch, s1_srw,
         s1_gacc, s1_ch1bb) = stage_alloc(
            [2 * 2048, 2 * 2048, 2 * 2048, 2 * 2048, 2048, 8 * 512, 2048,
             16 * 132, 3 * 2048, 2048,
             128, 128, 128, 128, 128, 128, 128, 4 * 128,
             128, 2048])
        s1_WS = s1_LFa
        (s2_U, s2_Z, s2_a, s2_r, s2_k, s2_g,
         s2_T1, s2_T2, s2_T3, s2_T4, s2_T5, s2_dif,
         s2_P, s2_kE, s2_kp, s2_anch,
         s2_S, s2_Sb) = stage_alloc(
            [4 * 2048, 4 * 2048, 2 * 512, 2 * 512, 2 * 512, 2 * 512,
             2 * 512, 2 * 512, 2 * 512, 2 * 512, 2 * 512, 2 * 512,
             128, 64, 2 * 192, 2 * 384,
             2048, 2048])
        (s4_Y, s4_LF, s4_LT, s4_W, s4_L, s4_A, s4_w8, s4_wT,
         s4_Sst, s4_Sst2, _s4pad, s4_hbb, p_rhat) = stage_alloc(
            [4 * 2048, 4 * 2048, 2048, 8 * 512, 2048, 64, 64, 8,
             2048, 2048, 3256, 2048, 2048])
        assert s4_hbb >= s2_Sb + 2048

        globals()["LAYOUT"] = {k: v for k, v in locals().items()
                               if isinstance(v, int)}
        pst = [es.enter_context(nc.psum_tensor(f"ps{i}", [128, 512], F32)).ap()
               for i in range(7)]
        psPf = es.enter_context(nc.psum_tensor("psP", [128, 512], F32)).ap()
        psP = psPf[:, 0:128]
        psP2 = psPf[:, 128:256]

        c_ld = Ctr(es.enter_context(nc.semaphore("c_ld")))
        c_pre = Ctr(es.enter_context(nc.semaphore("c_pre")))
        c_x = Ctr(es.enter_context(nc.semaphore("c_x")))
        c_wa = Ctr(es.enter_context(nc.semaphore("c_wa")))
        c_wb = Ctr(es.enter_context(nc.semaphore("c_wb")))
        c_xm = Ctr(es.enter_context(nc.semaphore("c_xm")))
        c_misc = Ctr(es.enter_context(nc.semaphore("c_misc")))
        c_sc = Ctr(es.enter_context(nc.semaphore("c_sc")))
        c_y = Ctr(es.enter_context(nc.semaphore("c_y")))
        c_sp = Ctr(es.enter_context(nc.semaphore("c_sp")))
        c_gp = Ctr(es.enter_context(nc.semaphore("c_gp")))
        c_gd = Ctr(es.enter_context(nc.semaphore("c_gd")))
        c_cc = Ctr(es.enter_context(nc.semaphore("c_cc")))
        c_pe = Ctr(es.enter_context(nc.semaphore("c_pe")))
        c_ve = Ctr(es.enter_context(nc.semaphore("c_ve")))
        c_ac = Ctr(es.enter_context(nc.semaphore("c_ac")))
        c_st = Ctr(es.enter_context(nc.semaphore("c_st")))
        ctrs = [c_ld, c_pre, c_x, c_wa, c_wb, c_xm, c_misc, c_sc, c_y, c_sp, c_gp, c_gd, c_cc, c_pe, c_ve, c_ac, c_st]
        block = es.enter_context(nc.Block())

        def ar(col, ncol, p0=0, np_=128):
            return arena[p0:p0 + np_, col:col + ncol]

        Sx = _Prox("sync"); V = _Prox("vector"); Tn = _Prox("tensor")
        Ac = _Prox("scalar"); G = _Prox("gpsimd")

        def schedule():
            Sx.dma_start(out=ar(p_ipT, 2048).rearrange("p (dc r) -> p dc r", r=128),
                         in_=ipT.rearrange("(dc p) r -> p dc r", p=128),
                         inc=c_ld, k=16)
            Sx.dma_start(out=ar(p_mgT, 2048).rearrange("p (dc r) -> p dc r", r=128),
                         in_=mgT.rearrange("(dc p) r -> p dc r", p=128),
                         inc=c_ld, k=16)
            Sx.dma_start(out=ar(p_ch0T, 2048).rearrange("p (dc r) -> p dc r", r=128),
                         in_=ch0T.rearrange("(dc p) r -> p dc r", p=128),
                         inc=c_ld, k=16)
            Sx.dma_start(out=ar(p_ch1rT, 2048), in_=ch1rT, inc=c_ld, k=16)
            Sx.dma_start(out=ar(p_convw, 64).rearrange("p (dc w) -> p dc w", w=4),
                         in_=convw.rearrange("(dc p) w -> p dc w", p=128),
                         inc=c_ld, k=16)
            Sx.dma_start(out=ar(p_convb, 16),
                         in_=convb.rearrange("(dc p) -> p dc", p=128),
                         allow_slow_non_contiguous=True, inc=c_ld, k=16)
            Sx.dma_start(out=ar(p_resw, 16),
                         in_=resw.rearrange("(dc p) -> p dc", p=128),
                         allow_slow_non_contiguous=True, inc=c_ld, k=16)
            Sx.dma_start(out=ar(p_mgb, 2, np_=64), in_=mgb, inc=c_ld, k=16)
            Sx.dma_start(out=ar(p_ident, 128), in_=ident_i, inc=c_ld, k=16)
            Sx.dma_start(out=ar(p_mask, 32), in_=mask32_i, inc=c_ld, k=16)
            Sx.dma_start(out=ar(p_ones, 512), in_=ones_i, inc=c_ld, k=16)
            Sx.dma_start(out=ar(p_small, 8, np_=8), in_=mj_i, inc=c_ld, k=16)
            Sx.dma_start(out=ar(p_small + 8, 1, np_=8), in_=sel_i, inc=c_ld, k=16)
            hb_row = ar(s1_LT, 2048, np_=1)
            cb_row = hb_row
            Sx.dma_start(out=hb_row, in_=fbias.rearrange("(one x) -> one x", one=1),
                         inc=c_ld, k=16)
            ld_all = c_ld.cnt

            Ac.wait(c_ld, ld_all)
            Ac.activation(hb_row, hb_row, AF.Copy, scale=0.5, inc=c_ac)
            Tn.wait(c_ac)
            Tn.wait(c_ld, ld_all)

            def bcast_row(src_row, dst):
                for j in range(4):
                    Tn.wait(c_ve)
                    Tn.matmul(pst[0], lhsT=ar(p_ones, 128, np_=1),
                              rhs=src_row[:, 512 * j:512 * (j + 1)],
                              start=True, stop=True, inc=c_pe)
                    V.wait(c_pe)
                    V.tensor_copy(ar(dst + 512 * j, 512), pst[0], inc=c_ve)

            bcast_row(hb_row, s1_hbb)
            V.memset(ar(p_zero, 1), 0.0, inc=c_ve)
            V.memset(ar(p_eps, 1), EPS, inc=c_ve)
            V.memset(ar(p_acar, 1, np_=64), 0.0, inc=c_ve)
            consts_done = c_ve.cnt
            eps_t = ar(p_eps, 1)
            zero_t = ar(p_zero, 1)

            for rr in range(8):
                G.wait(c_gd)
                kw = {} if rr == 0 else dict(accum_op=OP.add)
                G.dma_start(out=ar(s1_WS, 4096).rearrange(
                                "p (dc x) -> p dc x", x=ESL),
                            in_=fsl[rr].rearrange("(dc p) x -> p dc x", p=128),
                            inc=c_gd, k=16, **kw)
            G.wait(c_gd)
            G.dma_start(out=cc_in.rearrange("(dc p) x -> p dc x", p=128),
                        in_=ar(s1_WS, 4096).rearrange(
                            "p (dc x) -> p dc x", x=ESL),
                        inc=c_gd, k=16)
            G.wait(c_gd)
            ws_gd = c_gd.cnt
            G.collective_compute("AllGather", OP.bypass,
                                 replica_groups=[list(range(NCORE))],
                                 ins=[cc_in.opt()], outs=[cc_out.opt()],
                                 inc=c_cc, k=1)
            ag_cc = c_cc.cnt



            def ln_stats_apply(xap, out_ap, cfold, st):
                V.drain()
                for j in range(4):
                    V.bn_stats(ar(st + 6 * j, 6), xap[:, 512 * j:512 * (j + 1)])
                V.drain()
                V.bn_aggr(ar(st + 24, 2), ar(st, 24), inc=c_ve)
                Ac.wait(c_ve)
                Ac.drain()
                Ac.activation(ar(st + 26, 1), ar(st + 25, 1), AF.Ln,
                              bias=eps_t, scale=1.0)
                Ac.drain()
                Ac.activation(ar(st + 26, 1), ar(st + 26, 1), AF.Exp,
                              scale=-0.5, inc=c_ac)
                V.wait(c_ac)
                V.drain()
                V.tensor_scalar(ar(st + 26, 1), ar(st + 26, 1), cfold, None,
                                op0=OP.mult, inc=c_ve)
                V.drain()
                V.tensor_scalar(ar(st + 27, 1), ar(st + 24, 1),
                                ar(st + 26, 1), -1.0,
                                op0=OP.mult, op1=OP.mult, inc=c_ve)
                Ac.wait(c_ve)
                Ac.drain()
                Ac.activation(out_ap, xap, AF.Identity,
                              bias=ar(st + 27, 1), scale=ar(st + 26, 1),
                              inc=c_ac)

            def transpose_tile(src_col, dst_col, dst_stride, dst_off, nsub=16):
                Tn.wait(c_ac)
                evac_done = [0, 0]
                for dq in range(nsub // 4):
                    pt = pst[4] if dq % 2 == 0 else pst[6]
                    Tn.wait(c_ve, evac_done[dq % 2])
                    for dj in range(4):
                        dc = 4 * dq + dj
                        Tn.transpose(pt[:, 128 * dj:128 * (dj + 1)],
                                     ar(src_col + 128 * dc, 128),
                                     ar(p_ident, 128),
                                     inc=c_pe if dj == 3 else None)
                    V.wait(c_pe)
                    base = dst_col + dst_stride * 4 * dq + dst_off
                    if dst_stride == 128:
                        V.tensor_copy(ar(base, 512), pt, inc=c_ve)
                    else:
                        lo = dst_col + dst_stride * 4 * dq
                        V.tensor_copy(
                            arena[:, lo:lo + 4 * dst_stride].rearrange(
                                "p (g x) -> p g x", x=dst_stride)[
                                :, :, dst_off:dst_off + 128],
                            pt.rearrange("p (g x) -> p g x", x=128),
                            inc=c_ve)
                    evac_done[dq % 2] = c_ve.cnt

            ones64 = ar(p_ones, 512, np_=64)
            Sx.wait(c_pe)
            Sx.wait(c_misc)
            Sx.dma_start(out=cb_row,
                         in_=ch1b.rearrange("(one x) -> one x", one=1),
                         inc=c_misc, k=16)
            Tn.wait(c_misc)
            bcast_row(cb_row, s1_ch1bb)

            srw_last = [0]
            z_evac = [0, 0]
            pst_v_free = [0]
            pspfree = [0]
            psp2free = [0]
            X_ = [s1_Xa, s1_Xb]
            LF_ = [s1_LFa, s1_LFb]
            xfree = [[0, 0], [0, 0]]
            lffree = [0, 0]
            wfree = [0, 0]
            ffn_evac = [0, 0]
            wdma = {}
            tst = {}
            wq = cc_out.rearrange("(e2 i2 dc p) xi -> e2 p dc i2 xi",
                                  e2=4, i2=2, p=128)

            def w_issue(d):
                if d >= 16:
                    return
                Sx.wait(c_cc, ag_cc)
                Sx.wait(c_pe, wfree[d % 2])
                cwx = c_wa if d % 2 == 0 else c_wb
                e = d // 4
                dc0 = (4 * d) % 16
                for i2 in range(2):
                    Sx.dma_start(
                        out=ar(s1_W + 2048 * (d % 2), 2048).rearrange(
                            "p (c xi) -> p c xi", xi=512
                            )[:, :, 256 * i2:256 * i2 + 256],
                        in_=wq[e][:, dc0:dc0 + 4, i2],
                        inc=cwx, k=16)
                wdma[d] = (cwx, cwx.cnt)

            def post_a(sl, gi, tl):
                rt = tl - 1
                xm = ar(X_[sl] + 2048 * gi, 2048)
                G.wait(c_ve)
                G.tensor_tensor(xm, xm, ar(s1_hbb, 2048), op=OP.add, inc=c_gp)
                V.wait(c_gp)
                Ac.wait(c_gp)
                Ac.wait(c_pe)
                ln_stats_apply(xm, ar(s1_LT, 2048), 1.0, p_stat)
                V.wait(c_ac)
                V.wait(c_gp)
                V.drain()
                if tl == 1:
                    V.tensor_copy(
                        arena[:, s1_HF:s1_HF + 16 * 132].rearrange(
                            "p (g x) -> p g x", x=132)[:, :, 1:4],
                        arena[:, s1_HF:s1_HF + 16 * 132].rearrange(
                            "p (g x) -> p g x", x=132)[:, :, 4:7], inc=c_ve)
                elif tl > 1:
                    V.tensor_copy(
                        arena[:, s1_HF:s1_HF + 16 * 132].rearrange(
                            "p (g x) -> p g x", x=132)[:, :, 1:4],
                        arena[:, s1_HF:s1_HF + 16 * 132].rearrange(
                            "p (g x) -> p g x", x=132)[:, :, 129:132],
                        inc=c_ve)
                carry_ve = c_ve.cnt
                Tn.wait(c_ve, carry_ve)
                transpose_tile(s1_LT, s1_HF, 132, 4)
                if tl == 0:
                    return
                hf_ve = c_ve.cnt
                Tn.wait(c_ve, hf_ve)
                for dc in range(16):
                    Tn.matmul(pst[5][:, 0:128],
                              lhsT=ar(p_ch0T + 128 * dc, 128),
                              rhs=ar(s1_HF + 132 * dc + 4, 128),
                              start=(dc == 0), stop=(dc == 15),
                              inc=c_pe if dc == 15 else None)
                V.wait(c_pe)
                V.drain()
                V.tensor_copy(ar(s1_ch, 128), pst[5][:, 0:128], inc=c_ve)
                ch_ve = c_ve.cnt
                Tn.wait(c_ve, ch_ve)
                pr_ve = ffn_evac[1]
                for pr in range(2):
                    Tn.wait(c_ve, pr_ve)
                    for j in range(2):
                        e = 2 * pr + j
                        Tn.matmul(pst[2 + j], lhsT=ar(s1_ch, 128),
                                  rhs=ar(p_ch1rT + 512 * e, 512),
                                  start=True, stop=True,
                                  inc=c_pe if j == 1 else None)
                    V.wait(c_pe)
                    if pr == 0:
                        V.wait(c_sp)
                    V.drain()
                    for j in range(2):
                        e = 2 * pr + j
                        V.scalar_tensor_tensor(
                            ar(s1_U + 512 * e, 512), pst[2 + j], 1.0,
                            ar(s1_ch1bb + 512 * e, 512),
                            op0=OP.mult, op1=OP.add,
                            inc=c_ve if j == 1 else None)
                    pr_ve = c_ve.cnt
                pst_v_free[0] = c_ve.cnt
                Ac.wait(c_ve)
                Ac.drain()
                for e in range(4):
                    Ac.activation(ar(s1_U + 512 * e, 512),
                                  ar(s1_U + 512 * e, 512), AF.Sigmoid,
                                  inc=c_ac if e == 3 else None)
                V.wait(c_ac)
                V.drain()
                for e in range(4):
                    V.tensor_tensor(ar(s1_U + 512 * e, 512),
                                    ar(s1_U + 512 * e, 512),
                                    ar(s1_LT + 512 * e, 512),
                                    op=OP.mult, inc=c_ve if e == 3 else None)
                u_ve = c_ve.cnt
                G.wait(c_ve, u_ve)
                G.wait(c_sp)
                G.dma_start(out=u_hbm[128 * rt:128 * (rt + 1), :],
                            in_=ar(s1_U, 2048), inc=c_sp, k=16)
                u_sp = c_sp.cnt
                sh = s1_SH + 2048 * (tl % 3)
                V.wait(c_ac)
                silu_done = [0, 0]
                for dc in range(8):
                    hb_ = s1_HF + 132 * dc
                    pc = pst[4] if dc % 2 == 0 else pst[6]
                    V.wait(c_ac, silu_done[dc % 2])
                    V.wait(c_pe)
                    V.drain()
                    V.tensor_scalar(pc[:, 0:128], ar(hb_ + 1, 128),
                                    ar(p_convw + 4 * dc, 1), None,
                                    op0=OP.mult, inc=c_ve)
                    for i in range(1, 4):
                        V.drain()
                        V.scalar_tensor_tensor(
                            pc[:, 0:128], ar(hb_ + 1 + i, 128),
                            ar(p_convw + 4 * dc + i, 1), pc[:, 0:128],
                            op0=OP.mult, op1=OP.add, inc=c_ve)
                    Ac.wait(c_ve)
                    Ac.activation(ar(sh + 128 * dc, 128), pc[:, 0:128],
                                  AF.Silu, bias=ar(p_convb + dc, 1), scale=1.0,
                                  inc=c_ac)
                    silu_done[dc % 2] = c_ac.cnt
                G.wait(c_pe)
                for dc in range(8, 16):
                    hb_ = s1_HF + 132 * dc
                    G.wait(c_ac)
                    G.drain()
                    G.tensor_scalar(ar(s1_gacc, 128), ar(hb_ + 1, 128),
                                    ar(p_convw + 4 * dc, 1), zero_t,
                                    op0=OP.mult, op1=OP.add, inc=c_gp)
                    for i in range(1, 4):
                        G.drain()
                        G.tensor_scalar(ar(s1_an, 128), ar(hb_ + 1 + i, 128),
                                        ar(p_convw + 4 * dc + i, 1), zero_t,
                                        op0=OP.mult, op1=OP.add, inc=c_gp)
                        G.drain()
                        G.tensor_tensor(ar(s1_gacc, 128), ar(s1_gacc, 128),
                                        ar(s1_an, 128), op=OP.add, inc=c_gp)
                    Ac.wait(c_gp)
                    Ac.drain()
                    Ac.activation(ar(sh + 128 * dc, 128), ar(s1_gacc, 128),
                                  AF.Silu, bias=ar(p_convb + dc, 1), scale=1.0,
                                  inc=c_ac)
                tst[(sl, gi)] = (c_ac.cnt, u_sp)

            def post_b(sl, gi, tl):
                if tl == 0:
                    return
                rt = tl - 1
                silu_ac, u_sp = tst[(sl, gi)]
                sh = s1_SH + 2048 * (tl % 3)
                Tn.wait(c_ac, max(silu_ac, psp2free[0]))
                Tn.wait(c_ve, pspfree[0])

                def gate_mm(ps, wcol, mov_base):
                    for dc in range(16):
                        Tn.matmul(ps, lhsT=ar(wcol + 128 * dc, 128),
                                  rhs=ar(mov_base + 128 * dc, 128),
                                  start=(dc == 0), stop=(dc == 15),
                                  inc=c_pe if dc == 15 else None)

                gate_mm(psP[0:128, 0:128], p_ipT, sh)
                gate_mm(psP2[0:128, 0:128], p_mgT, sh)
                gates_pe = c_pe.cnt
                V.wait(c_pe, gates_pe)
                V.wait(c_sp)
                V.drain()
                V.tensor_copy(ar(s1_dif, 128, np_=64), psP[0:64, 0:128],
                              inc=c_ve)
                V.tensor_copy(ar(s1_k0, 128, np_=64), psP[64:128, 0:128],
                              inc=c_ve)
                r_ve = c_ve.cnt
                G.wait(c_ve, r_ve)
                G.dma_start(out=r_hbm[:, 128 * rt:128 * (rt + 1)],
                            in_=ar(s1_dif, 128, np_=64), inc=c_sp, k=16)
                sb_ = [s1_srw, s1_U]
                scale_gp = [0, 0, 0, 0]
                tpe = [0, 0, 0, 0]

                def srw_scales(dq):
                    G.wait(c_ac, silu_ac)
                    if dq < 2:
                        G.wait(c_pe, srw_last[0])
                    else:
                        G.wait(c_pe, tpe[dq - 2])
                    if dq % 2 == 1:
                        G.wait(c_sp, u_sp)
                    G.drain()
                    for dj in range(4):
                        dc = 4 * dq + dj
                        G.tensor_scalar(ar(sb_[dq % 2] + 128 * dj, 128),
                                        ar(sh + 128 * dc, 128),
                                        ar(p_resw + dc, 1), zero_t,
                                        op0=OP.mult, op1=OP.add, inc=c_gp)
                    scale_gp[dq] = c_gp.cnt

                srw_scales(0)
                srw_scales(1)
                Ac.wait(c_pe, gates_pe)
                Ac.wait(c_ve, r_ve)
                Ac.drain()
                Ac.activation(ar(s1_sel, 128, np_=64), psP2[0:64, 0:128],
                              AF.Exp, bias=ar(p_mgb, 1, np_=64), scale=1.0)
                Ac.drain()
                Ac.activation(ar(s1_sel, 128, np_=64), ar(s1_sel, 128, np_=64),
                              AF.Ln, bias=ones64[:, 0:1], scale=1.0)
                Ac.drain()
                Ac.activation(ar(s1_it, 128, np_=64), ar(s1_sel, 128, np_=64),
                              AF.Ln)
                Ac.activation(ar(s1_tau, 128, np_=64), psP2[64:128, 0:128],
                              AF.Sigmoid, bias=ar(p_mgb + 1, 1, np_=64),
                              scale=1.0, inc=c_ac)
                psp2free[0] = c_ac.cnt
                selit_ac = c_ac.cnt
                for dq in range(2):
                    pz = pst[4] if dq % 2 == 0 else pst[6]
                    Tn.wait(c_gp, scale_gp[dq])
                    Tn.wait(c_ve, z_evac[dq % 2])
                    for dj in range(4):
                        Tn.transpose(pz[:, 128 * dj:128 * (dj + 1)],
                                     ar(sb_[dq % 2] + 128 * dj, 128),
                                     ar(p_ident, 128),
                                     inc=c_pe if dj == 3 else None)
                    tpe[dq] = c_pe.cnt
                    srw_scales(dq + 2)
                V.wait(c_ac, selit_ac)
                V.drain()
                V.tensor_tensor(ar(s1_it, 128, np_=64), ar(s1_it, 128, np_=64),
                                ar(s1_tau, 128, np_=64), op=OP.mult, inc=c_ve)
                V.wait(c_sp)
                V.drain()
                V.scalar_tensor_tensor(ar(s1_dif, 128, np_=64),
                                       ar(s1_sel, 128, np_=64), -1.0,
                                       ar(s1_tau, 128, np_=64),
                                       op0=OP.mult, op1=OP.mult, inc=c_ve)
                it_ve = c_ve.cnt
                g_ve = c_ve.cnt
                G.wait(c_ve, g_ve)
                G.dma_start(out=g_hbm[:, 128 * rt:128 * (rt + 1)],
                            in_=ar(s1_dif, 128, np_=64), inc=c_sp, k=16)
                gsp = c_sp.cnt

                def z_stt(dq):
                    pz = pst[4] if dq % 2 == 0 else pst[6]
                    V.wait(c_pe, tpe[dq])
                    V.drain()
                    V.scalar_tensor_tensor(
                        ar(X_[sl] + 2048 * gi + 512 * dq, 512), pz, 1.0,
                        ar(X_[sl] + 2048 * gi + 512 * dq, 512),
                        op0=OP.mult, op1=OP.add, inc=c_ve)
                    z_evac[dq % 2] = c_ve.cnt

                def transp(dq):
                    pz = pst[4] if dq % 2 == 0 else pst[6]
                    Tn.wait(c_gp, scale_gp[dq])
                    Tn.wait(c_ve, z_evac[dq % 2])
                    for dj in range(4):
                        Tn.transpose(pz[:, 128 * dj:128 * (dj + 1)],
                                     ar(sb_[dq % 2] + 128 * dj, 128),
                                     ar(p_ident, 128),
                                     inc=c_pe if dj == 3 else None)
                    tpe[dq] = c_pe.cnt

                z_stt(0)
                transp(2)
                z_stt(1)
                transp(3)
                z_stt(2)
                z_stt(3)
                srw_last[0] = tpe[3]
                G.wait(c_ve, z_evac[1])
                G.wait(c_ve, z_evac[0])
                G.wait(c_sp)
                G.dma_start(out=z_hbm[128 * rt:128 * (rt + 1), :],
                            in_=ar(X_[sl] + 2048 * gi, 2048),
                            inc=c_sp, k=16)
                xfree[sl][gi] = c_sp.cnt
                Ac.wait(c_ve, it_ve)
                Ac.drain()
                Ac.activation(ar(s1_it, 128, np_=64), ar(s1_it, 128, np_=64),
                              AF.Exp)
                Ac.wait(c_sp, gsp)
                Ac.drain()
                Ac.activation(ar(s1_dif, 128, np_=64), ar(s1_k0, 128, np_=64),
                              AF.Square, inc=c_ac)
                Tn.wait(c_ac)
                Tn.matmul(psPf[0:1, 256:384], lhsT=ones64[:, 0:1],
                          rhs=ar(s1_dif, 128, np_=64), start=True, stop=True,
                          inc=c_pe)
                V.wait(c_pe)
                V.wait(c_gp)
                V.drain()
                V.tensor_scalar(ar(s1_an, 128, np_=1),
                                psPf[0:1, 256:384], 1e-19, None,
                                op0=OP.max, inc=c_ve)
                Ac.wait(c_ve)
                Ac.drain()
                Ac.activation(ar(s1_an, 128, np_=1),
                              ar(s1_an, 128, np_=1), AF.Ln)
                Ac.drain()
                Ac.activation(ar(s1_an, 128, np_=1),
                              ar(s1_an, 128, np_=1), AF.Exp,
                              scale=-0.5, inc=c_ac)
                Tn.wait(c_ac)
                Tn.matmul(psPf[64:128, 256:384], lhsT=ones64[0:1, 0:64],
                          rhs=ar(s1_an, 128, np_=1),
                          start=True, stop=True, inc=c_pe)
                V.wait(c_pe)
                V.drain()
                V.tensor_tensor(ar(s1_k0, 128, np_=64),
                                ar(s1_k0, 128, np_=64),
                                psPf[64:128, 256:384], op=OP.mult, inc=c_ve)
                V.drain()
                V.tensor_tensor(ar(s1_k0, 128, np_=64),
                                ar(s1_k0, 128, np_=64),
                                ar(s1_it, 128, np_=64), op=OP.mult, inc=c_ve)
                k_ve = c_ve.cnt
                pspfree[0] = k_ve
                G.wait(c_ve, k_ve)
                G.wait(c_sp)
                G.dma_start(out=k_hbm[:, 128 * rt:128 * (rt + 1)],
                            in_=ar(s1_k0, 128, np_=64), inc=c_sp, k=16)

            def ffn_half(sl, grp, es, lf_ve):
                for e in es:
                    for dc in range(16):
                        q = 16 * e + dc
                        d = q // 4
                        slot = q % 8
                        wc, wv = wdma[d]
                        Tn.wait(wc, wv)
                        if dc == 0:
                            Tn.wait(c_ve, lf_ve)
                            if e % 2 == 0:
                                Tn.wait(c_ve, ffn_evac[0])
                            else:
                                Tn.wait(c_ve, max(pst_v_free[0], ffn_evac[1]))
                        for gi in range(len(grp)):
                            Tn.matmul(pst[2 * (e % 2) + gi],
                                      lhsT=ar(LF_[sl] + 2048 * gi + 128 * dc,
                                              128),
                                      rhs=ar(s1_W + 512 * slot, 512),
                                      start=(dc == 0), stop=(dc == 15),
                                      inc=c_pe if gi == len(grp) - 1 else None)
                        if slot == 7 or slot == 3:
                            wfree[(q // 4) % 2] = c_pe.cnt
                            w_issue(q // 4 + 2)
                    for gi in range(len(grp)):
                        V.wait(c_pe)
                        V.drain()
                        V.scalar_tensor_tensor(
                            ar(X_[sl] + 2048 * gi + 512 * e, 512),
                            pst[2 * (e % 2) + gi], 1.0,
                            ar(X_[sl] + 2048 * gi + 512 * e, 512),
                            op0=OP.mult, op1=OP.add, inc=c_ve)
                    ffn_evac[e % 2] = c_ve.cnt

            grps = [list(range(2 * g, min(2 * g + 2, NT + 1)))
                    for g in range((NT + 2) // 2)]
            aq = []
            bq = []
            first = [True]
            for gidx, grp in enumerate(grps):
                sl = gidx % 2
                Sx.wait(c_ve, consts_done)
                Sx.wait(c_sp, xfree[sl][0])
                Sx.dma_start(out=ar(X_[sl], 2048),
                             in_=xin[128 * grp[0]:128 * (grp[0] + 1), :],
                             inc=c_x, k=16)
                if bq:
                    post_b(*bq.pop())
                for gi, tl in enumerate(grp):
                    if gi == 0:
                        continue
                    Sx.wait(c_sp, xfree[sl][gi])
                    Sx.dma_start(out=ar(X_[sl] + 2048 * gi, 2048),
                                 in_=xin[128 * tl:128 * (tl + 1), :],
                                 inc=c_x, k=16)
                x_ld = c_x.cnt
                for gi, tl in enumerate(grp):
                    V.wait(c_x, x_ld)
                    Ac.wait(c_x, x_ld)
                    if first[0]:
                        V.wait(c_gd, ws_gd)
                        first[0] = False
                    V.wait(c_pe, lffree[sl])
                    ln_stats_apply(ar(X_[sl] + 2048 * gi, 2048),
                                   ar(s1_LT, 2048), C1, p_stat + 32 * (gi % 2))
                    transpose_tile(s1_LT, LF_[sl] + 2048 * gi, 128, 0)
                lf_ve = c_ve.cnt
                w_issue(0)
                w_issue(1)
                if aq:
                    post_a(*aq[0])
                ffn_half(sl, grp, [0, 1], lf_ve)
                if aq:
                    post_b(*aq.pop(0))
                if aq:
                    post_a(*aq[0])
                ffn_half(sl, grp, [2, 3], lf_ve)
                lffree[sl] = c_pe.cnt
                bq = aq
                aq = [(sl, gi, tl) for gi, tl in enumerate(grp)]
            for t in bq + aq:
                if t in aq:
                    post_a(*t)
                post_b(*t)
            s1_gd = c_sp.cnt

            V.wait(c_sp, s1_gd)
            V.drain()
            V.memset(ar(s2_S, 2048, np_=64), 0.0, inc=c_ve)
            expst = {"i": 0, "subs": [0, 0]}
            rhat_gp = [0]
            tgp = [{}, {}]
            psc_acs = [0, 0]
            kpv = [0, 0]
            anv = [0, 0]
            rkgv = [0, 0]
            uzv = [(0, 0)] * 4
            stc = [0] * 4
            dspe = [0] * 4
            yev = [0] * 4
            oev2 = [0, 0]
            sup2 = [0, 0]
            ke_st = [0]

            def rkg_loads(b):
                pb = b % 2
                Sx.wait(c_gp)
                if b == 0:
                    Sx.wait(c_sp, s1_gd)
                Sx.wait(c_xm)
                Sx.dma_start(out=ar(s2_r + 512 * pb, 512, np_=64),
                             in_=r_hbm[:, 512 * b:512 * (b + 1)],
                             inc=c_xm, k=16)
                Sx.wait(c_xm)
                Sx.dma_start(out=ar(s2_k + 512 * pb, 512, np_=64),
                             in_=k_hbm[:, 512 * b:512 * (b + 1)],
                             inc=c_xm, k=16)
                Sx.wait(c_xm)
                Sx.dma_start(out=ar(s2_g + 512 * pb, 512, np_=64),
                             in_=g_hbm[:, 512 * b:512 * (b + 1)],
                             inc=c_xm, k=16)
                rkgv[pb] = c_xm.cnt

            def exp_tile(b, dst, anch_off, n_g, neg, mul_src):
                pb = b % 2
                aa = ar(s2_a + 512 * pb, 512, np_=64)
                i = expst["i"]
                expst["i"] += 1
                dbuf = s2_dif + 512 * (i % 2)
                gsz = 512 // n_g
                V.wait(c_ac, expst["subs"][i % 2])
                V.drain()
                V.tensor_tensor(
                    ar(dbuf, 512, np_=64).rearrange(
                        "p (g x) -> p g x", x=gsz),
                    aa.rearrange("p (g x) -> p g x", x=gsz),
                    ar(s2_anch + 384 * pb + anch_off, n_g,
                       np_=64).to_broadcast((64, n_g, gsz)),
                    op=OP.subtract, inc=c_ve)
                Ac.wait(c_ve)
                Ac.drain()
                Ac.activation(ar(dst, 512, np_=64),
                              ar(dbuf, 512, np_=64), AF.Exp,
                              scale=(-1.0 if neg else 1.0), inc=c_ac)
                expst["subs"][i % 2] = c_ac.cnt
                G.wait(c_ac)
                G.drain()
                G.tensor_tensor(ar(dst, 512, np_=64),
                                ar(dst, 512, np_=64), mul_src,
                                op=OP.mult, inc=c_gp)
                return c_gp.cnt

            def prep_piece(b, piece):
                pb = b % 2
                an = s2_anch + 384 * pb
                aa = ar(s2_a + 512 * pb, 512, np_=64)
                rr = ar(s2_r + 512 * pb, 512, np_=64)
                kk_ = ar(s2_k + 512 * pb, 512, np_=64)
                gg = ar(s2_g + 512 * pb, 512, np_=64)
                if piece == 0:
                    V.wait(c_xm, rkgv[pb])
                    V.drain()
                    V.tensor_tensor_scan(aa, ones64[:, 0:512], gg,
                                         initial=ar(p_acar, 1, np_=64),
                                         op0=OP.mult, op1=OP.add, inc=c_ve)
                    V.drain()
                    V.tensor_copy(ar(an, 16, np_=64), aa[:, 15:512:32],
                                  inc=c_ve)
                    V.tensor_copy(ar(an + 16, 16, np_=64), aa[:, 31:512:32],
                                  inc=c_ve)
                    V.tensor_copy(ar(an + 32, 1, np_=64),
                                  ar(p_acar, 1, np_=64), inc=c_ve)
                    V.drain()
                    V.tensor_copy(ar(an + 33, 15, np_=64),
                                  ar(an + 16, 15, np_=64), inc=c_ve)
                    V.tensor_copy(ar(an + 48, 1, np_=64),
                                  ar(p_acar, 1, np_=64), inc=c_ve)
                    V.tensor_copy(ar(an + 49, 3, np_=64), aa[:, 127:384:128],
                                  inc=c_ve)
                    V.tensor_copy(ar(an + 52, 4, np_=64), aa[:, 127:512:128],
                                  inc=c_ve)
                    V.memset(ar(an + 63, 1, np_=64), 0.0, inc=c_ve)
                    V.drain()
                    V.tensor_copy(ar(p_acar, 1, np_=64), aa[:, 511:512],
                                  inc=c_ve)
                    V.drain()
                    for ou in range(4):
                        for pi, (t_, s_) in enumerate(pairs):
                            V.tensor_tensor(
                                ar(an + 64 + 6 * ou + pi, 1, np_=64),
                                ar(an + 32 + 4 * ou + t_, 1, np_=64),
                                ar(an + 4 * ou + s_, 1, np_=64),
                                op=OP.subtract)
                        V.tensor_tensor(ar(an + 88 + ou, 1, np_=64),
                                        ar(an + 52 + ou, 1, np_=64),
                                        ar(an + 48 + ou, 1, np_=64),
                                        op=OP.subtract,
                                        inc=c_ve if ou == 3 else None)
                    Ac.wait(c_ve)
                    Ac.drain()
                    Ac.activation(ar(an + 64, 28, np_=64),
                                  ar(an + 64, 28, np_=64), AF.Exp, inc=c_ac)
                    anv[pb] = c_ve.cnt
                    psc_acs[pb] = c_ac.cnt
                elif piece == 1:
                    tgp[pb][1] = exp_tile(b, s2_T1 + 512 * pb, 0, 16,
                                          False, rr)
                    tgp[pb][2] = exp_tile(b, s2_T2 + 512 * pb, 0, 16,
                                          True, kk_)
                elif piece == 2:
                    tgp[pb][3] = exp_tile(b, s2_T3 + 512 * pb, 32, 16,
                                          False, rr)
                    tgp[pb][4] = exp_tile(b, s2_T4 + 512 * pb, 48, 4,
                                          False, rr)
                else:
                    tgp[pb][5] = exp_tile(b, s2_T5 + 512 * pb, 52, 4,
                                          True, kk_)
                    rhat_gp[0] = exp_tile(b, p_rhat + 512 * b, 63, 1,
                                          False, rr)

            def uz_load0():
                Sx.wait(c_sp, s1_gd)
                for tt in range(4):
                    Sx.wait(c_xm)
                    Sx.dma_start(out=ar(s2_U + 2048 * tt, 2048),
                                 in_=u_hbm[128 * tt:128 * (tt + 1), :],
                                 inc=c_xm, k=16)
                    Sx.wait(c_y)
                    Sx.dma_start(out=ar(s2_Z + 2048 * tt, 2048),
                                 in_=z_hbm[128 * tt:128 * (tt + 1), :],
                                 inc=c_y, k=16)
                    uzv[tt] = (c_xm.cnt, c_y.cnt)

            def uz_load(b1, ou):
                Sx.wait(c_st, stc[ou])
                Sx.wait(c_pe, dspe[ou])
                Sx.wait(c_xm)
                Sx.dma_start(out=ar(s2_U + 2048 * ou, 2048),
                             in_=u_hbm[512 * b1 + 128 * ou:
                                       512 * b1 + 128 * (ou + 1), :],
                             inc=c_xm, k=16)
                Sx.wait(c_y)
                Sx.dma_start(out=ar(s2_Z + 2048 * ou, 2048),
                             in_=z_hbm[512 * b1 + 128 * ou:
                                       512 * b1 + 128 * (ou + 1), :],
                             inc=c_y, k=16)
                uzv[ou] = (c_xm.cnt, c_y.cnt)

            pairs = [(t_, s_) for t_ in range(1, 4) for s_ in range(t_)]

            def kp_scale(b, ou):
                pb = b % 2
                an = s2_anch + 384 * pb
                V.wait(c_ac, psc_acs[pb])
                V.wait(c_gp, tgp[pb][2])
                V.wait(c_pe)
                V.drain()
                for pi, (t_, s_) in enumerate(pairs):
                    V.tensor_scalar(ar(s2_kp + 192 * (ou % 2) + 32 * pi, 32,
                                       np_=64),
                                    ar(s2_T2 + 512 * pb + 128 * ou + 32 * s_,
                                       32, np_=64),
                                    ar(an + 64 + 6 * ou + pi, 1, np_=64),
                                    None, op0=OP.mult,
                                    inc=c_ve if pi == 5 else None)
                kpv[ou % 2] = c_ve.cnt

            def comp_ou(b, ou):
                pb = b % 2
                an = s2_anch + 384 * pb
                ob = 128 * ou
                if ou == 0:
                    kp_scale(b, 0)
                kp_ve = kpv[ou % 2]
                Tn.wait(c_ve, kp_ve)
                Tn.wait(c_gp, tgp[pb][3])
                for s_ in range(4):
                    sc = 512 * pb + ob + 32 * s_
                    Tn.matmul(psP[32 * s_:32 * s_ + 32,
                                  32 * s_:32 * s_ + 32],
                              lhsT=ar(s2_T2 + sc, 32, np_=64),
                              rhs=ar(s2_T1 + sc, 32, np_=64),
                              start=True, stop=(s_ == 3),
                              tile_position=(0, 32 * s_), inc=c_pe)
                    for t_ in range(s_ + 1, 4):
                        pi = pairs.index((t_, s_))
                        Tn.matmul(psP[32 * s_:32 * s_ + 32,
                                      32 * t_:32 * t_ + 32],
                                  lhsT=ar(s2_kp + 192 * (ou % 2) + 32 * pi,
                                          32, np_=64),
                                  rhs=ar(s2_T3 + 512 * pb + ob + 32 * t_, 32,
                                         np_=64),
                                  start=False, stop=(t_ == 3),
                                  tile_position=(0, 32 * s_), inc=c_pe)
                if ou < 3:
                    kp_scale(b, ou + 1)
                V.wait(c_pe)
                V.drain()
                V.memset(ar(s2_P, 128), 0.0, inc=c_ve)
                V.drain()
                for s_ in range(4):
                    V.tensor_tensor(
                        arena[32 * s_:32 * s_ + 32,
                              s2_P + 32 * s_:s2_P + 32 * s_ + 32],
                        psP[32 * s_:32 * s_ + 32, 32 * s_:32 * s_ + 32],
                        arena[32 * s_:32 * s_ + 32, p_mask:p_mask + 32],
                        op=OP.mult, inc=c_ve)
                    for t_ in range(s_ + 1, 4):
                        V.tensor_copy(
                            arena[32 * s_:32 * s_ + 32,
                                  s2_P + 32 * t_:s2_P + 32 * t_ + 32],
                            psP[32 * s_:32 * s_ + 32,
                                32 * t_:32 * t_ + 32], inc=c_ve)
                p_ve = c_ve.cnt
                Tn.wait(c_ve, p_ve)
                Tn.wait(c_ve, ke_st[0])
                Tn.wait(c_gp, tgp[pb][5])
                Tn.transpose(pst[6][:, 0:64],
                             ar(s2_T5 + 512 * pb + ob, 128, np_=64),
                             ar(p_ident, 128)[0:64, 0:64], inc=c_pe)
                V.wait(c_pe)
                V.drain()
                V.tensor_copy(ar(s2_kE, 64), pst[6][:, 0:64], inc=c_ve)
                ke_st[0] = c_ve.cnt
                Tn.wait(c_xm, uzv[ou][0])
                Tn.wait(c_gp, tgp[pb][4])
                V.wait(c_y, uzv[ou][1])
                for dt in range(4):
                    po = pst[dt % 2]
                    Tn.wait(c_ve, oev2[dt % 2])
                    Tn.matmul(po, lhsT=ar(s2_T4 + 512 * pb + ob, 128,
                                          np_=64),
                              rhs=ar(s2_S + 512 * dt, 512, np_=64),
                              start=True, stop=False)
                    Tn.matmul(po, lhsT=ar(s2_P, 128),
                              rhs=ar(s2_U + 2048 * ou + 512 * dt, 512),
                              start=False, stop=True, inc=c_pe)
                    V.wait(c_pe)
                    V.drain()
                    V.scalar_tensor_tensor(
                        ar(s2_Z + 2048 * ou + 512 * dt, 512), po, SCALE,
                        ar(s2_Z + 2048 * ou + 512 * dt, 512),
                        op0=OP.mult, op1=OP.add, inc=c_ve)
                    oev2[dt % 2] = c_ve.cnt
                yev[ou] = c_ve.cnt
                G.wait(c_ve, sup2[0])
                G.wait(c_ve, sup2[1])
                G.wait(c_ac, psc_acs[pb])
                G.drain()
                G.tensor_scalar(ar(s2_Sb, 2048, np_=64),
                                ar(s2_S, 2048, np_=64),
                                ar(an + 88 + ou, 1, np_=64), zero_t[0:64, :],
                                op0=OP.mult, op1=OP.add, inc=c_gp)
                sb_gp = c_gp.cnt
                Tn.wait(c_ve, ke_st[0])
                for dt in range(4):
                    pd = pst[2 + dt % 2]
                    Tn.wait(c_ve, sup2[dt % 2])
                    Tn.matmul(pd[0:64, :], lhsT=ar(s2_kE, 64),
                              rhs=ar(s2_U + 2048 * ou + 512 * dt, 512),
                              start=True, stop=True, inc=c_pe)
                    V.wait(c_pe)
                    V.wait(c_gp, sb_gp)
                    V.drain()
                    V.tensor_tensor(ar(s2_S + 512 * dt, 512, np_=64),
                                    ar(s2_Sb + 512 * dt, 512, np_=64),
                                    pd[0:64, :], op=OP.add, inc=c_ve)
                    sup2[dt % 2] = c_ve.cnt
                dspe[ou] = c_pe.cnt

            def y_spill(b, ou):
                Ac.wait(c_ve, yev[ou])
                Ac.wait(c_st)
                Ac.dma_start(out=ypart_hbm[512 * b + 128 * ou:
                                           512 * b + 128 * (ou + 1), :],
                             in_=ar(s2_Z + 2048 * ou, 2048),
                             inc=c_st, k=16)
                stc[ou] = c_st.cnt

            rkg_loads(0)
            uz_load0()
            for piece in range(4):
                prep_piece(0, piece)
            for b in range(NB):
                if b + 1 < NB:
                    rkg_loads(b + 1)
                for ou in range(4):
                    comp_ou(b, ou)
                    y_spill(b, ou)
                    if b + 1 < NB:
                        uz_load(b + 1, ou)
                        prep_piece(b + 1, ou)


            if _os.environ.get("KERNEL_NOS3") == "1":
                Sx.wait(c_st)
                Sx.wait(c_sp)
                return
            hb_row4 = ar(s4_LT, 2048, np_=1)
            Sx.wait(c_misc)
            Sx.wait(c_gp)
            Sx.wait(c_ve)
            Sx.wait(c_ac)
            Sx.wait(c_pe)
            Sx.dma_start(out=hb_row4,
                         in_=fbias.rearrange("(one x) -> one x", one=1),
                         inc=c_misc, k=16)
            Ac.wait(c_misc)
            Ac.wait(c_ve)
            Ac.activation(hb_row4, hb_row4, AF.Copy, scale=0.5, inc=c_ac)
            Tn.wait(c_ac)
            bcast_row(hb_row4, s4_hbb)
            hbb_ve = c_ve.cnt
            G.wait(c_ve)
            G.dma_start(out=sc_in[:, 0:2048], in_=ar(s2_S, 2048, np_=64),
                        inc=c_gd, k=16)
            G.wait(c_gd)
            G.dma_start(out=sc_in[:, 2048:2049], in_=ar(p_acar, 1, np_=64),
                        allow_slow_non_contiguous=True, inc=c_gd, k=16)
            G.wait(c_gd)
            G.collective_compute("AllGather", OP.bypass,
                                 replica_groups=[list(range(NCORE))],
                                 ins=[sc_in.opt()], outs=[sc_out.opt()],
                                 inc=c_cc, k=1)
            Sx.wait(c_cc)
            Sx.dma_start(out=ar(s4_A, 64, np_=8),
                         in_=sc_out[:, 2048:2049].rearrange(
                             "(i m) one -> i (m one)", m=M),
                         allow_slow_non_contiguous=True, inc=c_sc, k=16)
            Tn.wait(c_sc)
            Tn.wait(c_ve)
            Tn.matmul(pst[0][0:8, 0:64], lhsT=ar(p_small, 8, np_=8),
                      rhs=ar(s4_A, 64, np_=8), start=True, stop=True, inc=c_pe)
            Ac.wait(c_pe)
            Ac.activation(ar(s4_w8, 64, np_=8), pst[0][0:8, 0:64], AF.Exp,
                          inc=c_ac)
            V.wait(c_ac)
            V.tensor_scalar(ar(s4_w8, 64, np_=8), ar(s4_w8, 64, np_=8),
                            ar(p_small + 8, 1, np_=8), None, op0=OP.mult,
                            inc=c_ve)
            Tn.wait(c_ve)
            Tn.transpose(pst[1][0:64, 0:8], ar(s4_w8, 64, np_=8),
                         ar(p_ident, 128)[0:8, 0:8], inc=c_pe)
            V.wait(c_pe)
            V.tensor_copy(ar(s4_wT, 8, np_=64), pst[1][0:64, 0:8], inc=c_ve)
            V.memset(ar(s4_Sst, 2048, np_=64), 0.0, inc=c_ve)
            Sx.wait(c_ve)
            Sx.wait(c_cc)
            Sx.wait(c_st)
            Sx.wait(c_pe)
            Sx.dma_start(out=ar(s4_Y, 8 * 2048, np_=64).rearrange(
                             "m (i x) -> m i x", x=2048),
                         in_=sc_out[:, 0:2048].rearrange(
                             "(i m) x -> m i x", m=M),
                         inc=c_sc, k=16)
            l_sc = c_sc.cnt
            sst = [s4_Sst, s4_Sst2]
            V.wait(c_sc, l_sc)
            for i in range(8):
                V.drain()
                V.scalar_tensor_tensor(ar(sst[(i + 1) % 2], 2048, np_=64),
                                       ar(s4_Y + 2048 * i, 2048, np_=64),
                                       ar(s4_wT + i, 1, np_=64),
                                       ar(sst[i % 2], 2048, np_=64),
                                       op0=OP.mult, op1=OP.add, inc=c_ve)
            sstart_col = sst[8 % 2]
            sst_ve = c_ve.cnt

            if _os.environ.get("KERNEL_NOS4") == "1":
                Sx.wait(c_st)
                Sx.wait(c_gd)
                return
            wfree2 = [0, 0]
            wdma2 = {}
            ffn_evac2 = [0, 0]
            oev = [0, 0]
            yfree = [0, 0]
            lffree2 = [0, 0]
            Y_2 = [s4_Y, s4_Y + 4096]
            LF_2 = [s4_LF, s4_LF + 4096]
            def w_issue2(d):
                if d >= 16:
                    return
                Sx.wait(c_pe, wfree2[d % 2])
                cwx = c_wa if d % 2 == 0 else c_wb
                e = d // 4
                dc0 = (4 * d) % 16
                for i2 in range(2):
                    Sx.dma_start(
                        out=ar(s4_W + 2048 * (d % 2), 2048).rearrange(
                            "p (c xi) -> p c xi", xi=512
                            )[:, :, 256 * i2:256 * i2 + 256],
                        in_=wq[e][:, dc0:dc0 + 4, i2],
                        inc=cwx, k=16)
                wdma2[d] = (cwx, cwx.cnt)

            def ffn2_half(sl, grp, es, lf_ve2):
                for e in es:
                    for dc in range(16):
                        q = 16 * e + dc
                        d = q // 4
                        slot = q % 8
                        wc, wv = wdma2[d]
                        Tn.wait(wc, wv)
                        if dc == 0:
                            Tn.wait(c_ve, lf_ve2)
                            Tn.wait(c_ve, ffn_evac2[e % 2])
                        for gi in range(len(grp)):
                            Tn.matmul(pst[2 * (e % 2) + gi],
                                      lhsT=ar(LF_2[sl] + 2048 * gi + 128 * dc,
                                              128),
                                      rhs=ar(s4_W + 512 * slot, 512),
                                      start=(dc == 0), stop=(dc == 15),
                                      inc=c_pe if gi == len(grp) - 1 else None)
                        if slot == 7 or slot == 3:
                            wfree2[(q // 4) % 2] = c_pe.cnt
                            w_issue2(q // 4 + 2)
                    for gi in range(len(grp)):
                        V.wait(c_pe)
                        V.drain()
                        V.scalar_tensor_tensor(
                            ar(Y_2[sl] + 2048 * gi + 512 * e, 512),
                            pst[2 * (e % 2) + gi], 1.0,
                            ar(Y_2[sl] + 2048 * gi + 512 * e, 512),
                            op0=OP.mult, op1=OP.add, inc=c_ve)
                    ffn_evac2[e % 2] = c_ve.cnt

            grps2 = [[2 * g, 2 * g + 1] for g in range(NT // 2)]
            Tn.wait(c_gp, rhat_gp[0])
            Sx.wait(c_st)
            Sx.wait(c_ve, sst_ve)
            pend2 = None
            for gidx, grp in enumerate(grps2):
                sl = gidx % 2
                Sx.wait(c_gd, yfree[sl])
                for gi, tl in enumerate(grp):
                    Sx.wait(c_y)
                    Sx.dma_start(out=ar(Y_2[sl] + 2048 * gi, 2048),
                                 in_=ypart_hbm[128 * tl:128 * (tl + 1), :],
                                 inc=c_y, k=16)
                y_ld = c_y.cnt
                if pend2 is not None:
                    w_issue2(0)
                    w_issue2(1)
                    ffn2_half(pend2[0], pend2[1], [0, 1], pend2[2])
                oevt = [0, 0]
                for gi, tl in enumerate(grp):
                    Tn.wait(c_y, y_ld)
                    Tn.wait(c_ve)
                    for dt in range(4):
                        po = pst[4] if dt % 2 == 0 else pst[6]
                        Tn.wait(c_ve, oev[dt % 2])
                        Tn.matmul(po, lhsT=ar(p_rhat + 128 * tl, 128, np_=64),
                                  rhs=ar(sstart_col + 512 * dt, 512, np_=64),
                                  start=True, stop=True, inc=c_pe)
                        V.wait(c_pe)
                        V.wait(c_ve, sst_ve)
                        V.wait(c_y, y_ld)
                        V.drain()
                        V.scalar_tensor_tensor(
                            ar(Y_2[sl] + 2048 * gi + 512 * dt, 512), po, SCALE,
                            ar(Y_2[sl] + 2048 * gi + 512 * dt, 512),
                            op0=OP.mult, op1=OP.add, inc=c_ve)
                        oev[dt % 2] = c_ve.cnt
                    oevt[gi] = c_ve.cnt
                for gi, tl in enumerate(grp):
                    Ac.wait(c_ve, oevt[gi])
                    V.wait(c_pe, lffree2[sl])
                    ln_stats_apply(ar(Y_2[sl] + 2048 * gi, 2048),
                                   ar(s4_LT, 2048), C1, p_stat + 32 * (gi % 2))
                    transpose_tile(s4_LT, LF_2[sl] + 2048 * gi, 128, 0)
                lf_ve2 = c_ve.cnt

                def spill_grp(psl, pgrp):
                    for gi, tl in enumerate(pgrp):
                        G.wait(c_ve)
                        G.wait(c_ve, hbb_ve)
                        yt = ar(Y_2[psl] + 2048 * gi, 2048)
                        G.tensor_tensor(yt, yt, ar(s4_hbb, 2048), op=OP.add,
                                        inc=c_gp)
                        G.drain()
                        G.wait(c_gd)
                        G.dma_start(out=yout[128 * tl:128 * (tl + 1), :],
                                    in_=yt, inc=c_gd, k=16)
                    yfree[psl] = c_gd.cnt

                if pend2 is not None:
                    ffn2_half(pend2[0], pend2[1], [2, 3], pend2[2])
                    lffree2[pend2[0]] = c_pe.cnt
                    spill_grp(pend2[0], pend2[1])
                pend2 = (sl, grp, lf_ve2)
            w_issue2(0)
            w_issue2(1)
            ffn2_half(pend2[0], pend2[1], [0, 1], pend2[2])
            ffn2_half(pend2[0], pend2[1], [2, 3], pend2[2])
            spill_grp(pend2[0], pend2[1])
            out_gd = c_gd.cnt
            Sx.wait(c_gd, out_gd)

        @block.sync
        def _(eng):
            for c in ctrs:
                c.cnt = 0
            Sx.real = eng
            schedule()
            Sx.real = None

        @block.vector
        def _(eng):
            for c in ctrs:
                c.cnt = 0
            V.real = eng
            schedule()
            V.real = None

        @block.tensor
        def _(eng):
            for c in ctrs:
                c.cnt = 0
            Tn.real = eng
            schedule()
            Tn.real = None

        @block.scalar
        def _(eng):
            for c in ctrs:
                c.cnt = 0
            Ac.real = eng
            schedule()
            Ac.real = None

        @block.gpsimd
        def _(eng):
            for c in ctrs:
                c.cnt = 0
            G.real = eng
            schedule()
            G.real = None

    return nc


_NC_CACHE = [None]


def kernel(L0_attn_hidden_states, L0_attn_conv_w, L0_attn_conv_b,
           L0_attn_in_proj_w, L0_attn_ch_gate_w0, L0_attn_ch_gate_w1,
           L0_attn_ch_gate_b1, L0_attn_mem_gate_w, L0_attn_mem_gate_b,
           L0_attn_residual_weight, L0_ffn_factors, L0_ffn_bias):
    f32 = np.float32
    x = np.ascontiguousarray(np.asarray(L0_attn_hidden_states, f32)[0])
    factors = np.ascontiguousarray(np.asarray(L0_ffn_factors, f32))
    ipT = np.ascontiguousarray(np.asarray(L0_attn_in_proj_w, f32).T)
    mgT = np.ascontiguousarray(np.asarray(L0_attn_mem_gate_w, f32).T)
    ch0T = np.ascontiguousarray(np.asarray(L0_attn_ch_gate_w0, f32).T)
    ch1rT = np.ascontiguousarray(np.asarray(L0_attn_ch_gate_w1, f32).T)
    convw = np.ascontiguousarray(np.asarray(L0_attn_conv_w, f32))
    convb = np.ascontiguousarray(np.asarray(L0_attn_conv_b, f32))
    mgb = np.ascontiguousarray(np.asarray(L0_attn_mem_gate_b, f32).reshape(2, 64).T)
    resw = np.ascontiguousarray(np.asarray(L0_attn_residual_weight, f32))
    ch1b = np.ascontiguousarray(np.asarray(L0_attn_ch_gate_b1, f32))
    fbias = np.ascontiguousarray(np.asarray(L0_ffn_bias, f32))

    ident = np.eye(128, dtype=f32)
    mask32 = np.tile(np.triu(np.ones((32, 32), f32)), (4, 1))
    ones_t = np.ones((128, 512), f32)

    if _NC_CACHE[0] is None:
        _NC_CACHE[0] = build_nc()
    nc = _NC_CACHE[0]

    in_maps = []
    for c in range(NCORE):
        s0 = c * SEG
        xp = np.zeros(((NT + 1) * 128, D), f32)
        if c > 0:
            xp[0:3] = x[s0 - 3:s0]
        xp[128:128 + SEG] = x[s0:s0 + SEG]
        fsl = np.ascontiguousarray(factors[:, :, ESL * c:ESL * (c + 1)])
        mj = np.zeros((8, 8), f32)
        sel = np.zeros((8, 1), f32)
        for i in range(c):
            sel[i, 0] = 1.0
            for l in range(i + 1, c):
                mj[l, i] = 1.0
        in_maps.append(dict(
            xin=xp, fsl=fsl, ipT=ipT, mgT=mgT, ch0T=ch0T, ch1rT=ch1rT,
            convw=convw, convb=convb, mgb=mgb, resw=resw, ch1b=ch1b,
            fbias=fbias, identi=ident, mask32i=mask32, onesi=ones_t,
            mji=mj, seli=sel))

    res = run_bass_kernel_spmd(nc, in_maps, core_ids=list(range(NCORE)))
    out = np.concatenate([r["yout"] for r in res.results], axis=0)
    return out[None, :, :]


# version 24
# speedup vs baseline: 1.0238x; 1.0238x over previous
import contextlib
import os as _os
import numpy as np
import concourse.bass as bass
import concourse.mybir as mybir
from concourse.bass_utils import run_bass_kernel_spmd
from concourse.alu_op_type import AluOpType

F32 = mybir.dt.float32
AF = mybir.ActivationFunctionType
OP = AluOpType
AX = mybir.AxisListType

NCORE = 8
T_TOT, D, M = 16384, 2048, 64
if _os.environ.get("KERNEL_SMALL") == "1":
    T_TOT = 8192
SEG = T_TOT // NCORE
NT = SEG // 128
NB = SEG // 512
ESL = D // NCORE
SCALE = float(1.0 / np.sqrt(M))
C1 = float(0.5 / np.sqrt(8.0))
EPS = 1e-5
HFS = 260


class Ctr:
    def __init__(self, handle):
        self.h = handle
        self.cnt = 0


class _FakeIns:
    def then_inc(self, h, k):
        return self


class _Prox:

    def __init__(self, name):
        self.name = name
        self.real = None

    def wait(self, ctr, val=None):
        v = ctr.cnt if val is None else val
        if self.real is not None and v > 0:
            self.real.wait_ge(ctr.h, v)

    def __getattr__(self, attr):
        def call(*a, _attr=attr, inc=None, k=1, **kw):
            if self.real is not None:
                ins = getattr(self.real, _attr)(*a, **kw)
            else:
                ins = _FakeIns()
            if inc is not None:
                ins.then_inc(inc.h, k)
                inc.cnt += k
            return ins
        return call


def build_nc():
    nc = bass.Bass("TRN2", target_bir_lowering=False)

    xin = nc.dram_tensor("xin", [(NT + 1) * 128, D], F32, kind="ExternalInput").ap()
    fsl = nc.dram_tensor("fsl", [8, D, ESL], F32, kind="ExternalInput").ap()
    ipT = nc.dram_tensor("ipT", [D, 128], F32, kind="ExternalInput").ap()
    mgT = nc.dram_tensor("mgT", [D, 128], F32, kind="ExternalInput").ap()
    ch0T = nc.dram_tensor("ch0T", [D, 128], F32, kind="ExternalInput").ap()
    ch1rT = nc.dram_tensor("ch1rT", [128, D], F32, kind="ExternalInput").ap()
    convw = nc.dram_tensor("convw", [D, 4], F32, kind="ExternalInput").ap()
    convb = nc.dram_tensor("convb", [D], F32, kind="ExternalInput").ap()
    mgb = nc.dram_tensor("mgb", [64, 2], F32, kind="ExternalInput").ap()
    resw = nc.dram_tensor("resw", [D], F32, kind="ExternalInput").ap()
    ch1b = nc.dram_tensor("ch1b", [D], F32, kind="ExternalInput").ap()
    fbias = nc.dram_tensor("fbias", [D], F32, kind="ExternalInput").ap()
    ident_i = nc.dram_tensor("identi", [128, 128], F32, kind="ExternalInput").ap()
    mask32_i = nc.dram_tensor("mask32i", [128, 32], F32, kind="ExternalInput").ap()
    ones_i = nc.dram_tensor("onesi", [128, 512], F32, kind="ExternalInput").ap()
    mj_i = nc.dram_tensor("mji", [8, 8], F32, kind="ExternalInput").ap()
    sel_i = nc.dram_tensor("seli", [8, 1], F32, kind="ExternalInput").ap()

    yout = nc.dram_tensor("yout", [SEG, D], F32, kind="ExternalOutput").ap()

    cc_in = nc.dram_tensor("cc_in", [D, ESL], F32).ap()
    cc_out = nc.dram_tensor("cc_out", [NCORE * D, ESL], F32).ap()
    w_hbm = nc.dram_tensor("w_hbm", [64 * 128, 512], F32).ap()
    xmid_hbm = nc.dram_tensor("xmid_hbm", [(NT + 1) * 128, D], F32).ap()
    ypart_hbm = nc.dram_tensor("ypart_hbm", [SEG, D], F32).ap()
    u_hbm = nc.dram_tensor("u_hbm", [SEG, D], F32).ap()
    z_hbm = nc.dram_tensor("z_hbm", [SEG, D], F32).ap()
    r_hbm = nc.dram_tensor("r_hbm", [M, SEG], F32).ap()
    k_hbm = nc.dram_tensor("k_hbm", [M, SEG], F32).ap()
    g_hbm = nc.dram_tensor("g_hbm", [M, SEG], F32).ap()
    sc_in = nc.dram_tensor("sc_in", [M, 2049], F32).ap()
    sc_out = nc.dram_tensor("sc_out", [NCORE * M, 2049], F32).ap()

    es = contextlib.ExitStack()
    with es:
        AR = 47616
        arena = es.enter_context(nc.sbuf_tensor("arena", [128, AR], F32)).ap()

        pos = [0]

        def A_(n):
            p = pos[0]
            pos[0] += n
            assert pos[0] <= AR, f"arena overflow {pos[0]}"
            return p

        p_ipT = A_(2048); p_mgT = A_(2048); p_ch0T = A_(2048); p_ch1rT = A_(2048)
        p_convw = A_(64); p_convb = A_(16); p_resw = A_(16); p_mgb = A_(2)
        p_ident = A_(128); p_mask = A_(32); p_ones = A_(512)
        p_acar = A_(4); p_zero = A_(1); p_eps = A_(1)
        p_stat = A_(64); p_small = A_(64)

        stage0 = pos[0]

        def stage_alloc(sizes):
            out, q = [], stage0
            for n in sizes:
                out.append(q)
                q += n
            assert q <= AR, f"stage arena overflow {q}"
            return out

        (s1_Xa, s1_Xb, s1_LFa, s1_LFb, s1_LT, s1_W, s1_hbb,
         s1_HF, s1_SH, s1_U,
         s1_k0, s1_sel, s1_tau, s1_it, s1_dif, s1_an, s1_ch, s1_srw,
         s1_gacc, s1_ch1bb) = stage_alloc(
            [2 * 2048, 2 * 2048, 2 * 2048, 2 * 2048, 2048, 8 * 512, 2048,
             16 * 132, 3 * 2048, 2048,
             128, 128, 128, 128, 128, 128, 128, 4 * 128,
             128, 2048])
        s1_WS = s1_LFa
        (s2_U, s2_Z, s2_a, s2_r, s2_k, s2_g,
         s2_T1, s2_T2, s2_T3, s2_T4, s2_T5, s2_dif,
         s2_P, s2_kE, s2_kp, s2_anch,
         s2_S, s2_Sb) = stage_alloc(
            [4 * 2048, 4 * 2048, 2 * 512, 2 * 512, 2 * 512, 2 * 512,
             2 * 512, 2 * 512, 2 * 512, 2 * 512, 2 * 512, 2 * 512,
             128, 64, 2 * 192, 2 * 384,
             2048, 2048])
        (s4_Y, s4_LF, s4_LT, s4_W, s4_L, s4_A, s4_w8, s4_wT,
         s4_Sst, s4_Sst2, _s4pad, s4_hbb, p_rhat) = stage_alloc(
            [4 * 2048, 4 * 2048, 2048, 8 * 512, 2048, 64, 64, 8,
             2048, 2048, 3256, 2048, 2048])
        assert s4_hbb >= s2_Sb + 2048

        globals()["LAYOUT"] = {k: v for k, v in locals().items()
                               if isinstance(v, int)}
        pst = [es.enter_context(nc.psum_tensor(f"ps{i}", [128, 512], F32)).ap()
               for i in range(7)]
        psPf = es.enter_context(nc.psum_tensor("psP", [128, 512], F32)).ap()
        psP = psPf[:, 0:128]
        psP2 = psPf[:, 128:256]

        c_ld = Ctr(es.enter_context(nc.semaphore("c_ld")))
        c_pre = Ctr(es.enter_context(nc.semaphore("c_pre")))
        c_x = Ctr(es.enter_context(nc.semaphore("c_x")))
        c_wa = Ctr(es.enter_context(nc.semaphore("c_wa")))
        c_wb = Ctr(es.enter_context(nc.semaphore("c_wb")))
        c_xm = Ctr(es.enter_context(nc.semaphore("c_xm")))
        c_misc = Ctr(es.enter_context(nc.semaphore("c_misc")))
        c_sc = Ctr(es.enter_context(nc.semaphore("c_sc")))
        c_y = Ctr(es.enter_context(nc.semaphore("c_y")))
        c_sp = Ctr(es.enter_context(nc.semaphore("c_sp")))
        c_gp = Ctr(es.enter_context(nc.semaphore("c_gp")))
        c_gd = Ctr(es.enter_context(nc.semaphore("c_gd")))
        c_cc = Ctr(es.enter_context(nc.semaphore("c_cc")))
        c_pe = Ctr(es.enter_context(nc.semaphore("c_pe")))
        c_ve = Ctr(es.enter_context(nc.semaphore("c_ve")))
        c_ac = Ctr(es.enter_context(nc.semaphore("c_ac")))
        c_st = Ctr(es.enter_context(nc.semaphore("c_st")))
        ctrs = [c_ld, c_pre, c_x, c_wa, c_wb, c_xm, c_misc, c_sc, c_y, c_sp, c_gp, c_gd, c_cc, c_pe, c_ve, c_ac, c_st]
        block = es.enter_context(nc.Block())

        def ar(col, ncol, p0=0, np_=128):
            return arena[p0:p0 + np_, col:col + ncol]

        Sx = _Prox("sync"); V = _Prox("vector"); Tn = _Prox("tensor")
        Ac = _Prox("scalar"); G = _Prox("gpsimd")

        def schedule():
            Sx.dma_start(out=ar(p_ipT, 2048).rearrange("p (dc r) -> p dc r", r=128),
                         in_=ipT.rearrange("(dc p) r -> p dc r", p=128),
                         inc=c_ld, k=16)
            Sx.dma_start(out=ar(p_mgT, 2048).rearrange("p (dc r) -> p dc r", r=128),
                         in_=mgT.rearrange("(dc p) r -> p dc r", p=128),
                         inc=c_ld, k=16)
            Sx.dma_start(out=ar(p_ch0T, 2048).rearrange("p (dc r) -> p dc r", r=128),
                         in_=ch0T.rearrange("(dc p) r -> p dc r", p=128),
                         inc=c_ld, k=16)
            Sx.dma_start(out=ar(p_ch1rT, 2048), in_=ch1rT, inc=c_ld, k=16)
            Sx.dma_start(out=ar(p_convw, 64).rearrange("p (dc w) -> p dc w", w=4),
                         in_=convw.rearrange("(dc p) w -> p dc w", p=128),
                         inc=c_ld, k=16)
            Sx.dma_start(out=ar(p_convb, 16),
                         in_=convb.rearrange("(dc p) -> p dc", p=128),
                         allow_slow_non_contiguous=True, inc=c_ld, k=16)
            Sx.dma_start(out=ar(p_resw, 16),
                         in_=resw.rearrange("(dc p) -> p dc", p=128),
                         allow_slow_non_contiguous=True, inc=c_ld, k=16)
            Sx.dma_start(out=ar(p_mgb, 2, np_=64), in_=mgb, inc=c_ld, k=16)
            Sx.dma_start(out=ar(p_ident, 128), in_=ident_i, inc=c_ld, k=16)
            Sx.dma_start(out=ar(p_mask, 32), in_=mask32_i, inc=c_ld, k=16)
            Sx.dma_start(out=ar(p_ones, 512), in_=ones_i, inc=c_ld, k=16)
            Sx.dma_start(out=ar(p_small, 8, np_=8), in_=mj_i, inc=c_ld, k=16)
            Sx.dma_start(out=ar(p_small + 8, 1, np_=8), in_=sel_i, inc=c_ld, k=16)
            hb_row = ar(s1_LT, 2048, np_=1)
            cb_row = hb_row
            Sx.dma_start(out=hb_row, in_=fbias.rearrange("(one x) -> one x", one=1),
                         inc=c_ld, k=16)
            ld_all = c_ld.cnt

            Ac.wait(c_ld, ld_all)
            Ac.activation(hb_row, hb_row, AF.Copy, scale=0.5, inc=c_ac)
            Tn.wait(c_ac)
            Tn.wait(c_ld, ld_all)

            def bcast_row(src_row, dst):
                for j in range(4):
                    Tn.wait(c_ve)
                    Tn.matmul(pst[0], lhsT=ar(p_ones, 128, np_=1),
                              rhs=src_row[:, 512 * j:512 * (j + 1)],
                              start=True, stop=True, inc=c_pe)
                    V.wait(c_pe)
                    V.tensor_copy(ar(dst + 512 * j, 512), pst[0], inc=c_ve)

            bcast_row(hb_row, s1_hbb)
            V.memset(ar(p_zero, 1), 0.0, inc=c_ve)
            V.memset(ar(p_eps, 1), EPS, inc=c_ve)
            V.memset(ar(p_acar, 1, np_=64), 0.0, inc=c_ve)
            consts_done = c_ve.cnt
            eps_t = ar(p_eps, 1)
            zero_t = ar(p_zero, 1)

            for rr in range(8):
                G.wait(c_gd)
                kw = {} if rr == 0 else dict(accum_op=OP.add)
                G.dma_start(out=ar(s1_WS, 4096).rearrange(
                                "p (dc x) -> p dc x", x=ESL),
                            in_=fsl[rr].rearrange("(dc p) x -> p dc x", p=128),
                            inc=c_gd, k=16, **kw)
            G.wait(c_gd)
            G.dma_start(out=cc_in.rearrange("(dc p) x -> p dc x", p=128),
                        in_=ar(s1_WS, 4096).rearrange(
                            "p (dc x) -> p dc x", x=ESL),
                        inc=c_gd, k=16)
            G.wait(c_gd)
            ws_gd = c_gd.cnt
            G.collective_compute("AllGather", OP.bypass,
                                 replica_groups=[list(range(NCORE))],
                                 ins=[cc_in.opt()], outs=[cc_out.opt()],
                                 inc=c_cc, k=1)
            ag_cc = c_cc.cnt



            def ln_stats_apply(xap, out_ap, cfold, st):
                V.drain()
                for j in range(4):
                    V.bn_stats(ar(st + 6 * j, 6), xap[:, 512 * j:512 * (j + 1)])
                V.drain()
                V.bn_aggr(ar(st + 24, 2), ar(st, 24), inc=c_ve)
                Ac.wait(c_ve)
                Ac.drain()
                Ac.activation(ar(st + 26, 1), ar(st + 25, 1), AF.Ln,
                              bias=eps_t, scale=1.0)
                Ac.drain()
                Ac.activation(ar(st + 26, 1), ar(st + 26, 1), AF.Exp,
                              scale=-0.5, inc=c_ac)
                V.wait(c_ac)
                V.drain()
                V.tensor_scalar(ar(st + 26, 1), ar(st + 26, 1), cfold, None,
                                op0=OP.mult, inc=c_ve)
                V.drain()
                V.tensor_scalar(ar(st + 27, 1), ar(st + 24, 1),
                                ar(st + 26, 1), -1.0,
                                op0=OP.mult, op1=OP.mult, inc=c_ve)
                Ac.wait(c_ve)
                Ac.drain()
                Ac.activation(out_ap, xap, AF.Identity,
                              bias=ar(st + 27, 1), scale=ar(st + 26, 1),
                              inc=c_ac)

            def transpose_tile(src_col, dst_col, dst_stride, dst_off, nsub=16):
                Tn.wait(c_ac)
                evac_done = [0, 0]
                for dq in range(nsub // 4):
                    pt = pst[4] if dq % 2 == 0 else pst[6]
                    Tn.wait(c_ve, evac_done[dq % 2])
                    for dj in range(4):
                        dc = 4 * dq + dj
                        Tn.transpose(pt[:, 128 * dj:128 * (dj + 1)],
                                     ar(src_col + 128 * dc, 128),
                                     ar(p_ident, 128),
                                     inc=c_pe if dj == 3 else None)
                    V.wait(c_pe)
                    base = dst_col + dst_stride * 4 * dq + dst_off
                    if dst_stride == 128:
                        V.tensor_copy(ar(base, 512), pt, inc=c_ve)
                    else:
                        lo = dst_col + dst_stride * 4 * dq
                        V.tensor_copy(
                            arena[:, lo:lo + 4 * dst_stride].rearrange(
                                "p (g x) -> p g x", x=dst_stride)[
                                :, :, dst_off:dst_off + 128],
                            pt.rearrange("p (g x) -> p g x", x=128),
                            inc=c_ve)
                    evac_done[dq % 2] = c_ve.cnt

            ones64 = ar(p_ones, 512, np_=64)
            Sx.wait(c_pe)
            Sx.wait(c_misc)
            Sx.dma_start(out=cb_row,
                         in_=ch1b.rearrange("(one x) -> one x", one=1),
                         inc=c_misc, k=16)
            Tn.wait(c_misc)
            bcast_row(cb_row, s1_ch1bb)

            srw_last = [0]
            z_evac = [0, 0]
            pst_v_free = [0]
            pspfree = [0]
            psp2free = [0]
            X_ = [s1_Xa, s1_Xb]
            LF_ = [s1_LFa, s1_LFb]
            xfree = [[0, 0], [0, 0]]
            lffree = [0, 0]
            wfree = [0, 0]
            ffn_evac = [0, 0]
            wdma = {}
            tst = {}
            wq = cc_out.rearrange("(e2 i2 dc p) xi -> e2 p dc i2 xi",
                                  e2=4, i2=2, p=128)

            def w_issue(d):
                if d >= 16:
                    return
                Sx.wait(c_cc, ag_cc)
                Sx.wait(c_pe, wfree[d % 2])
                cwx = c_wa if d % 2 == 0 else c_wb
                e = d // 4
                dc0 = (4 * d) % 16
                for i2 in range(2):
                    Sx.dma_start(
                        out=ar(s1_W + 2048 * (d % 2), 2048).rearrange(
                            "p (c xi) -> p c xi", xi=512
                            )[:, :, 256 * i2:256 * i2 + 256],
                        in_=wq[e][:, dc0:dc0 + 4, i2],
                        inc=cwx, k=16)
                wdma[d] = (cwx, cwx.cnt)

            def post_a(sl, gi, tl):
                rt = tl - 1
                xm = ar(X_[sl] + 2048 * gi, 2048)
                G.wait(c_ve)
                G.tensor_tensor(xm, xm, ar(s1_hbb, 2048), op=OP.add, inc=c_gp)
                V.wait(c_gp)
                Ac.wait(c_gp)
                Ac.wait(c_pe)
                ln_stats_apply(xm, ar(s1_LT, 2048), 1.0, p_stat)
                V.wait(c_ac)
                V.wait(c_gp)
                V.drain()
                if tl == 1:
                    V.tensor_copy(
                        arena[:, s1_HF:s1_HF + 16 * 132].rearrange(
                            "p (g x) -> p g x", x=132)[:, :, 1:4],
                        arena[:, s1_HF:s1_HF + 16 * 132].rearrange(
                            "p (g x) -> p g x", x=132)[:, :, 4:7], inc=c_ve)
                elif tl > 1:
                    V.tensor_copy(
                        arena[:, s1_HF:s1_HF + 16 * 132].rearrange(
                            "p (g x) -> p g x", x=132)[:, :, 1:4],
                        arena[:, s1_HF:s1_HF + 16 * 132].rearrange(
                            "p (g x) -> p g x", x=132)[:, :, 129:132],
                        inc=c_ve)
                carry_ve = c_ve.cnt
                Tn.wait(c_ve, carry_ve)
                transpose_tile(s1_LT, s1_HF, 132, 4)
                if tl == 0:
                    return
                hf_ve = c_ve.cnt
                Tn.wait(c_ve, hf_ve)
                for dc in range(16):
                    Tn.matmul(pst[5][:, 0:128],
                              lhsT=ar(p_ch0T + 128 * dc, 128),
                              rhs=ar(s1_HF + 132 * dc + 4, 128),
                              start=(dc == 0), stop=(dc == 15),
                              inc=c_pe if dc == 15 else None)
                V.wait(c_pe)
                V.drain()
                V.tensor_copy(ar(s1_ch, 128), pst[5][:, 0:128], inc=c_ve)
                ch_ve = c_ve.cnt
                Tn.wait(c_ve, ch_ve)
                pr_ve = ffn_evac[1]
                for pr in range(2):
                    Tn.wait(c_ve, pr_ve)
                    for j in range(2):
                        e = 2 * pr + j
                        Tn.matmul(pst[2 + j], lhsT=ar(s1_ch, 128),
                                  rhs=ar(p_ch1rT + 512 * e, 512),
                                  start=True, stop=True,
                                  inc=c_pe if j == 1 else None)
                    V.wait(c_pe)
                    if pr == 0:
                        V.wait(c_sp)
                    V.drain()
                    for j in range(2):
                        e = 2 * pr + j
                        V.scalar_tensor_tensor(
                            ar(s1_U + 512 * e, 512), pst[2 + j], 1.0,
                            ar(s1_ch1bb + 512 * e, 512),
                            op0=OP.mult, op1=OP.add,
                            inc=c_ve if j == 1 else None)
                    pr_ve = c_ve.cnt
                pst_v_free[0] = c_ve.cnt
                Ac.wait(c_ve)
                Ac.drain()
                for e in range(4):
                    Ac.activation(ar(s1_U + 512 * e, 512),
                                  ar(s1_U + 512 * e, 512), AF.Sigmoid,
                                  inc=c_ac if e == 3 else None)
                V.wait(c_ac)
                V.drain()
                for e in range(4):
                    V.tensor_tensor(ar(s1_U + 512 * e, 512),
                                    ar(s1_U + 512 * e, 512),
                                    ar(s1_LT + 512 * e, 512),
                                    op=OP.mult, inc=c_ve if e == 3 else None)
                u_ve = c_ve.cnt
                G.wait(c_ve, u_ve)
                G.wait(c_sp)
                G.dma_start(out=u_hbm[128 * rt:128 * (rt + 1), :],
                            in_=ar(s1_U, 2048), inc=c_sp, k=16)
                u_sp = c_sp.cnt
                sh = s1_SH + 2048 * (tl % 3)
                V.wait(c_ac)
                silu_done = [0, 0]
                for dc in range(8):
                    hb_ = s1_HF + 132 * dc
                    pc = pst[4] if dc % 2 == 0 else pst[6]
                    V.wait(c_ac, silu_done[dc % 2])
                    V.wait(c_pe)
                    V.drain()
                    V.tensor_scalar(pc[:, 0:128], ar(hb_ + 1, 128),
                                    ar(p_convw + 4 * dc, 1), None,
                                    op0=OP.mult, inc=c_ve)
                    for i in range(1, 4):
                        V.drain()
                        V.scalar_tensor_tensor(
                            pc[:, 0:128], ar(hb_ + 1 + i, 128),
                            ar(p_convw + 4 * dc + i, 1), pc[:, 0:128],
                            op0=OP.mult, op1=OP.add, inc=c_ve)
                    Ac.wait(c_ve)
                    Ac.activation(ar(sh + 128 * dc, 128), pc[:, 0:128],
                                  AF.Silu, bias=ar(p_convb + dc, 1), scale=1.0,
                                  inc=c_ac)
                    silu_done[dc % 2] = c_ac.cnt
                G.wait(c_pe)
                for dc in range(8, 16):
                    hb_ = s1_HF + 132 * dc
                    G.wait(c_ac)
                    G.drain()
                    G.tensor_scalar(ar(s1_gacc, 128), ar(hb_ + 1, 128),
                                    ar(p_convw + 4 * dc, 1), zero_t,
                                    op0=OP.mult, op1=OP.add, inc=c_gp)
                    for i in range(1, 4):
                        G.drain()
                        G.tensor_scalar(ar(s1_an, 128), ar(hb_ + 1 + i, 128),
                                        ar(p_convw + 4 * dc + i, 1), zero_t,
                                        op0=OP.mult, op1=OP.add, inc=c_gp)
                        G.drain()
                        G.tensor_tensor(ar(s1_gacc, 128), ar(s1_gacc, 128),
                                        ar(s1_an, 128), op=OP.add, inc=c_gp)
                    Ac.wait(c_gp)
                    Ac.drain()
                    Ac.activation(ar(sh + 128 * dc, 128), ar(s1_gacc, 128),
                                  AF.Silu, bias=ar(p_convb + dc, 1), scale=1.0,
                                  inc=c_ac)
                tst[(sl, gi)] = (c_ac.cnt, u_sp)

            def post_b(sl, gi, tl):
                if tl == 0:
                    return
                rt = tl - 1
                silu_ac, u_sp = tst[(sl, gi)]
                sh = s1_SH + 2048 * (tl % 3)
                Tn.wait(c_ac, max(silu_ac, psp2free[0]))
                Tn.wait(c_ve, pspfree[0])

                def gate_mm(ps, wcol, mov_base):
                    for dc in range(16):
                        Tn.matmul(ps, lhsT=ar(wcol + 128 * dc, 128),
                                  rhs=ar(mov_base + 128 * dc, 128),
                                  start=(dc == 0), stop=(dc == 15),
                                  inc=c_pe if dc == 15 else None)

                gate_mm(psP[0:128, 0:128], p_ipT, sh)
                gate_mm(psP2[0:128, 0:128], p_mgT, sh)
                gates_pe = c_pe.cnt
                V.wait(c_pe, gates_pe)
                V.wait(c_sp)
                V.drain()
                V.tensor_copy(ar(s1_dif, 128, np_=64), psP[0:64, 0:128],
                              inc=c_ve)
                V.tensor_copy(ar(s1_k0, 128, np_=64), psP[64:128, 0:128],
                              inc=c_ve)
                r_ve = c_ve.cnt
                G.wait(c_ve, r_ve)
                G.dma_start(out=r_hbm[:, 128 * rt:128 * (rt + 1)],
                            in_=ar(s1_dif, 128, np_=64), inc=c_sp, k=16)
                sb_ = [s1_srw, s1_U]
                scale_gp = [0, 0, 0, 0]
                tpe = [0, 0, 0, 0]

                def srw_scales(dq):
                    G.wait(c_ac, silu_ac)
                    if dq < 2:
                        G.wait(c_pe, srw_last[0])
                    else:
                        G.wait(c_pe, tpe[dq - 2])
                    if dq % 2 == 1:
                        G.wait(c_sp, u_sp)
                    G.drain()
                    for dj in range(4):
                        dc = 4 * dq + dj
                        G.tensor_scalar(ar(sb_[dq % 2] + 128 * dj, 128),
                                        ar(sh + 128 * dc, 128),
                                        ar(p_resw + dc, 1), zero_t,
                                        op0=OP.mult, op1=OP.add, inc=c_gp)
                    scale_gp[dq] = c_gp.cnt

                srw_scales(0)
                srw_scales(1)
                Ac.wait(c_pe, gates_pe)
                Ac.wait(c_ve, r_ve)
                Ac.drain()
                Ac.activation(ar(s1_sel, 128, np_=64), psP2[0:64, 0:128],
                              AF.Exp, bias=ar(p_mgb, 1, np_=64), scale=1.0)
                Ac.drain()
                Ac.activation(ar(s1_sel, 128, np_=64), ar(s1_sel, 128, np_=64),
                              AF.Ln, bias=ones64[:, 0:1], scale=1.0)
                Ac.drain()
                Ac.activation(ar(s1_it, 128, np_=64), ar(s1_sel, 128, np_=64),
                              AF.Ln)
                Ac.activation(ar(s1_tau, 128, np_=64), psP2[64:128, 0:128],
                              AF.Sigmoid, bias=ar(p_mgb + 1, 1, np_=64),
                              scale=1.0, inc=c_ac)
                psp2free[0] = c_ac.cnt
                selit_ac = c_ac.cnt
                for dq in range(2):
                    pz = pst[4] if dq % 2 == 0 else pst[6]
                    Tn.wait(c_gp, scale_gp[dq])
                    Tn.wait(c_ve, z_evac[dq % 2])
                    for dj in range(4):
                        Tn.transpose(pz[:, 128 * dj:128 * (dj + 1)],
                                     ar(sb_[dq % 2] + 128 * dj, 128),
                                     ar(p_ident, 128),
                                     inc=c_pe if dj == 3 else None)
                    tpe[dq] = c_pe.cnt
                    srw_scales(dq + 2)
                V.wait(c_ac, selit_ac)
                V.drain()
                V.tensor_tensor(ar(s1_it, 128, np_=64), ar(s1_it, 128, np_=64),
                                ar(s1_tau, 128, np_=64), op=OP.mult, inc=c_ve)
                V.wait(c_sp)
                V.drain()
                V.scalar_tensor_tensor(ar(s1_dif, 128, np_=64),
                                       ar(s1_sel, 128, np_=64), -1.0,
                                       ar(s1_tau, 128, np_=64),
                                       op0=OP.mult, op1=OP.mult, inc=c_ve)
                it_ve = c_ve.cnt
                g_ve = c_ve.cnt
                G.wait(c_ve, g_ve)
                G.dma_start(out=g_hbm[:, 128 * rt:128 * (rt + 1)],
                            in_=ar(s1_dif, 128, np_=64), inc=c_sp, k=16)
                gsp = c_sp.cnt

                def z_stt(dq):
                    pz = pst[4] if dq % 2 == 0 else pst[6]
                    V.wait(c_pe, tpe[dq])
                    V.drain()
                    V.scalar_tensor_tensor(
                        ar(X_[sl] + 2048 * gi + 512 * dq, 512), pz, 1.0,
                        ar(X_[sl] + 2048 * gi + 512 * dq, 512),
                        op0=OP.mult, op1=OP.add, inc=c_ve)
                    z_evac[dq % 2] = c_ve.cnt

                def transp(dq):
                    pz = pst[4] if dq % 2 == 0 else pst[6]
                    Tn.wait(c_gp, scale_gp[dq])
                    Tn.wait(c_ve, z_evac[dq % 2])
                    for dj in range(4):
                        Tn.transpose(pz[:, 128 * dj:128 * (dj + 1)],
                                     ar(sb_[dq % 2] + 128 * dj, 128),
                                     ar(p_ident, 128),
                                     inc=c_pe if dj == 3 else None)
                    tpe[dq] = c_pe.cnt

                z_stt(0)
                transp(2)
                z_stt(1)
                transp(3)
                z_stt(2)
                z_stt(3)
                srw_last[0] = tpe[3]
                G.wait(c_ve, z_evac[1])
                G.wait(c_ve, z_evac[0])
                G.wait(c_sp)
                G.dma_start(out=z_hbm[128 * rt:128 * (rt + 1), :],
                            in_=ar(X_[sl] + 2048 * gi, 2048),
                            inc=c_sp, k=16)
                xfree[sl][gi] = c_sp.cnt
                Ac.wait(c_ve, it_ve)
                Ac.drain()
                Ac.activation(ar(s1_it, 128, np_=64), ar(s1_it, 128, np_=64),
                              AF.Exp)
                Ac.wait(c_sp, gsp)
                Ac.drain()
                Ac.activation(ar(s1_dif, 128, np_=64), ar(s1_k0, 128, np_=64),
                              AF.Square, inc=c_ac)
                Tn.wait(c_ac)
                Tn.matmul(psPf[0:1, 256:384], lhsT=ones64[:, 0:1],
                          rhs=ar(s1_dif, 128, np_=64), start=True, stop=True,
                          inc=c_pe)
                V.wait(c_pe)
                V.wait(c_gp)
                V.drain()
                V.tensor_scalar(ar(s1_an, 128, np_=1),
                                psPf[0:1, 256:384], 1e-19, None,
                                op0=OP.max, inc=c_ve)
                Ac.wait(c_ve)
                Ac.drain()
                Ac.activation(ar(s1_an, 128, np_=1),
                              ar(s1_an, 128, np_=1), AF.Ln)
                Ac.drain()
                Ac.activation(ar(s1_an, 128, np_=1),
                              ar(s1_an, 128, np_=1), AF.Exp,
                              scale=-0.5, inc=c_ac)
                Tn.wait(c_ac)
                Tn.matmul(psPf[64:128, 256:384], lhsT=ones64[0:1, 0:64],
                          rhs=ar(s1_an, 128, np_=1),
                          start=True, stop=True, inc=c_pe)
                V.wait(c_pe)
                V.drain()
                V.tensor_tensor(ar(s1_k0, 128, np_=64),
                                ar(s1_k0, 128, np_=64),
                                psPf[64:128, 256:384], op=OP.mult, inc=c_ve)
                V.drain()
                V.tensor_tensor(ar(s1_k0, 128, np_=64),
                                ar(s1_k0, 128, np_=64),
                                ar(s1_it, 128, np_=64), op=OP.mult, inc=c_ve)
                k_ve = c_ve.cnt
                pspfree[0] = k_ve
                G.wait(c_ve, k_ve)
                G.wait(c_sp)
                G.dma_start(out=k_hbm[:, 128 * rt:128 * (rt + 1)],
                            in_=ar(s1_k0, 128, np_=64), inc=c_sp, k=16)

            def ffn_half(sl, grp, es, lf_ve):
                for e in es:
                    for dc in range(16):
                        q = 16 * e + dc
                        d = q // 4
                        slot = q % 8
                        wc, wv = wdma[d]
                        Tn.wait(wc, wv)
                        if dc == 0:
                            Tn.wait(c_ve, lf_ve)
                            if e % 2 == 0:
                                Tn.wait(c_ve, ffn_evac[0])
                            else:
                                Tn.wait(c_ve, max(pst_v_free[0], ffn_evac[1]))
                        for gi in range(len(grp)):
                            Tn.matmul(pst[2 * (e % 2) + gi],
                                      lhsT=ar(LF_[sl] + 2048 * gi + 128 * dc,
                                              128),
                                      rhs=ar(s1_W + 512 * slot, 512),
                                      start=(dc == 0), stop=(dc == 15),
                                      inc=c_pe if gi == len(grp) - 1 else None)
                        if slot == 7 or slot == 3:
                            wfree[(q // 4) % 2] = c_pe.cnt
                            w_issue(q // 4 + 2)
                    for gi in range(len(grp)):
                        V.wait(c_pe)
                        V.drain()
                        V.scalar_tensor_tensor(
                            ar(X_[sl] + 2048 * gi + 512 * e, 512),
                            pst[2 * (e % 2) + gi], 1.0,
                            ar(X_[sl] + 2048 * gi + 512 * e, 512),
                            op0=OP.mult, op1=OP.add, inc=c_ve)
                    ffn_evac[e % 2] = c_ve.cnt

            grps = [list(range(2 * g, min(2 * g + 2, NT + 1)))
                    for g in range((NT + 2) // 2)]
            aq = []
            bq = []
            first = [True]
            for gidx, grp in enumerate(grps):
                sl = gidx % 2
                Sx.wait(c_ve, consts_done)
                Sx.wait(c_sp, xfree[sl][0])
                Sx.dma_start(out=ar(X_[sl], 2048),
                             in_=xin[128 * grp[0]:128 * (grp[0] + 1), :],
                             inc=c_x, k=16)
                if bq:
                    post_b(*bq.pop())
                for gi, tl in enumerate(grp):
                    if gi == 0:
                        continue
                    Sx.wait(c_sp, xfree[sl][gi])
                    Sx.dma_start(out=ar(X_[sl] + 2048 * gi, 2048),
                                 in_=xin[128 * tl:128 * (tl + 1), :],
                                 inc=c_x, k=16)
                x_ld = c_x.cnt
                for gi, tl in enumerate(grp):
                    V.wait(c_x, x_ld)
                    Ac.wait(c_x, x_ld)
                    if first[0]:
                        V.wait(c_gd, ws_gd)
                        first[0] = False
                    V.wait(c_pe, lffree[sl])
                    ln_stats_apply(ar(X_[sl] + 2048 * gi, 2048),
                                   ar(s1_LT, 2048), C1, p_stat + 32 * (gi % 2))
                    transpose_tile(s1_LT, LF_[sl] + 2048 * gi, 128, 0)
                lf_ve = c_ve.cnt
                w_issue(0)
                w_issue(1)
                if aq:
                    post_a(*aq[0])
                ffn_half(sl, grp, [0, 1], lf_ve)
                if aq:
                    post_b(*aq.pop(0))
                if aq:
                    post_a(*aq[0])
                ffn_half(sl, grp, [2, 3], lf_ve)
                lffree[sl] = c_pe.cnt
                bq = aq
                aq = [(sl, gi, tl) for gi, tl in enumerate(grp)]
            for t in bq + aq:
                if t in aq:
                    post_a(*t)
                post_b(*t)
            s1_gd = c_sp.cnt

            V.wait(c_sp, s1_gd)
            V.drain()
            V.memset(ar(s2_S, 2048, np_=64), 0.0, inc=c_ve)
            expst = {"i": 0, "subs": [0, 0]}
            rhat_gp = [0]
            tgp = [{}, {}]
            psc_acs = [0, 0]
            kpv = [0, 0]
            anv = [0, 0]
            rkgv = [0, 0]
            uzv = [(0, 0)] * 4
            stc = [0] * 4
            dspe = [0] * 4
            yev = [0] * 4
            oev2 = [0, 0]
            sup2 = [0, 0]
            ke_st = [0]

            def rkg_loads(b):
                pb = b % 2
                Sx.wait(c_gp)
                if b == 0:
                    Sx.wait(c_sp, s1_gd)
                Sx.wait(c_xm)
                Sx.dma_start(out=ar(s2_r + 512 * pb, 512, np_=64),
                             in_=r_hbm[:, 512 * b:512 * (b + 1)],
                             inc=c_xm, k=16)
                Sx.wait(c_xm)
                Sx.dma_start(out=ar(s2_k + 512 * pb, 512, np_=64),
                             in_=k_hbm[:, 512 * b:512 * (b + 1)],
                             inc=c_xm, k=16)
                Sx.wait(c_xm)
                Sx.dma_start(out=ar(s2_g + 512 * pb, 512, np_=64),
                             in_=g_hbm[:, 512 * b:512 * (b + 1)],
                             inc=c_xm, k=16)
                rkgv[pb] = c_xm.cnt

            def exp_tile(b, dst, anch_off, n_g, neg, mul_src):
                pb = b % 2
                aa = ar(s2_a + 512 * pb, 512, np_=64)
                i = expst["i"]
                expst["i"] += 1
                dbuf = s2_dif + 512 * (i % 2)
                gsz = 512 // n_g
                V.wait(c_ac, expst["subs"][i % 2])
                V.drain()
                V.tensor_tensor(
                    ar(dbuf, 512, np_=64).rearrange(
                        "p (g x) -> p g x", x=gsz),
                    aa.rearrange("p (g x) -> p g x", x=gsz),
                    ar(s2_anch + 384 * pb + anch_off, n_g,
                       np_=64).to_broadcast((64, n_g, gsz)),
                    op=OP.subtract, inc=c_ve)
                Ac.wait(c_ve)
                Ac.drain()
                Ac.activation(ar(dst, 512, np_=64),
                              ar(dbuf, 512, np_=64), AF.Exp,
                              scale=(-1.0 if neg else 1.0), inc=c_ac)
                expst["subs"][i % 2] = c_ac.cnt
                G.wait(c_ac)
                G.drain()
                G.tensor_tensor(ar(dst, 512, np_=64),
                                ar(dst, 512, np_=64), mul_src,
                                op=OP.mult, inc=c_gp)
                return c_gp.cnt

            def prep_piece(b, piece):
                pb = b % 2
                an = s2_anch + 384 * pb
                aa = ar(s2_a + 512 * pb, 512, np_=64)
                rr = ar(s2_r + 512 * pb, 512, np_=64)
                kk_ = ar(s2_k + 512 * pb, 512, np_=64)
                gg = ar(s2_g + 512 * pb, 512, np_=64)
                if piece == 0:
                    V.wait(c_xm, rkgv[pb])
                    V.drain()
                    V.tensor_tensor_scan(aa, ones64[:, 0:512], gg,
                                         initial=ar(p_acar, 1, np_=64),
                                         op0=OP.mult, op1=OP.add, inc=c_ve)
                    V.drain()
                    V.tensor_copy(ar(an, 16, np_=64), aa[:, 15:512:32],
                                  inc=c_ve)
                    V.tensor_copy(ar(an + 16, 16, np_=64), aa[:, 31:512:32],
                                  inc=c_ve)
                    V.tensor_copy(ar(an + 32, 1, np_=64),
                                  ar(p_acar, 1, np_=64), inc=c_ve)
                    V.drain()
                    V.tensor_copy(ar(an + 33, 15, np_=64),
                                  ar(an + 16, 15, np_=64), inc=c_ve)
                    V.tensor_copy(ar(an + 48, 1, np_=64),
                                  ar(p_acar, 1, np_=64), inc=c_ve)
                    V.tensor_copy(ar(an + 49, 3, np_=64), aa[:, 127:384:128],
                                  inc=c_ve)
                    V.tensor_copy(ar(an + 52, 4, np_=64), aa[:, 127:512:128],
                                  inc=c_ve)
                    V.memset(ar(an + 63, 1, np_=64), 0.0, inc=c_ve)
                    V.drain()
                    V.tensor_copy(ar(p_acar, 1, np_=64), aa[:, 511:512],
                                  inc=c_ve)
                    V.drain()
                    for ou in range(4):
                        for pi, (t_, s_) in enumerate(pairs):
                            V.tensor_tensor(
                                ar(an + 64 + 6 * ou + pi, 1, np_=64),
                                ar(an + 32 + 4 * ou + t_, 1, np_=64),
                                ar(an + 4 * ou + s_, 1, np_=64),
                                op=OP.subtract)
                        V.tensor_tensor(ar(an + 88 + ou, 1, np_=64),
                                        ar(an + 52 + ou, 1, np_=64),
                                        ar(an + 48 + ou, 1, np_=64),
                                        op=OP.subtract,
                                        inc=c_ve if ou == 3 else None)
                    Ac.wait(c_ve)
                    Ac.drain()
                    Ac.activation(ar(an + 64, 28, np_=64),
                                  ar(an + 64, 28, np_=64), AF.Exp, inc=c_ac)
                    anv[pb] = c_ve.cnt
                    psc_acs[pb] = c_ac.cnt
                elif piece == 1:
                    tgp[pb][1] = exp_tile(b, s2_T1 + 512 * pb, 0, 16,
                                          False, rr)
                    tgp[pb][2] = exp_tile(b, s2_T2 + 512 * pb, 0, 16,
                                          True, kk_)
                elif piece == 2:
                    tgp[pb][3] = exp_tile(b, s2_T3 + 512 * pb, 32, 16,
                                          False, rr)
                    tgp[pb][4] = exp_tile(b, s2_T4 + 512 * pb, 48, 4,
                                          False, rr)
                else:
                    tgp[pb][5] = exp_tile(b, s2_T5 + 512 * pb, 52, 4,
                                          True, kk_)
                    rhat_gp[0] = exp_tile(b, p_rhat + 512 * b, 63, 1,
                                          False, rr)

            def uz_load0():
                Sx.wait(c_sp, s1_gd)
                for tt in range(4):
                    Sx.wait(c_xm)
                    Sx.dma_start(out=ar(s2_U + 2048 * tt, 2048),
                                 in_=u_hbm[128 * tt:128 * (tt + 1), :],
                                 inc=c_xm, k=16)
                    Sx.wait(c_y)
                    Sx.dma_start(out=ar(s2_Z + 2048 * tt, 2048),
                                 in_=z_hbm[128 * tt:128 * (tt + 1), :],
                                 inc=c_y, k=16)
                    uzv[tt] = (c_xm.cnt, c_y.cnt)

            def uz_load(b1, ou):
                Sx.wait(c_st, stc[ou])
                Sx.wait(c_pe, dspe[ou])
                Sx.wait(c_xm)
                Sx.dma_start(out=ar(s2_U + 2048 * ou, 2048),
                             in_=u_hbm[512 * b1 + 128 * ou:
                                       512 * b1 + 128 * (ou + 1), :],
                             inc=c_xm, k=16)
                Sx.wait(c_y)
                Sx.dma_start(out=ar(s2_Z + 2048 * ou, 2048),
                             in_=z_hbm[512 * b1 + 128 * ou:
                                       512 * b1 + 128 * (ou + 1), :],
                             inc=c_y, k=16)
                uzv[ou] = (c_xm.cnt, c_y.cnt)

            pairs = [(t_, s_) for t_ in range(1, 4) for s_ in range(t_)]

            def kp_scale(b, ou):
                pb = b % 2
                an = s2_anch + 384 * pb
                V.wait(c_ac, psc_acs[pb])
                V.wait(c_gp, tgp[pb][2])
                V.wait(c_pe)
                V.drain()
                for pi, (t_, s_) in enumerate(pairs):
                    V.tensor_scalar(ar(s2_kp + 192 * (ou % 2) + 32 * pi, 32,
                                       np_=64),
                                    ar(s2_T2 + 512 * pb + 128 * ou + 32 * s_,
                                       32, np_=64),
                                    ar(an + 64 + 6 * ou + pi, 1, np_=64),
                                    None, op0=OP.mult,
                                    inc=c_ve if pi == 5 else None)
                kpv[ou % 2] = c_ve.cnt

            def comp_ou(b, ou):
                pb = b % 2
                an = s2_anch + 384 * pb
                ob = 128 * ou
                if ou == 0:
                    kp_scale(b, 0)
                kp_ve = kpv[ou % 2]
                Tn.wait(c_ve, kp_ve)
                Tn.wait(c_gp, tgp[pb][3])
                for s_ in range(4):
                    sc = 512 * pb + ob + 32 * s_
                    Tn.matmul(psP[32 * s_:32 * s_ + 32,
                                  32 * s_:32 * s_ + 32],
                              lhsT=ar(s2_T2 + sc, 32, np_=64),
                              rhs=ar(s2_T1 + sc, 32, np_=64),
                              start=True, stop=(s_ == 3),
                              tile_position=(0, 32 * s_), inc=c_pe)
                    for t_ in range(s_ + 1, 4):
                        pi = pairs.index((t_, s_))
                        Tn.matmul(psP[32 * s_:32 * s_ + 32,
                                      32 * t_:32 * t_ + 32],
                                  lhsT=ar(s2_kp + 192 * (ou % 2) + 32 * pi,
                                          32, np_=64),
                                  rhs=ar(s2_T3 + 512 * pb + ob + 32 * t_, 32,
                                         np_=64),
                                  start=False, stop=(t_ == 3),
                                  tile_position=(0, 32 * s_), inc=c_pe)
                if ou < 3:
                    kp_scale(b, ou + 1)
                V.wait(c_pe)
                V.drain()
                V.memset(ar(s2_P, 128), 0.0, inc=c_ve)
                V.drain()
                for s_ in range(4):
                    V.tensor_tensor(
                        arena[32 * s_:32 * s_ + 32,
                              s2_P + 32 * s_:s2_P + 32 * s_ + 32],
                        psP[32 * s_:32 * s_ + 32, 32 * s_:32 * s_ + 32],
                        arena[32 * s_:32 * s_ + 32, p_mask:p_mask + 32],
                        op=OP.mult, inc=c_ve)
                    for t_ in range(s_ + 1, 4):
                        V.tensor_copy(
                            arena[32 * s_:32 * s_ + 32,
                                  s2_P + 32 * t_:s2_P + 32 * t_ + 32],
                            psP[32 * s_:32 * s_ + 32,
                                32 * t_:32 * t_ + 32], inc=c_ve)
                p_ve = c_ve.cnt
                Tn.wait(c_ve, p_ve)
                Tn.wait(c_ve, ke_st[0])
                Tn.wait(c_gp, tgp[pb][5])
                Tn.transpose(pst[6][:, 0:64],
                             ar(s2_T5 + 512 * pb + ob, 128, np_=64),
                             ar(p_ident, 128)[0:64, 0:64], inc=c_pe)
                V.wait(c_pe)
                V.drain()
                V.tensor_copy(ar(s2_kE, 64), pst[6][:, 0:64], inc=c_ve)
                ke_st[0] = c_ve.cnt
                Tn.wait(c_xm, uzv[ou][0])
                Tn.wait(c_gp, tgp[pb][4])
                V.wait(c_y, uzv[ou][1])
                for dt in range(4):
                    po = pst[dt % 2]
                    Tn.wait(c_ve, oev2[dt % 2])
                    Tn.matmul(po, lhsT=ar(s2_T4 + 512 * pb + ob, 128,
                                          np_=64),
                              rhs=ar(s2_S + 512 * dt, 512, np_=64),
                              start=True, stop=False)
                    Tn.matmul(po, lhsT=ar(s2_P, 128),
                              rhs=ar(s2_U + 2048 * ou + 512 * dt, 512),
                              start=False, stop=True, inc=c_pe)
                    V.wait(c_pe)
                    V.drain()
                    V.scalar_tensor_tensor(
                        ar(s2_Z + 2048 * ou + 512 * dt, 512), po, SCALE,
                        ar(s2_Z + 2048 * ou + 512 * dt, 512),
                        op0=OP.mult, op1=OP.add, inc=c_ve)
                    oev2[dt % 2] = c_ve.cnt
                yev[ou] = c_ve.cnt
                G.wait(c_ve, sup2[0])
                G.wait(c_ve, sup2[1])
                G.wait(c_ac, psc_acs[pb])
                G.drain()
                G.tensor_scalar(ar(s2_Sb, 2048, np_=64),
                                ar(s2_S, 2048, np_=64),
                                ar(an + 88 + ou, 1, np_=64), zero_t[0:64, :],
                                op0=OP.mult, op1=OP.add, inc=c_gp)
                sb_gp = c_gp.cnt
                Tn.wait(c_ve, ke_st[0])
                for dt in range(4):
                    pd = pst[2 + dt % 2]
                    Tn.wait(c_ve, sup2[dt % 2])
                    Tn.matmul(pd[0:64, :], lhsT=ar(s2_kE, 64),
                              rhs=ar(s2_U + 2048 * ou + 512 * dt, 512),
                              start=True, stop=True, inc=c_pe)
                    V.wait(c_pe)
                    V.wait(c_gp, sb_gp)
                    V.drain()
                    V.tensor_tensor(ar(s2_S + 512 * dt, 512, np_=64),
                                    ar(s2_Sb + 512 * dt, 512, np_=64),
                                    pd[0:64, :], op=OP.add, inc=c_ve)
                    sup2[dt % 2] = c_ve.cnt
                dspe[ou] = c_pe.cnt

            def y_spill(b, ou):
                Ac.wait(c_ve, yev[ou])
                Ac.wait(c_st)
                Ac.dma_start(out=ypart_hbm[512 * b + 128 * ou:
                                           512 * b + 128 * (ou + 1), :],
                             in_=ar(s2_Z + 2048 * ou, 2048),
                             inc=c_st, k=16)
                stc[ou] = c_st.cnt

            rkg_loads(0)
            uz_load0()
            for piece in range(4):
                prep_piece(0, piece)
            for b in range(NB):
                if b + 1 < NB:
                    rkg_loads(b + 1)
                for ou in range(4):
                    comp_ou(b, ou)
                    y_spill(b, ou)
                    if b + 1 < NB:
                        uz_load(b + 1, ou)
                        prep_piece(b + 1, ou)


            if _os.environ.get("KERNEL_NOS3") == "1":
                Sx.wait(c_st)
                Sx.wait(c_sp)
                return
            hb_row4 = ar(s4_LT, 2048, np_=1)
            Sx.wait(c_misc)
            Sx.wait(c_gp)
            Sx.wait(c_ve)
            Sx.wait(c_ac)
            Sx.wait(c_pe)
            Sx.dma_start(out=hb_row4,
                         in_=fbias.rearrange("(one x) -> one x", one=1),
                         inc=c_misc, k=16)
            Ac.wait(c_misc)
            Ac.wait(c_ve)
            Ac.activation(hb_row4, hb_row4, AF.Copy, scale=0.5, inc=c_ac)
            Tn.wait(c_ac)
            bcast_row(hb_row4, s4_hbb)
            hbb_ve = c_ve.cnt
            G.wait(c_ve)
            G.dma_start(out=sc_in[:, 0:2048], in_=ar(s2_S, 2048, np_=64),
                        inc=c_gd, k=16)
            G.wait(c_gd)
            G.dma_start(out=sc_in[:, 2048:2049], in_=ar(p_acar, 1, np_=64),
                        allow_slow_non_contiguous=True, inc=c_gd, k=16)
            G.wait(c_gd)
            G.collective_compute("AllGather", OP.bypass,
                                 replica_groups=[list(range(NCORE))],
                                 ins=[sc_in.opt()], outs=[sc_out.opt()],
                                 inc=c_cc, k=1)
            Sx.wait(c_cc)
            Sx.dma_start(out=ar(s4_A, 64, np_=8),
                         in_=sc_out[:, 2048:2049].rearrange(
                             "(i m) one -> i (m one)", m=M),
                         allow_slow_non_contiguous=True, inc=c_sc, k=16)
            Tn.wait(c_sc)
            Tn.wait(c_ve)
            Tn.matmul(pst[0][0:8, 0:64], lhsT=ar(p_small, 8, np_=8),
                      rhs=ar(s4_A, 64, np_=8), start=True, stop=True, inc=c_pe)
            Ac.wait(c_pe)
            Ac.activation(ar(s4_w8, 64, np_=8), pst[0][0:8, 0:64], AF.Exp,
                          inc=c_ac)
            V.wait(c_ac)
            V.tensor_scalar(ar(s4_w8, 64, np_=8), ar(s4_w8, 64, np_=8),
                            ar(p_small + 8, 1, np_=8), None, op0=OP.mult,
                            inc=c_ve)
            Tn.wait(c_ve)
            Tn.transpose(pst[1][0:64, 0:8], ar(s4_w8, 64, np_=8),
                         ar(p_ident, 128)[0:8, 0:8], inc=c_pe)
            V.wait(c_pe)
            V.tensor_copy(ar(s4_wT, 8, np_=64), pst[1][0:64, 0:8], inc=c_ve)
            V.memset(ar(s4_Sst, 2048, np_=64), 0.0, inc=c_ve)
            Sx.wait(c_ve)
            Sx.wait(c_cc)
            Sx.wait(c_st)
            Sx.wait(c_pe)
            Sx.dma_start(out=ar(s4_Y, 8 * 2048, np_=64).rearrange(
                             "m (i x) -> m i x", x=2048),
                         in_=sc_out[:, 0:2048].rearrange(
                             "(i m) x -> m i x", m=M),
                         inc=c_sc, k=16)
            l_sc = c_sc.cnt
            sst = [s4_Sst, s4_Sst2]
            V.wait(c_sc, l_sc)
            for i in range(8):
                V.drain()
                V.scalar_tensor_tensor(ar(sst[(i + 1) % 2], 2048, np_=64),
                                       ar(s4_Y + 2048 * i, 2048, np_=64),
                                       ar(s4_wT + i, 1, np_=64),
                                       ar(sst[i % 2], 2048, np_=64),
                                       op0=OP.mult, op1=OP.add, inc=c_ve)
            sstart_col = sst[8 % 2]
            sst_ve = c_ve.cnt

            if _os.environ.get("KERNEL_NOS4") == "1":
                Sx.wait(c_st)
                Sx.wait(c_gd)
                return
            wfree2 = [0, 0]
            wdma2 = {}
            ffn_evac2 = [0, 0]
            oev = [0, 0]
            yfree = [0, 0]
            lffree2 = [0, 0]
            Y_2 = [s4_Y, s4_Y + 4096]
            LF_2 = [s4_LF, s4_LF + 4096]
            def w_issue2(d):
                if d >= 16:
                    return
                Sx.wait(c_pe, wfree2[d % 2])
                cwx = c_wa if d % 2 == 0 else c_wb
                e = d // 4
                dc0 = (4 * d) % 16
                for i2 in range(2):
                    Sx.dma_start(
                        out=ar(s4_W + 2048 * (d % 2), 2048).rearrange(
                            "p (c xi) -> p c xi", xi=512
                            )[:, :, 256 * i2:256 * i2 + 256],
                        in_=wq[e][:, dc0:dc0 + 4, i2],
                        inc=cwx, k=16)
                wdma2[d] = (cwx, cwx.cnt)

            def ffn2_half(sl, grp, es, lf_ve2):
                for e in es:
                    for dc in range(16):
                        q = 16 * e + dc
                        d = q // 4
                        slot = q % 8
                        wc, wv = wdma2[d]
                        Tn.wait(wc, wv)
                        if dc == 0:
                            Tn.wait(c_ve, lf_ve2)
                            Tn.wait(c_ve, ffn_evac2[e % 2])
                        for gi in range(len(grp)):
                            Tn.matmul(pst[2 * (e % 2) + gi],
                                      lhsT=ar(LF_2[sl] + 2048 * gi + 128 * dc,
                                              128),
                                      rhs=ar(s4_W + 512 * slot, 512),
                                      start=(dc == 0), stop=(dc == 15),
                                      inc=c_pe if gi == len(grp) - 1 else None)
                        if slot == 7 or slot == 3:
                            wfree2[(q // 4) % 2] = c_pe.cnt
                            w_issue2(q // 4 + 2)
                    for gi in range(len(grp)):
                        V.wait(c_pe)
                        V.drain()
                        V.scalar_tensor_tensor(
                            ar(Y_2[sl] + 2048 * gi + 512 * e, 512),
                            pst[2 * (e % 2) + gi], 1.0,
                            ar(Y_2[sl] + 2048 * gi + 512 * e, 512),
                            op0=OP.mult, op1=OP.add, inc=c_ve)
                    ffn_evac2[e % 2] = c_ve.cnt

            grps2 = [[2 * g, 2 * g + 1] for g in range(NT // 2)]
            Tn.wait(c_gp, rhat_gp[0])
            Sx.wait(c_st)
            Sx.wait(c_ve, sst_ve)
            pend2 = None
            for gidx, grp in enumerate(grps2):
                sl = gidx % 2
                if pend2 is not None:
                    w_issue2(0)
                    w_issue2(1)
                Sx.wait(c_gd, yfree[sl])
                for gi, tl in enumerate(grp):
                    Sx.wait(c_y)
                    Sx.dma_start(out=ar(Y_2[sl] + 2048 * gi, 2048),
                                 in_=ypart_hbm[128 * tl:128 * (tl + 1), :],
                                 inc=c_y, k=16)
                y_ld = c_y.cnt
                if pend2 is not None:
                    ffn2_half(pend2[0], pend2[1], [0, 1], pend2[2])
                oevt = [0, 0]
                for gi, tl in enumerate(grp):
                    Tn.wait(c_y, y_ld)
                    Tn.wait(c_ve)
                    for dt in range(4):
                        po = pst[4] if dt % 2 == 0 else pst[6]
                        Tn.wait(c_ve, oev[dt % 2])
                        Tn.matmul(po, lhsT=ar(p_rhat + 128 * tl, 128, np_=64),
                                  rhs=ar(sstart_col + 512 * dt, 512, np_=64),
                                  start=True, stop=True, inc=c_pe)
                        V.wait(c_pe)
                        V.wait(c_ve, sst_ve)
                        V.wait(c_y, y_ld)
                        V.drain()
                        V.scalar_tensor_tensor(
                            ar(Y_2[sl] + 2048 * gi + 512 * dt, 512), po, SCALE,
                            ar(Y_2[sl] + 2048 * gi + 512 * dt, 512),
                            op0=OP.mult, op1=OP.add, inc=c_ve)
                        oev[dt % 2] = c_ve.cnt
                    oevt[gi] = c_ve.cnt
                for gi, tl in enumerate(grp):
                    Ac.wait(c_ve, oevt[gi])
                    V.wait(c_pe, lffree2[sl])
                    ln_stats_apply(ar(Y_2[sl] + 2048 * gi, 2048),
                                   ar(s4_LT, 2048), C1, p_stat + 32 * (gi % 2))
                    transpose_tile(s4_LT, LF_2[sl] + 2048 * gi, 128, 0)
                lf_ve2 = c_ve.cnt

                def spill_grp(psl, pgrp):
                    for gi, tl in enumerate(pgrp):
                        G.wait(c_ve)
                        G.wait(c_ve, hbb_ve)
                        yt = ar(Y_2[psl] + 2048 * gi, 2048)
                        G.tensor_tensor(yt, yt, ar(s4_hbb, 2048), op=OP.add,
                                        inc=c_gp)
                        G.drain()
                        G.wait(c_gd)
                        G.dma_start(out=yout[128 * tl:128 * (tl + 1), :],
                                    in_=yt, inc=c_gd, k=16)
                    yfree[psl] = c_gd.cnt

                if pend2 is not None:
                    ffn2_half(pend2[0], pend2[1], [2, 3], pend2[2])
                    lffree2[pend2[0]] = c_pe.cnt
                    spill_grp(pend2[0], pend2[1])
                pend2 = (sl, grp, lf_ve2)
            w_issue2(0)
            w_issue2(1)
            ffn2_half(pend2[0], pend2[1], [0, 1], pend2[2])
            ffn2_half(pend2[0], pend2[1], [2, 3], pend2[2])
            spill_grp(pend2[0], pend2[1])
            out_gd = c_gd.cnt
            Sx.wait(c_gd, out_gd)

        @block.sync
        def _(eng):
            for c in ctrs:
                c.cnt = 0
            Sx.real = eng
            schedule()
            Sx.real = None

        @block.vector
        def _(eng):
            for c in ctrs:
                c.cnt = 0
            V.real = eng
            schedule()
            V.real = None

        @block.tensor
        def _(eng):
            for c in ctrs:
                c.cnt = 0
            Tn.real = eng
            schedule()
            Tn.real = None

        @block.scalar
        def _(eng):
            for c in ctrs:
                c.cnt = 0
            Ac.real = eng
            schedule()
            Ac.real = None

        @block.gpsimd
        def _(eng):
            for c in ctrs:
                c.cnt = 0
            G.real = eng
            schedule()
            G.real = None

    return nc


_NC_CACHE = [None]


def kernel(L0_attn_hidden_states, L0_attn_conv_w, L0_attn_conv_b,
           L0_attn_in_proj_w, L0_attn_ch_gate_w0, L0_attn_ch_gate_w1,
           L0_attn_ch_gate_b1, L0_attn_mem_gate_w, L0_attn_mem_gate_b,
           L0_attn_residual_weight, L0_ffn_factors, L0_ffn_bias):
    f32 = np.float32
    x = np.ascontiguousarray(np.asarray(L0_attn_hidden_states, f32)[0])
    factors = np.ascontiguousarray(np.asarray(L0_ffn_factors, f32))
    ipT = np.ascontiguousarray(np.asarray(L0_attn_in_proj_w, f32).T)
    mgT = np.ascontiguousarray(np.asarray(L0_attn_mem_gate_w, f32).T)
    ch0T = np.ascontiguousarray(np.asarray(L0_attn_ch_gate_w0, f32).T)
    ch1rT = np.ascontiguousarray(np.asarray(L0_attn_ch_gate_w1, f32).T)
    convw = np.ascontiguousarray(np.asarray(L0_attn_conv_w, f32))
    convb = np.ascontiguousarray(np.asarray(L0_attn_conv_b, f32))
    mgb = np.ascontiguousarray(np.asarray(L0_attn_mem_gate_b, f32).reshape(2, 64).T)
    resw = np.ascontiguousarray(np.asarray(L0_attn_residual_weight, f32))
    ch1b = np.ascontiguousarray(np.asarray(L0_attn_ch_gate_b1, f32))
    fbias = np.ascontiguousarray(np.asarray(L0_ffn_bias, f32))

    ident = np.eye(128, dtype=f32)
    mask32 = np.tile(np.triu(np.ones((32, 32), f32)), (4, 1))
    ones_t = np.ones((128, 512), f32)

    if _NC_CACHE[0] is None:
        _NC_CACHE[0] = build_nc()
    nc = _NC_CACHE[0]

    in_maps = []
    for c in range(NCORE):
        s0 = c * SEG
        xp = np.zeros(((NT + 1) * 128, D), f32)
        if c > 0:
            xp[0:3] = x[s0 - 3:s0]
        xp[128:128 + SEG] = x[s0:s0 + SEG]
        fsl = np.ascontiguousarray(factors[:, :, ESL * c:ESL * (c + 1)])
        mj = np.zeros((8, 8), f32)
        sel = np.zeros((8, 1), f32)
        for i in range(c):
            sel[i, 0] = 1.0
            for l in range(i + 1, c):
                mj[l, i] = 1.0
        in_maps.append(dict(
            xin=xp, fsl=fsl, ipT=ipT, mgT=mgT, ch0T=ch0T, ch1rT=ch1rT,
            convw=convw, convb=convb, mgb=mgb, resw=resw, ch1b=ch1b,
            fbias=fbias, identi=ident, mask32i=mask32, onesi=ones_t,
            mji=mj, seli=sel))

    res = run_bass_kernel_spmd(nc, in_maps, core_ids=list(range(NCORE)))
    out = np.concatenate([r["yout"] for r in res.results], axis=0)
    return out[None, :, :]


# version 26
# speedup vs baseline: 1.0477x; 1.0233x over previous
import contextlib
import os as _os
import numpy as np
import concourse.bass as bass
import concourse.mybir as mybir
from concourse.bass_utils import run_bass_kernel_spmd
from concourse.alu_op_type import AluOpType

F32 = mybir.dt.float32
AF = mybir.ActivationFunctionType
OP = AluOpType
AX = mybir.AxisListType

NCORE = 8
T_TOT, D, M = 16384, 2048, 64
if _os.environ.get("KERNEL_SMALL") == "1":
    T_TOT = 8192
SEG = T_TOT // NCORE
NT = SEG // 128
NB = SEG // 512
ESL = D // NCORE
SCALE = float(1.0 / np.sqrt(M))
C1 = float(0.5 / np.sqrt(8.0))
EPS = 1e-5
HFS = 260


class Ctr:
    def __init__(self, handle):
        self.h = handle
        self.cnt = 0


class _FakeIns:
    def then_inc(self, h, k):
        return self


class _Prox:

    def __init__(self, name):
        self.name = name
        self.real = None

    def wait(self, ctr, val=None):
        v = ctr.cnt if val is None else val
        if self.real is not None and v > 0:
            self.real.wait_ge(ctr.h, v)

    def __getattr__(self, attr):
        def call(*a, _attr=attr, inc=None, k=1, **kw):
            if self.real is not None:
                ins = getattr(self.real, _attr)(*a, **kw)
            else:
                ins = _FakeIns()
            if inc is not None:
                ins.then_inc(inc.h, k)
                inc.cnt += k
            return ins
        return call


def build_nc():
    nc = bass.Bass("TRN2", target_bir_lowering=False)

    xin = nc.dram_tensor("xin", [(NT + 1) * 128, D], F32, kind="ExternalInput").ap()
    fsl = nc.dram_tensor("fsl", [8, D, ESL], F32, kind="ExternalInput").ap()
    ipT = nc.dram_tensor("ipT", [D, 128], F32, kind="ExternalInput").ap()
    mgT = nc.dram_tensor("mgT", [D, 128], F32, kind="ExternalInput").ap()
    ch0T = nc.dram_tensor("ch0T", [D, 128], F32, kind="ExternalInput").ap()
    ch1rT = nc.dram_tensor("ch1rT", [128, D], F32, kind="ExternalInput").ap()
    convw = nc.dram_tensor("convw", [D, 4], F32, kind="ExternalInput").ap()
    convb = nc.dram_tensor("convb", [D], F32, kind="ExternalInput").ap()
    mgb = nc.dram_tensor("mgb", [64, 2], F32, kind="ExternalInput").ap()
    resw = nc.dram_tensor("resw", [D], F32, kind="ExternalInput").ap()
    ch1b = nc.dram_tensor("ch1b", [D], F32, kind="ExternalInput").ap()
    fbias = nc.dram_tensor("fbias", [D], F32, kind="ExternalInput").ap()
    ident_i = nc.dram_tensor("identi", [128, 128], F32, kind="ExternalInput").ap()
    mask32_i = nc.dram_tensor("mask32i", [128, 32], F32, kind="ExternalInput").ap()
    ones_i = nc.dram_tensor("onesi", [128, 512], F32, kind="ExternalInput").ap()
    mj_i = nc.dram_tensor("mji", [8, 8], F32, kind="ExternalInput").ap()
    sel_i = nc.dram_tensor("seli", [8, 1], F32, kind="ExternalInput").ap()

    yout = nc.dram_tensor("yout", [SEG, D], F32, kind="ExternalOutput").ap()

    cc_in = nc.dram_tensor("cc_in", [D, ESL], F32).ap()
    cc_out = nc.dram_tensor("cc_out", [NCORE * D, ESL], F32).ap()
    w_hbm = nc.dram_tensor("w_hbm", [64 * 128, 512], F32).ap()
    xmid_hbm = nc.dram_tensor("xmid_hbm", [(NT + 1) * 128, D], F32).ap()
    ypart_hbm = nc.dram_tensor("ypart_hbm", [SEG, D], F32).ap()
    u_hbm = nc.dram_tensor("u_hbm", [SEG, D], F32).ap()
    z_hbm = nc.dram_tensor("z_hbm", [SEG, D], F32).ap()
    r_hbm = nc.dram_tensor("r_hbm", [M, SEG], F32).ap()
    k_hbm = nc.dram_tensor("k_hbm", [M, SEG], F32).ap()
    g_hbm = nc.dram_tensor("g_hbm", [M, SEG], F32).ap()
    sc_in = nc.dram_tensor("sc_in", [M, 2049], F32).ap()
    sc_out = nc.dram_tensor("sc_out", [NCORE * M, 2049], F32).ap()

    es = contextlib.ExitStack()
    with es:
        AR = 47616
        arena = es.enter_context(nc.sbuf_tensor("arena", [128, AR], F32)).ap()

        pos = [0]

        def A_(n):
            p = pos[0]
            pos[0] += n
            assert pos[0] <= AR, f"arena overflow {pos[0]}"
            return p

        p_ipT = A_(2048); p_mgT = A_(2048); p_ch0T = A_(2048); p_ch1rT = A_(2048)
        p_convw = A_(64); p_convb = A_(16); p_resw = A_(16); p_mgb = A_(2)
        p_ident = A_(128); p_mask = A_(32); p_ones = A_(512)
        p_acar = A_(4); p_zero = A_(1); p_eps = A_(1)
        p_stat = A_(64); p_small = A_(64)

        stage0 = pos[0]

        def stage_alloc(sizes):
            out, q = [], stage0
            for n in sizes:
                out.append(q)
                q += n
            assert q <= AR, f"stage arena overflow {q}"
            return out

        (s1_Xa, s1_Xb, s1_LFa, s1_LFb, s1_LT, s1_W, s1_hbb,
         s1_HF, s1_SH, s1_U,
         s1_k0, s1_sel, s1_tau, s1_it, s1_dif, s1_an, s1_ch, s1_srw,
         s1_gacc, s1_ch1bb) = stage_alloc(
            [2 * 2048, 2 * 2048, 2 * 2048, 2 * 2048, 2048, 8 * 512, 2048,
             16 * 132, 3 * 2048, 2048,
             128, 128, 128, 128, 128, 128, 128, 4 * 128,
             128, 2048])
        s1_WS = s1_LFa
        (s2_U, s2_Z, s2_a, s2_r, s2_k, s2_g,
         s2_T1, s2_T2, s2_T3, s2_T4, s2_T5, s2_dif,
         s2_P, s2_kE, s2_kp, s2_anch,
         s2_S, s2_Sb) = stage_alloc(
            [4 * 2048, 4 * 2048, 2 * 512, 2 * 512, 2 * 512, 2 * 512,
             2 * 512, 2 * 512, 2 * 512, 2 * 512, 2 * 512, 2 * 512,
             128, 64, 2 * 192, 2 * 384,
             2048, 2048])
        (s4_Y, s4_LF, s4_LT, s4_W, s4_L, s4_A, s4_w8, s4_wT,
         s4_Sst, s4_Sst2, _s4pad, s4_hbb, p_rhat) = stage_alloc(
            [4 * 2048, 4 * 2048, 2048, 8 * 512, 2048, 64, 64, 8,
             2048, 2048, 3256, 2048, 2048])
        assert s4_hbb >= s2_Sb + 2048

        globals()["LAYOUT"] = {k: v for k, v in locals().items()
                               if isinstance(v, int)}
        pst = [es.enter_context(nc.psum_tensor(f"ps{i}", [128, 512], F32)).ap()
               for i in range(7)]
        psPf = es.enter_context(nc.psum_tensor("psP", [128, 512], F32)).ap()
        psP = psPf[:, 0:128]
        psP2 = psPf[:, 128:256]

        c_ld = Ctr(es.enter_context(nc.semaphore("c_ld")))
        c_pre = Ctr(es.enter_context(nc.semaphore("c_pre")))
        c_x = Ctr(es.enter_context(nc.semaphore("c_x")))
        c_wa = Ctr(es.enter_context(nc.semaphore("c_wa")))
        c_wb = Ctr(es.enter_context(nc.semaphore("c_wb")))
        c_xm = Ctr(es.enter_context(nc.semaphore("c_xm")))
        c_misc = Ctr(es.enter_context(nc.semaphore("c_misc")))
        c_sc = Ctr(es.enter_context(nc.semaphore("c_sc")))
        c_y = Ctr(es.enter_context(nc.semaphore("c_y")))
        c_sp = Ctr(es.enter_context(nc.semaphore("c_sp")))
        c_gp = Ctr(es.enter_context(nc.semaphore("c_gp")))
        c_gd = Ctr(es.enter_context(nc.semaphore("c_gd")))
        c_cc = Ctr(es.enter_context(nc.semaphore("c_cc")))
        c_pe = Ctr(es.enter_context(nc.semaphore("c_pe")))
        c_ve = Ctr(es.enter_context(nc.semaphore("c_ve")))
        c_ac = Ctr(es.enter_context(nc.semaphore("c_ac")))
        c_st = Ctr(es.enter_context(nc.semaphore("c_st")))
        ctrs = [c_ld, c_pre, c_x, c_wa, c_wb, c_xm, c_misc, c_sc, c_y, c_sp, c_gp, c_gd, c_cc, c_pe, c_ve, c_ac, c_st]
        block = es.enter_context(nc.Block())

        def ar(col, ncol, p0=0, np_=128):
            return arena[p0:p0 + np_, col:col + ncol]

        Sx = _Prox("sync"); V = _Prox("vector"); Tn = _Prox("tensor")
        Ac = _Prox("scalar"); G = _Prox("gpsimd")

        def schedule():
            Sx.dma_start(out=ar(p_ipT, 2048).rearrange("p (dc r) -> p dc r", r=128),
                         in_=ipT.rearrange("(dc p) r -> p dc r", p=128),
                         inc=c_ld, k=16)
            Sx.dma_start(out=ar(p_mgT, 2048).rearrange("p (dc r) -> p dc r", r=128),
                         in_=mgT.rearrange("(dc p) r -> p dc r", p=128),
                         inc=c_ld, k=16)
            Sx.dma_start(out=ar(p_ch0T, 2048).rearrange("p (dc r) -> p dc r", r=128),
                         in_=ch0T.rearrange("(dc p) r -> p dc r", p=128),
                         inc=c_ld, k=16)
            Sx.dma_start(out=ar(p_ch1rT, 2048), in_=ch1rT, inc=c_ld, k=16)
            Sx.dma_start(out=ar(p_convw, 64).rearrange("p (dc w) -> p dc w", w=4),
                         in_=convw.rearrange("(dc p) w -> p dc w", p=128),
                         inc=c_ld, k=16)
            Sx.dma_start(out=ar(p_convb, 16),
                         in_=convb.rearrange("(dc p) -> p dc", p=128),
                         allow_slow_non_contiguous=True, inc=c_ld, k=16)
            Sx.dma_start(out=ar(p_resw, 16),
                         in_=resw.rearrange("(dc p) -> p dc", p=128),
                         allow_slow_non_contiguous=True, inc=c_ld, k=16)
            Sx.dma_start(out=ar(p_mgb, 2, np_=64), in_=mgb, inc=c_ld, k=16)
            Sx.dma_start(out=ar(p_ident, 128), in_=ident_i, inc=c_ld, k=16)
            Sx.dma_start(out=ar(p_mask, 32), in_=mask32_i, inc=c_ld, k=16)
            Sx.dma_start(out=ar(p_ones, 512), in_=ones_i, inc=c_ld, k=16)
            Sx.dma_start(out=ar(p_small, 8, np_=8), in_=mj_i, inc=c_ld, k=16)
            Sx.dma_start(out=ar(p_small + 8, 1, np_=8), in_=sel_i, inc=c_ld, k=16)
            hb_row = ar(s1_LT, 2048, np_=1)
            cb_row = hb_row
            Sx.dma_start(out=hb_row, in_=fbias.rearrange("(one x) -> one x", one=1),
                         inc=c_ld, k=16)
            ld_all = c_ld.cnt

            Ac.wait(c_ld, ld_all)
            Ac.activation(hb_row, hb_row, AF.Copy, scale=0.5, inc=c_ac)
            Tn.wait(c_ac)
            Tn.wait(c_ld, ld_all)

            def bcast_row(src_row, dst):
                for j in range(4):
                    Tn.wait(c_ve)
                    Tn.matmul(pst[0], lhsT=ar(p_ones, 128, np_=1),
                              rhs=src_row[:, 512 * j:512 * (j + 1)],
                              start=True, stop=True, inc=c_pe)
                    V.wait(c_pe)
                    V.tensor_copy(ar(dst + 512 * j, 512), pst[0], inc=c_ve)

            bcast_row(hb_row, s1_hbb)
            V.memset(ar(p_zero, 1), 0.0, inc=c_ve)
            V.memset(ar(p_eps, 1), EPS, inc=c_ve)
            V.memset(ar(p_acar, 1, np_=64), 0.0, inc=c_ve)
            consts_done = c_ve.cnt
            eps_t = ar(p_eps, 1)
            zero_t = ar(p_zero, 1)

            for rr in range(8):
                G.wait(c_gd)
                kw = {} if rr == 0 else dict(accum_op=OP.add)
                G.dma_start(out=ar(s1_WS, 4096).rearrange(
                                "p (dc x) -> p dc x", x=ESL),
                            in_=fsl[rr].rearrange("(dc p) x -> p dc x", p=128),
                            inc=c_gd, k=16, **kw)
            G.wait(c_gd)
            G.dma_start(out=cc_in.rearrange("(dc p) x -> p dc x", p=128),
                        in_=ar(s1_WS, 4096).rearrange(
                            "p (dc x) -> p dc x", x=ESL),
                        inc=c_gd, k=16)
            G.wait(c_gd)
            ws_gd = c_gd.cnt
            G.collective_compute("AllGather", OP.bypass,
                                 replica_groups=[list(range(NCORE))],
                                 ins=[cc_in.opt()], outs=[cc_out.opt()],
                                 inc=c_cc, k=1)
            ag_cc = c_cc.cnt



            def ln_stats_apply(xap, out_ap, cfold, st):
                V.drain()
                for j in range(4):
                    V.bn_stats(ar(st + 6 * j, 6), xap[:, 512 * j:512 * (j + 1)])
                V.drain()
                V.bn_aggr(ar(st + 24, 2), ar(st, 24), inc=c_ve)
                Ac.wait(c_ve)
                Ac.drain()
                Ac.activation(ar(st + 26, 1), ar(st + 25, 1), AF.Ln,
                              bias=eps_t, scale=1.0)
                Ac.drain()
                Ac.activation(ar(st + 26, 1), ar(st + 26, 1), AF.Exp,
                              scale=-0.5, inc=c_ac)
                V.wait(c_ac)
                V.drain()
                V.tensor_scalar(ar(st + 26, 1), ar(st + 26, 1), cfold, None,
                                op0=OP.mult, inc=c_ve)
                V.drain()
                V.tensor_scalar(ar(st + 27, 1), ar(st + 24, 1),
                                ar(st + 26, 1), -1.0,
                                op0=OP.mult, op1=OP.mult, inc=c_ve)
                V.drain()
                V.tensor_scalar(out_ap, xap, ar(st + 26, 1), ar(st + 27, 1),
                                op0=OP.mult, op1=OP.add, inc=c_ve)

            def transpose_tile(src_col, dst_col, dst_stride, dst_off, nsub=16):
                Tn.wait(c_ac)
                evac_done = [0, 0]
                for dq in range(nsub // 4):
                    pt = pst[4] if dq % 2 == 0 else pst[6]
                    Tn.wait(c_ve, evac_done[dq % 2])
                    for dj in range(4):
                        dc = 4 * dq + dj
                        Tn.transpose(pt[:, 128 * dj:128 * (dj + 1)],
                                     ar(src_col + 128 * dc, 128),
                                     ar(p_ident, 128),
                                     inc=c_pe if dj == 3 else None)
                    V.wait(c_pe)
                    base = dst_col + dst_stride * 4 * dq + dst_off
                    if dst_stride == 128:
                        V.tensor_copy(ar(base, 512), pt, inc=c_ve)
                    else:
                        lo = dst_col + dst_stride * 4 * dq
                        V.tensor_copy(
                            arena[:, lo:lo + 4 * dst_stride].rearrange(
                                "p (g x) -> p g x", x=dst_stride)[
                                :, :, dst_off:dst_off + 128],
                            pt.rearrange("p (g x) -> p g x", x=128),
                            inc=c_ve)
                    evac_done[dq % 2] = c_ve.cnt

            ones64 = ar(p_ones, 512, np_=64)
            Sx.wait(c_pe)
            Sx.wait(c_misc)
            Sx.dma_start(out=cb_row,
                         in_=ch1b.rearrange("(one x) -> one x", one=1),
                         inc=c_misc, k=16)
            Tn.wait(c_misc)
            bcast_row(cb_row, s1_ch1bb)

            srw_last = [0]
            z_evac = [0, 0]
            pst_v_free = [0]
            pspfree = [0]
            psp2free = [0]
            X_ = [s1_Xa, s1_Xb]
            LF_ = [s1_LFa, s1_LFb]
            xfree = [[0, 0], [0, 0]]
            lffree = [0, 0]
            wfree = [0, 0]
            ffn_evac = [0, 0]
            wdma = {}
            tst = {}
            wq = cc_out.rearrange("(e2 i2 dc p) xi -> e2 p dc i2 xi",
                                  e2=4, i2=2, p=128)

            def w_issue(d):
                if d >= 16:
                    return
                Sx.wait(c_cc, ag_cc)
                Sx.wait(c_pe, wfree[d % 2])
                cwx = c_wa if d % 2 == 0 else c_wb
                e = d // 4
                dc0 = (4 * d) % 16
                for i2 in range(2):
                    Sx.dma_start(
                        out=ar(s1_W + 2048 * (d % 2), 2048).rearrange(
                            "p (c xi) -> p c xi", xi=512
                            )[:, :, 256 * i2:256 * i2 + 256],
                        in_=wq[e][:, dc0:dc0 + 4, i2],
                        inc=cwx, k=16)
                wdma[d] = (cwx, cwx.cnt)

            def post_a(sl, gi, tl):
                rt = tl - 1
                xm = ar(X_[sl] + 2048 * gi, 2048)
                G.wait(c_ve)
                G.tensor_tensor(xm, xm, ar(s1_hbb, 2048), op=OP.add, inc=c_gp)
                V.wait(c_gp)
                Ac.wait(c_gp)
                Ac.wait(c_pe)
                ln_stats_apply(xm, ar(s1_LT, 2048), 1.0, p_stat)
                V.wait(c_ac)
                V.wait(c_gp)
                V.drain()
                if tl == 1:
                    V.tensor_copy(
                        arena[:, s1_HF:s1_HF + 16 * 132].rearrange(
                            "p (g x) -> p g x", x=132)[:, :, 1:4],
                        arena[:, s1_HF:s1_HF + 16 * 132].rearrange(
                            "p (g x) -> p g x", x=132)[:, :, 4:7], inc=c_ve)
                elif tl > 1:
                    V.tensor_copy(
                        arena[:, s1_HF:s1_HF + 16 * 132].rearrange(
                            "p (g x) -> p g x", x=132)[:, :, 1:4],
                        arena[:, s1_HF:s1_HF + 16 * 132].rearrange(
                            "p (g x) -> p g x", x=132)[:, :, 129:132],
                        inc=c_ve)
                carry_ve = c_ve.cnt
                Tn.wait(c_ve, carry_ve)
                transpose_tile(s1_LT, s1_HF, 132, 4)
                if tl == 0:
                    return
                hf_ve = c_ve.cnt
                Tn.wait(c_ve, hf_ve)
                for dc in range(16):
                    Tn.matmul(pst[5][:, 0:128],
                              lhsT=ar(p_ch0T + 128 * dc, 128),
                              rhs=ar(s1_HF + 132 * dc + 4, 128),
                              start=(dc == 0), stop=(dc == 15),
                              inc=c_pe if dc == 15 else None)
                V.wait(c_pe)
                V.drain()
                V.tensor_copy(ar(s1_ch, 128), pst[5][:, 0:128], inc=c_ve)
                ch_ve = c_ve.cnt
                Tn.wait(c_ve, ch_ve)
                pr_ve = ffn_evac[1]
                for pr in range(2):
                    Tn.wait(c_ve, pr_ve)
                    for j in range(2):
                        e = 2 * pr + j
                        Tn.matmul(pst[2 + j], lhsT=ar(s1_ch, 128),
                                  rhs=ar(p_ch1rT + 512 * e, 512),
                                  start=True, stop=True,
                                  inc=c_pe if j == 1 else None)
                    V.wait(c_pe)
                    if pr == 0:
                        V.wait(c_sp)
                    V.drain()
                    for j in range(2):
                        e = 2 * pr + j
                        V.scalar_tensor_tensor(
                            ar(s1_U + 512 * e, 512), pst[2 + j], 1.0,
                            ar(s1_ch1bb + 512 * e, 512),
                            op0=OP.mult, op1=OP.add,
                            inc=c_ve if j == 1 else None)
                    pr_ve = c_ve.cnt
                pst_v_free[0] = c_ve.cnt
                Ac.wait(c_ve)
                Ac.drain()
                for e in range(4):
                    Ac.activation(ar(s1_U + 512 * e, 512),
                                  ar(s1_U + 512 * e, 512), AF.Sigmoid,
                                  inc=c_ac if e == 3 else None)
                V.wait(c_ac)
                V.drain()
                for e in range(4):
                    V.tensor_tensor(ar(s1_U + 512 * e, 512),
                                    ar(s1_U + 512 * e, 512),
                                    ar(s1_LT + 512 * e, 512),
                                    op=OP.mult, inc=c_ve if e == 3 else None)
                u_ve = c_ve.cnt
                G.wait(c_ve, u_ve)
                G.wait(c_sp)
                G.dma_start(out=u_hbm[128 * rt:128 * (rt + 1), :],
                            in_=ar(s1_U, 2048), inc=c_sp, k=16)
                u_sp = c_sp.cnt
                sh = s1_SH + 2048 * (tl % 3)
                V.wait(c_ac)
                silu_done = [0, 0]
                for dc in range(8):
                    hb_ = s1_HF + 132 * dc
                    pc = pst[4] if dc % 2 == 0 else pst[6]
                    V.wait(c_ac, silu_done[dc % 2])
                    V.wait(c_pe)
                    V.drain()
                    V.tensor_scalar(pc[:, 0:128], ar(hb_ + 1, 128),
                                    ar(p_convw + 4 * dc, 1), None,
                                    op0=OP.mult, inc=c_ve)
                    for i in range(1, 4):
                        V.drain()
                        V.scalar_tensor_tensor(
                            pc[:, 0:128], ar(hb_ + 1 + i, 128),
                            ar(p_convw + 4 * dc + i, 1), pc[:, 0:128],
                            op0=OP.mult, op1=OP.add, inc=c_ve)
                    Ac.wait(c_ve)
                    Ac.activation(ar(sh + 128 * dc, 128), pc[:, 0:128],
                                  AF.Silu, bias=ar(p_convb + dc, 1), scale=1.0,
                                  inc=c_ac)
                    silu_done[dc % 2] = c_ac.cnt
                G.wait(c_pe)
                for dc in range(8, 16):
                    hb_ = s1_HF + 132 * dc
                    G.wait(c_ac)
                    G.drain()
                    G.tensor_scalar(ar(s1_gacc, 128), ar(hb_ + 1, 128),
                                    ar(p_convw + 4 * dc, 1), zero_t,
                                    op0=OP.mult, op1=OP.add, inc=c_gp)
                    for i in range(1, 4):
                        G.drain()
                        G.tensor_scalar(ar(s1_an, 128), ar(hb_ + 1 + i, 128),
                                        ar(p_convw + 4 * dc + i, 1), zero_t,
                                        op0=OP.mult, op1=OP.add, inc=c_gp)
                        G.drain()
                        G.tensor_tensor(ar(s1_gacc, 128), ar(s1_gacc, 128),
                                        ar(s1_an, 128), op=OP.add, inc=c_gp)
                    Ac.wait(c_gp)
                    Ac.drain()
                    Ac.activation(ar(sh + 128 * dc, 128), ar(s1_gacc, 128),
                                  AF.Silu, bias=ar(p_convb + dc, 1), scale=1.0,
                                  inc=c_ac)
                tst[(sl, gi)] = (c_ac.cnt, u_sp)

            def post_b(sl, gi, tl):
                if tl == 0:
                    return
                rt = tl - 1
                silu_ac, u_sp = tst[(sl, gi)]
                sh = s1_SH + 2048 * (tl % 3)
                Tn.wait(c_ac, max(silu_ac, psp2free[0]))
                Tn.wait(c_ve, pspfree[0])

                def gate_mm(ps, wcol, mov_base):
                    for dc in range(16):
                        Tn.matmul(ps, lhsT=ar(wcol + 128 * dc, 128),
                                  rhs=ar(mov_base + 128 * dc, 128),
                                  start=(dc == 0), stop=(dc == 15),
                                  inc=c_pe if dc == 15 else None)

                gate_mm(psP[0:128, 0:128], p_ipT, sh)
                gate_mm(psP2[0:128, 0:128], p_mgT, sh)
                gates_pe = c_pe.cnt
                V.wait(c_pe, gates_pe)
                V.wait(c_sp)
                V.drain()
                V.tensor_copy(ar(s1_dif, 128, np_=64), psP[0:64, 0:128],
                              inc=c_ve)
                V.tensor_copy(ar(s1_k0, 128, np_=64), psP[64:128, 0:128],
                              inc=c_ve)
                r_ve = c_ve.cnt
                G.wait(c_ve, r_ve)
                G.dma_start(out=r_hbm[:, 128 * rt:128 * (rt + 1)],
                            in_=ar(s1_dif, 128, np_=64), inc=c_sp, k=16)
                sb_ = [s1_srw, s1_U]
                scale_gp = [0, 0, 0, 0]
                tpe = [0, 0, 0, 0]

                def srw_scales(dq):
                    G.wait(c_ac, silu_ac)
                    if dq < 2:
                        G.wait(c_pe, srw_last[0])
                    else:
                        G.wait(c_pe, tpe[dq - 2])
                    if dq % 2 == 1:
                        G.wait(c_sp, u_sp)
                    G.drain()
                    for dj in range(4):
                        dc = 4 * dq + dj
                        G.tensor_scalar(ar(sb_[dq % 2] + 128 * dj, 128),
                                        ar(sh + 128 * dc, 128),
                                        ar(p_resw + dc, 1), zero_t,
                                        op0=OP.mult, op1=OP.add, inc=c_gp)
                    scale_gp[dq] = c_gp.cnt

                srw_scales(0)
                srw_scales(1)
                Ac.wait(c_pe, gates_pe)
                Ac.wait(c_ve, r_ve)
                Ac.drain()
                Ac.activation(ar(s1_sel, 128, np_=64), psP2[0:64, 0:128],
                              AF.Exp, bias=ar(p_mgb, 1, np_=64), scale=1.0)
                Ac.drain()
                Ac.activation(ar(s1_sel, 128, np_=64), ar(s1_sel, 128, np_=64),
                              AF.Ln, bias=ones64[:, 0:1], scale=1.0)
                Ac.drain()
                Ac.activation(ar(s1_it, 128, np_=64), ar(s1_sel, 128, np_=64),
                              AF.Ln)
                Ac.activation(ar(s1_tau, 128, np_=64), psP2[64:128, 0:128],
                              AF.Sigmoid, bias=ar(p_mgb + 1, 1, np_=64),
                              scale=1.0, inc=c_ac)
                psp2free[0] = c_ac.cnt
                selit_ac = c_ac.cnt
                for dq in range(2):
                    pz = pst[4] if dq % 2 == 0 else pst[6]
                    Tn.wait(c_gp, scale_gp[dq])
                    Tn.wait(c_ve, z_evac[dq % 2])
                    for dj in range(4):
                        Tn.transpose(pz[:, 128 * dj:128 * (dj + 1)],
                                     ar(sb_[dq % 2] + 128 * dj, 128),
                                     ar(p_ident, 128),
                                     inc=c_pe if dj == 3 else None)
                    tpe[dq] = c_pe.cnt
                    srw_scales(dq + 2)
                V.wait(c_ac, selit_ac)
                V.drain()
                V.tensor_tensor(ar(s1_it, 128, np_=64), ar(s1_it, 128, np_=64),
                                ar(s1_tau, 128, np_=64), op=OP.mult, inc=c_ve)
                V.wait(c_sp)
                V.drain()
                V.scalar_tensor_tensor(ar(s1_dif, 128, np_=64),
                                       ar(s1_sel, 128, np_=64), -1.0,
                                       ar(s1_tau, 128, np_=64),
                                       op0=OP.mult, op1=OP.mult, inc=c_ve)
                it_ve = c_ve.cnt
                g_ve = c_ve.cnt
                G.wait(c_ve, g_ve)
                G.dma_start(out=g_hbm[:, 128 * rt:128 * (rt + 1)],
                            in_=ar(s1_dif, 128, np_=64), inc=c_sp, k=16)
                gsp = c_sp.cnt

                def z_stt(dq):
                    pz = pst[4] if dq % 2 == 0 else pst[6]
                    V.wait(c_pe, tpe[dq])
                    V.drain()
                    V.scalar_tensor_tensor(
                        ar(X_[sl] + 2048 * gi + 512 * dq, 512), pz, 1.0,
                        ar(X_[sl] + 2048 * gi + 512 * dq, 512),
                        op0=OP.mult, op1=OP.add, inc=c_ve)
                    z_evac[dq % 2] = c_ve.cnt

                def transp(dq):
                    pz = pst[4] if dq % 2 == 0 else pst[6]
                    Tn.wait(c_gp, scale_gp[dq])
                    Tn.wait(c_ve, z_evac[dq % 2])
                    for dj in range(4):
                        Tn.transpose(pz[:, 128 * dj:128 * (dj + 1)],
                                     ar(sb_[dq % 2] + 128 * dj, 128),
                                     ar(p_ident, 128),
                                     inc=c_pe if dj == 3 else None)
                    tpe[dq] = c_pe.cnt

                z_stt(0)
                transp(2)
                z_stt(1)
                transp(3)
                z_stt(2)
                z_stt(3)
                srw_last[0] = tpe[3]
                G.wait(c_ve, z_evac[1])
                G.wait(c_ve, z_evac[0])
                G.wait(c_sp)
                G.dma_start(out=z_hbm[128 * rt:128 * (rt + 1), :],
                            in_=ar(X_[sl] + 2048 * gi, 2048),
                            inc=c_sp, k=16)
                xfree[sl][gi] = c_sp.cnt
                Ac.wait(c_ve, it_ve)
                Ac.drain()
                Ac.activation(ar(s1_it, 128, np_=64), ar(s1_it, 128, np_=64),
                              AF.Exp)
                Ac.wait(c_sp, gsp)
                Ac.drain()
                Ac.activation(ar(s1_dif, 128, np_=64), ar(s1_k0, 128, np_=64),
                              AF.Square, inc=c_ac)
                Tn.wait(c_ac)
                Tn.matmul(psPf[0:1, 256:384], lhsT=ones64[:, 0:1],
                          rhs=ar(s1_dif, 128, np_=64), start=True, stop=True,
                          inc=c_pe)
                V.wait(c_pe)
                V.wait(c_gp)
                V.drain()
                V.tensor_scalar(ar(s1_an, 128, np_=1),
                                psPf[0:1, 256:384], 1e-19, None,
                                op0=OP.max, inc=c_ve)
                Ac.wait(c_ve)
                Ac.drain()
                Ac.activation(ar(s1_an, 128, np_=1),
                              ar(s1_an, 128, np_=1), AF.Ln)
                Ac.drain()
                Ac.activation(ar(s1_an, 128, np_=1),
                              ar(s1_an, 128, np_=1), AF.Exp,
                              scale=-0.5, inc=c_ac)
                Tn.wait(c_ac)
                Tn.matmul(psPf[64:128, 256:384], lhsT=ones64[0:1, 0:64],
                          rhs=ar(s1_an, 128, np_=1),
                          start=True, stop=True, inc=c_pe)
                V.wait(c_pe)
                V.drain()
                V.tensor_tensor(ar(s1_k0, 128, np_=64),
                                ar(s1_k0, 128, np_=64),
                                psPf[64:128, 256:384], op=OP.mult, inc=c_ve)
                V.drain()
                V.tensor_tensor(ar(s1_k0, 128, np_=64),
                                ar(s1_k0, 128, np_=64),
                                ar(s1_it, 128, np_=64), op=OP.mult, inc=c_ve)
                k_ve = c_ve.cnt
                pspfree[0] = k_ve
                G.wait(c_ve, k_ve)
                G.wait(c_sp)
                G.dma_start(out=k_hbm[:, 128 * rt:128 * (rt + 1)],
                            in_=ar(s1_k0, 128, np_=64), inc=c_sp, k=16)

            def ffn_half(sl, grp, es, lf_ve):
                for e in es:
                    for dc in range(16):
                        q = 16 * e + dc
                        d = q // 4
                        slot = q % 8
                        wc, wv = wdma[d]
                        Tn.wait(wc, wv)
                        if dc == 0:
                            Tn.wait(c_ve, lf_ve)
                            if e % 2 == 0:
                                Tn.wait(c_ve, ffn_evac[0])
                            else:
                                Tn.wait(c_ve, max(pst_v_free[0], ffn_evac[1]))
                        for gi in range(len(grp)):
                            Tn.matmul(pst[2 * (e % 2) + gi],
                                      lhsT=ar(LF_[sl] + 2048 * gi + 128 * dc,
                                              128),
                                      rhs=ar(s1_W + 512 * slot, 512),
                                      start=(dc == 0), stop=(dc == 15),
                                      inc=c_pe if gi == len(grp) - 1 else None)
                        if slot == 7 or slot == 3:
                            wfree[(q // 4) % 2] = c_pe.cnt
                            w_issue(q // 4 + 2)
                    for gi in range(len(grp)):
                        V.wait(c_pe)
                        V.drain()
                        V.scalar_tensor_tensor(
                            ar(X_[sl] + 2048 * gi + 512 * e, 512),
                            pst[2 * (e % 2) + gi], 1.0,
                            ar(X_[sl] + 2048 * gi + 512 * e, 512),
                            op0=OP.mult, op1=OP.add, inc=c_ve)
                    ffn_evac[e % 2] = c_ve.cnt

            grps = [list(range(2 * g, min(2 * g + 2, NT + 1)))
                    for g in range((NT + 2) // 2)]
            aq = []
            bq = []
            first = [True]
            for gidx, grp in enumerate(grps):
                sl = gidx % 2
                Sx.wait(c_ve, consts_done)
                Sx.wait(c_sp, xfree[sl][0])
                Sx.dma_start(out=ar(X_[sl], 2048),
                             in_=xin[128 * grp[0]:128 * (grp[0] + 1), :],
                             inc=c_x, k=16)
                if bq:
                    post_b(*bq.pop())
                for gi, tl in enumerate(grp):
                    if gi == 0:
                        continue
                    Sx.wait(c_sp, xfree[sl][gi])
                    Sx.dma_start(out=ar(X_[sl] + 2048 * gi, 2048),
                                 in_=xin[128 * tl:128 * (tl + 1), :],
                                 inc=c_x, k=16)
                x_ld = c_x.cnt
                for gi, tl in enumerate(grp):
                    V.wait(c_x, x_ld)
                    Ac.wait(c_x, x_ld)
                    if first[0]:
                        V.wait(c_gd, ws_gd)
                        first[0] = False
                    V.wait(c_pe, lffree[sl])
                    ln_stats_apply(ar(X_[sl] + 2048 * gi, 2048),
                                   ar(s1_LT, 2048), C1, p_stat + 32 * (gi % 2))
                    Tn.wait(c_ve)
                    transpose_tile(s1_LT, LF_[sl] + 2048 * gi, 128, 0)
                lf_ve = c_ve.cnt
                w_issue(0)
                w_issue(1)
                if aq:
                    post_a(*aq[0])
                ffn_half(sl, grp, [0, 1], lf_ve)
                if aq:
                    post_b(*aq.pop(0))
                if aq:
                    post_a(*aq[0])
                ffn_half(sl, grp, [2, 3], lf_ve)
                lffree[sl] = c_pe.cnt
                bq = aq
                aq = [(sl, gi, tl) for gi, tl in enumerate(grp)]
            for t in bq + aq:
                if t in aq:
                    post_a(*t)
                post_b(*t)
            s1_gd = c_sp.cnt

            V.wait(c_sp, s1_gd)
            V.drain()
            V.memset(ar(s2_S, 2048, np_=64), 0.0, inc=c_ve)
            expst = {"i": 0, "subs": [0, 0]}
            rhat_gp = [0]
            tgp = [{}, {}]
            psc_acs = [0, 0]
            kpv = [0, 0]
            anv = [0, 0]
            rkgv = [0, 0]
            uzv = [(0, 0)] * 4
            stc = [0] * 4
            dspe = [0] * 4
            yev = [0] * 4
            oev2 = [0, 0]
            sup2 = [0, 0]
            ke_st = [0]

            def rkg_loads(b):
                pb = b % 2
                Sx.wait(c_gp)
                if b == 0:
                    Sx.wait(c_sp, s1_gd)
                Sx.wait(c_xm)
                Sx.dma_start(out=ar(s2_r + 512 * pb, 512, np_=64),
                             in_=r_hbm[:, 512 * b:512 * (b + 1)],
                             inc=c_xm, k=16)
                Sx.wait(c_xm)
                Sx.dma_start(out=ar(s2_k + 512 * pb, 512, np_=64),
                             in_=k_hbm[:, 512 * b:512 * (b + 1)],
                             inc=c_xm, k=16)
                Sx.wait(c_xm)
                Sx.dma_start(out=ar(s2_g + 512 * pb, 512, np_=64),
                             in_=g_hbm[:, 512 * b:512 * (b + 1)],
                             inc=c_xm, k=16)
                rkgv[pb] = c_xm.cnt

            def exp_tile(b, dst, anch_off, n_g, neg, mul_src):
                pb = b % 2
                aa = ar(s2_a + 512 * pb, 512, np_=64)
                i = expst["i"]
                expst["i"] += 1
                dbuf = s2_dif + 512 * (i % 2)
                gsz = 512 // n_g
                V.wait(c_ac, expst["subs"][i % 2])
                V.drain()
                V.tensor_tensor(
                    ar(dbuf, 512, np_=64).rearrange(
                        "p (g x) -> p g x", x=gsz),
                    aa.rearrange("p (g x) -> p g x", x=gsz),
                    ar(s2_anch + 384 * pb + anch_off, n_g,
                       np_=64).to_broadcast((64, n_g, gsz)),
                    op=OP.subtract, inc=c_ve)
                Ac.wait(c_ve)
                Ac.drain()
                Ac.activation(ar(dst, 512, np_=64),
                              ar(dbuf, 512, np_=64), AF.Exp,
                              scale=(-1.0 if neg else 1.0), inc=c_ac)
                expst["subs"][i % 2] = c_ac.cnt
                G.wait(c_ac)
                G.drain()
                G.tensor_tensor(ar(dst, 512, np_=64),
                                ar(dst, 512, np_=64), mul_src,
                                op=OP.mult, inc=c_gp)
                return c_gp.cnt

            def prep_piece(b, piece):
                pb = b % 2
                an = s2_anch + 384 * pb
                aa = ar(s2_a + 512 * pb, 512, np_=64)
                rr = ar(s2_r + 512 * pb, 512, np_=64)
                kk_ = ar(s2_k + 512 * pb, 512, np_=64)
                gg = ar(s2_g + 512 * pb, 512, np_=64)
                if piece == 0:
                    V.wait(c_xm, rkgv[pb])
                    V.drain()
                    V.tensor_tensor_scan(aa, ones64[:, 0:512], gg,
                                         initial=ar(p_acar, 1, np_=64),
                                         op0=OP.mult, op1=OP.add, inc=c_ve)
                    V.drain()
                    V.tensor_copy(ar(an, 16, np_=64), aa[:, 15:512:32],
                                  inc=c_ve)
                    V.tensor_copy(ar(an + 16, 16, np_=64), aa[:, 31:512:32],
                                  inc=c_ve)
                    V.tensor_copy(ar(an + 32, 1, np_=64),
                                  ar(p_acar, 1, np_=64), inc=c_ve)
                    V.drain()
                    V.tensor_copy(ar(an + 33, 15, np_=64),
                                  ar(an + 16, 15, np_=64), inc=c_ve)
                    V.tensor_copy(ar(an + 48, 1, np_=64),
                                  ar(p_acar, 1, np_=64), inc=c_ve)
                    V.tensor_copy(ar(an + 49, 3, np_=64), aa[:, 127:384:128],
                                  inc=c_ve)
                    V.tensor_copy(ar(an + 52, 4, np_=64), aa[:, 127:512:128],
                                  inc=c_ve)
                    V.memset(ar(an + 63, 1, np_=64), 0.0, inc=c_ve)
                    V.drain()
                    V.tensor_copy(ar(p_acar, 1, np_=64), aa[:, 511:512],
                                  inc=c_ve)
                    V.drain()
                    for ou in range(4):
                        for pi, (t_, s_) in enumerate(pairs):
                            V.tensor_tensor(
                                ar(an + 64 + 6 * ou + pi, 1, np_=64),
                                ar(an + 32 + 4 * ou + t_, 1, np_=64),
                                ar(an + 4 * ou + s_, 1, np_=64),
                                op=OP.subtract)
                        V.tensor_tensor(ar(an + 88 + ou, 1, np_=64),
                                        ar(an + 52 + ou, 1, np_=64),
                                        ar(an + 48 + ou, 1, np_=64),
                                        op=OP.subtract,
                                        inc=c_ve if ou == 3 else None)
                    Ac.wait(c_ve)
                    Ac.drain()
                    Ac.activation(ar(an + 64, 28, np_=64),
                                  ar(an + 64, 28, np_=64), AF.Exp, inc=c_ac)
                    anv[pb] = c_ve.cnt
                    psc_acs[pb] = c_ac.cnt
                elif piece == 1:
                    tgp[pb][1] = exp_tile(b, s2_T1 + 512 * pb, 0, 16,
                                          False, rr)
                    tgp[pb][2] = exp_tile(b, s2_T2 + 512 * pb, 0, 16,
                                          True, kk_)
                elif piece == 2:
                    tgp[pb][3] = exp_tile(b, s2_T3 + 512 * pb, 32, 16,
                                          False, rr)
                    tgp[pb][4] = exp_tile(b, s2_T4 + 512 * pb, 48, 4,
                                          False, rr)
                else:
                    tgp[pb][5] = exp_tile(b, s2_T5 + 512 * pb, 52, 4,
                                          True, kk_)
                    rhat_gp[0] = exp_tile(b, p_rhat + 512 * b, 63, 1,
                                          False, rr)

            def uz_load0():
                Sx.wait(c_sp, s1_gd)
                for tt in range(4):
                    Sx.wait(c_xm)
                    Sx.dma_start(out=ar(s2_U + 2048 * tt, 2048),
                                 in_=u_hbm[128 * tt:128 * (tt + 1), :],
                                 inc=c_xm, k=16)
                    Sx.wait(c_y)
                    Sx.dma_start(out=ar(s2_Z + 2048 * tt, 2048),
                                 in_=z_hbm[128 * tt:128 * (tt + 1), :],
                                 inc=c_y, k=16)
                    uzv[tt] = (c_xm.cnt, c_y.cnt)

            def uz_load(b1, ou):
                Sx.wait(c_st, stc[ou])
                Sx.wait(c_pe, dspe[ou])
                Sx.wait(c_xm)
                Sx.dma_start(out=ar(s2_U + 2048 * ou, 2048),
                             in_=u_hbm[512 * b1 + 128 * ou:
                                       512 * b1 + 128 * (ou + 1), :],
                             inc=c_xm, k=16)
                Sx.wait(c_y)
                Sx.dma_start(out=ar(s2_Z + 2048 * ou, 2048),
                             in_=z_hbm[512 * b1 + 128 * ou:
                                       512 * b1 + 128 * (ou + 1), :],
                             inc=c_y, k=16)
                uzv[ou] = (c_xm.cnt, c_y.cnt)

            pairs = [(t_, s_) for t_ in range(1, 4) for s_ in range(t_)]

            def kp_scale(b, ou):
                pb = b % 2
                an = s2_anch + 384 * pb
                V.wait(c_ac, psc_acs[pb])
                V.wait(c_gp, tgp[pb][2])
                V.wait(c_pe)
                V.drain()
                for pi, (t_, s_) in enumerate(pairs):
                    V.tensor_scalar(ar(s2_kp + 192 * (ou % 2) + 32 * pi, 32,
                                       np_=64),
                                    ar(s2_T2 + 512 * pb + 128 * ou + 32 * s_,
                                       32, np_=64),
                                    ar(an + 64 + 6 * ou + pi, 1, np_=64),
                                    None, op0=OP.mult,
                                    inc=c_ve if pi == 5 else None)
                kpv[ou % 2] = c_ve.cnt

            def comp_ou(b, ou):
                pb = b % 2
                an = s2_anch + 384 * pb
                ob = 128 * ou
                if ou == 0:
                    kp_scale(b, 0)
                kp_ve = kpv[ou % 2]
                Tn.wait(c_ve, kp_ve)
                Tn.wait(c_gp, tgp[pb][3])
                for s_ in range(4):
                    sc = 512 * pb + ob + 32 * s_
                    Tn.matmul(psP[32 * s_:32 * s_ + 32,
                                  32 * s_:32 * s_ + 32],
                              lhsT=ar(s2_T2 + sc, 32, np_=64),
                              rhs=ar(s2_T1 + sc, 32, np_=64),
                              start=True, stop=(s_ == 3),
                              tile_position=(0, 32 * s_), inc=c_pe)
                    for t_ in range(s_ + 1, 4):
                        pi = pairs.index((t_, s_))
                        Tn.matmul(psP[32 * s_:32 * s_ + 32,
                                      32 * t_:32 * t_ + 32],
                                  lhsT=ar(s2_kp + 192 * (ou % 2) + 32 * pi,
                                          32, np_=64),
                                  rhs=ar(s2_T3 + 512 * pb + ob + 32 * t_, 32,
                                         np_=64),
                                  start=False, stop=(t_ == 3),
                                  tile_position=(0, 32 * s_), inc=c_pe)
                if ou < 3:
                    kp_scale(b, ou + 1)
                V.wait(c_pe)
                V.drain()
                V.memset(ar(s2_P, 128), 0.0, inc=c_ve)
                V.drain()
                for s_ in range(4):
                    V.tensor_tensor(
                        arena[32 * s_:32 * s_ + 32,
                              s2_P + 32 * s_:s2_P + 32 * s_ + 32],
                        psP[32 * s_:32 * s_ + 32, 32 * s_:32 * s_ + 32],
                        arena[32 * s_:32 * s_ + 32, p_mask:p_mask + 32],
                        op=OP.mult, inc=c_ve)
                    for t_ in range(s_ + 1, 4):
                        V.tensor_copy(
                            arena[32 * s_:32 * s_ + 32,
                                  s2_P + 32 * t_:s2_P + 32 * t_ + 32],
                            psP[32 * s_:32 * s_ + 32,
                                32 * t_:32 * t_ + 32], inc=c_ve)
                p_ve = c_ve.cnt
                Tn.wait(c_ve, p_ve)
                Tn.wait(c_ve, ke_st[0])
                Tn.wait(c_gp, tgp[pb][5])
                Tn.transpose(pst[6][:, 0:64],
                             ar(s2_T5 + 512 * pb + ob, 128, np_=64),
                             ar(p_ident, 128)[0:64, 0:64], inc=c_pe)
                V.wait(c_pe)
                V.drain()
                V.tensor_copy(ar(s2_kE, 64), pst[6][:, 0:64], inc=c_ve)
                ke_st[0] = c_ve.cnt
                Tn.wait(c_xm, uzv[ou][0])
                Tn.wait(c_gp, tgp[pb][4])
                V.wait(c_y, uzv[ou][1])
                for dt in range(4):
                    po = pst[dt % 2]
                    Tn.wait(c_ve, oev2[dt % 2])
                    Tn.matmul(po, lhsT=ar(s2_T4 + 512 * pb + ob, 128,
                                          np_=64),
                              rhs=ar(s2_S + 512 * dt, 512, np_=64),
                              start=True, stop=False)
                    Tn.matmul(po, lhsT=ar(s2_P, 128),
                              rhs=ar(s2_U + 2048 * ou + 512 * dt, 512),
                              start=False, stop=True, inc=c_pe)
                    V.wait(c_pe)
                    V.drain()
                    V.scalar_tensor_tensor(
                        ar(s2_Z + 2048 * ou + 512 * dt, 512), po, SCALE,
                        ar(s2_Z + 2048 * ou + 512 * dt, 512),
                        op0=OP.mult, op1=OP.add, inc=c_ve)
                    oev2[dt % 2] = c_ve.cnt
                yev[ou] = c_ve.cnt
                G.wait(c_ve, sup2[0])
                G.wait(c_ve, sup2[1])
                G.wait(c_ac, psc_acs[pb])
                G.drain()
                G.tensor_scalar(ar(s2_Sb, 2048, np_=64),
                                ar(s2_S, 2048, np_=64),
                                ar(an + 88 + ou, 1, np_=64), zero_t[0:64, :],
                                op0=OP.mult, op1=OP.add, inc=c_gp)
                sb_gp = c_gp.cnt
                Tn.wait(c_ve, ke_st[0])
                for dt in range(4):
                    pd = pst[2 + dt % 2]
                    Tn.wait(c_ve, sup2[dt % 2])
                    Tn.matmul(pd[0:64, :], lhsT=ar(s2_kE, 64),
                              rhs=ar(s2_U + 2048 * ou + 512 * dt, 512),
                              start=True, stop=True, inc=c_pe)
                    V.wait(c_pe)
                    V.wait(c_gp, sb_gp)
                    V.drain()
                    V.tensor_tensor(ar(s2_S + 512 * dt, 512, np_=64),
                                    ar(s2_Sb + 512 * dt, 512, np_=64),
                                    pd[0:64, :], op=OP.add, inc=c_ve)
                    sup2[dt % 2] = c_ve.cnt
                dspe[ou] = c_pe.cnt

            def y_spill(b, ou):
                Ac.wait(c_ve, yev[ou])
                Ac.wait(c_st)
                Ac.dma_start(out=ypart_hbm[512 * b + 128 * ou:
                                           512 * b + 128 * (ou + 1), :],
                             in_=ar(s2_Z + 2048 * ou, 2048),
                             inc=c_st, k=16)
                stc[ou] = c_st.cnt

            rkg_loads(0)
            uz_load0()
            for piece in range(4):
                prep_piece(0, piece)
            for b in range(NB):
                if b + 1 < NB:
                    rkg_loads(b + 1)
                for ou in range(4):
                    comp_ou(b, ou)
                    y_spill(b, ou)
                    if b + 1 < NB:
                        uz_load(b + 1, ou)
                        prep_piece(b + 1, ou)


            if _os.environ.get("KERNEL_NOS3") == "1":
                Sx.wait(c_st)
                Sx.wait(c_sp)
                return
            hb_row4 = ar(s4_LT, 2048, np_=1)
            Sx.wait(c_misc)
            Sx.wait(c_gp)
            Sx.wait(c_ve)
            Sx.wait(c_ac)
            Sx.wait(c_pe)
            Sx.dma_start(out=hb_row4,
                         in_=fbias.rearrange("(one x) -> one x", one=1),
                         inc=c_misc, k=16)
            Ac.wait(c_misc)
            Ac.wait(c_ve)
            Ac.activation(hb_row4, hb_row4, AF.Copy, scale=0.5, inc=c_ac)
            Tn.wait(c_ac)
            bcast_row(hb_row4, s4_hbb)
            hbb_ve = c_ve.cnt
            G.wait(c_ve)
            G.dma_start(out=sc_in[:, 0:2048], in_=ar(s2_S, 2048, np_=64),
                        inc=c_gd, k=16)
            G.wait(c_gd)
            G.dma_start(out=sc_in[:, 2048:2049], in_=ar(p_acar, 1, np_=64),
                        allow_slow_non_contiguous=True, inc=c_gd, k=16)
            G.wait(c_gd)
            G.collective_compute("AllGather", OP.bypass,
                                 replica_groups=[list(range(NCORE))],
                                 ins=[sc_in.opt()], outs=[sc_out.opt()],
                                 inc=c_cc, k=1)
            Sx.wait(c_cc)
            Sx.dma_start(out=ar(s4_A, 64, np_=8),
                         in_=sc_out[:, 2048:2049].rearrange(
                             "(i m) one -> i (m one)", m=M),
                         allow_slow_non_contiguous=True, inc=c_sc, k=16)
            Tn.wait(c_sc)
            Tn.wait(c_ve)
            Tn.matmul(pst[0][0:8, 0:64], lhsT=ar(p_small, 8, np_=8),
                      rhs=ar(s4_A, 64, np_=8), start=True, stop=True, inc=c_pe)
            Ac.wait(c_pe)
            Ac.activation(ar(s4_w8, 64, np_=8), pst[0][0:8, 0:64], AF.Exp,
                          inc=c_ac)
            V.wait(c_ac)
            V.tensor_scalar(ar(s4_w8, 64, np_=8), ar(s4_w8, 64, np_=8),
                            ar(p_small + 8, 1, np_=8), None, op0=OP.mult,
                            inc=c_ve)
            Tn.wait(c_ve)
            Tn.transpose(pst[1][0:64, 0:8], ar(s4_w8, 64, np_=8),
                         ar(p_ident, 128)[0:8, 0:8], inc=c_pe)
            V.wait(c_pe)
            V.tensor_copy(ar(s4_wT, 8, np_=64), pst[1][0:64, 0:8], inc=c_ve)
            V.memset(ar(s4_Sst, 2048, np_=64), 0.0, inc=c_ve)
            Sx.wait(c_ve)
            Sx.wait(c_cc)
            Sx.wait(c_st)
            Sx.wait(c_pe)
            Sx.dma_start(out=ar(s4_Y, 8 * 2048, np_=64).rearrange(
                             "m (i x) -> m i x", x=2048),
                         in_=sc_out[:, 0:2048].rearrange(
                             "(i m) x -> m i x", m=M),
                         inc=c_sc, k=16)
            l_sc = c_sc.cnt
            sst = [s4_Sst, s4_Sst2]
            V.wait(c_sc, l_sc)
            for i in range(8):
                V.drain()
                V.scalar_tensor_tensor(ar(sst[(i + 1) % 2], 2048, np_=64),
                                       ar(s4_Y + 2048 * i, 2048, np_=64),
                                       ar(s4_wT + i, 1, np_=64),
                                       ar(sst[i % 2], 2048, np_=64),
                                       op0=OP.mult, op1=OP.add, inc=c_ve)
            sstart_col = sst[8 % 2]
            sst_ve = c_ve.cnt

            if _os.environ.get("KERNEL_NOS4") == "1":
                Sx.wait(c_st)
                Sx.wait(c_gd)
                return
            wfree2 = [0, 0]
            wdma2 = {}
            ffn_evac2 = [0, 0]
            oev = [0, 0]
            yfree = [0, 0]
            lffree2 = [0, 0]
            Y_2 = [s4_Y, s4_Y + 4096]
            LF_2 = [s4_LF, s4_LF + 4096]
            def w_issue2(d):
                if d >= 16:
                    return
                Sx.wait(c_pe, wfree2[d % 2])
                cwx = c_wa if d % 2 == 0 else c_wb
                e = d // 4
                dc0 = (4 * d) % 16
                for i2 in range(2):
                    Sx.dma_start(
                        out=ar(s4_W + 2048 * (d % 2), 2048).rearrange(
                            "p (c xi) -> p c xi", xi=512
                            )[:, :, 256 * i2:256 * i2 + 256],
                        in_=wq[e][:, dc0:dc0 + 4, i2],
                        inc=cwx, k=16)
                wdma2[d] = (cwx, cwx.cnt)

            def ffn2_half(sl, grp, es, lf_ve2):
                for e in es:
                    for dc in range(16):
                        q = 16 * e + dc
                        d = q // 4
                        slot = q % 8
                        wc, wv = wdma2[d]
                        Tn.wait(wc, wv)
                        if dc == 0:
                            Tn.wait(c_ve, lf_ve2)
                            Tn.wait(c_ve, ffn_evac2[e % 2])
                        for gi in range(len(grp)):
                            Tn.matmul(pst[2 * (e % 2) + gi],
                                      lhsT=ar(LF_2[sl] + 2048 * gi + 128 * dc,
                                              128),
                                      rhs=ar(s4_W + 512 * slot, 512),
                                      start=(dc == 0), stop=(dc == 15),
                                      inc=c_pe if gi == len(grp) - 1 else None)
                        if slot == 7 or slot == 3:
                            wfree2[(q // 4) % 2] = c_pe.cnt
                            w_issue2(q // 4 + 2)
                    for gi in range(len(grp)):
                        V.wait(c_pe)
                        V.drain()
                        V.scalar_tensor_tensor(
                            ar(Y_2[sl] + 2048 * gi + 512 * e, 512),
                            pst[2 * (e % 2) + gi], 1.0,
                            ar(Y_2[sl] + 2048 * gi + 512 * e, 512),
                            op0=OP.mult, op1=OP.add, inc=c_ve)
                    ffn_evac2[e % 2] = c_ve.cnt

            grps2 = [[2 * g, 2 * g + 1] for g in range(NT // 2)]
            Tn.wait(c_gp, rhat_gp[0])
            Sx.wait(c_st)
            Sx.wait(c_ve, sst_ve)
            for gidx, grp in enumerate(grps2):
                sl = gidx % 2
                Sx.wait(c_gd, yfree[sl])
                for gi, tl in enumerate(grp):
                    Sx.dma_start(out=ar(Y_2[sl] + 2048 * gi, 2048),
                                 in_=ypart_hbm[128 * tl:128 * (tl + 1), :],
                                 inc=c_y, k=16)
                y_ld = c_y.cnt
                oevt = [0, 0]
                for gi, tl in enumerate(grp):
                    Tn.wait(c_y, y_ld)
                    Tn.wait(c_ve)
                    for dt in range(4):
                        po = pst[4] if dt % 2 == 0 else pst[6]
                        Tn.wait(c_ve, oev[dt % 2])
                        Tn.matmul(po, lhsT=ar(p_rhat + 128 * tl, 128, np_=64),
                                  rhs=ar(sstart_col + 512 * dt, 512, np_=64),
                                  start=True, stop=True, inc=c_pe)
                        V.wait(c_pe)
                        V.wait(c_ve, sst_ve)
                        V.wait(c_y, y_ld)
                        V.drain()
                        V.scalar_tensor_tensor(
                            ar(Y_2[sl] + 2048 * gi + 512 * dt, 512), po, SCALE,
                            ar(Y_2[sl] + 2048 * gi + 512 * dt, 512),
                            op0=OP.mult, op1=OP.add, inc=c_ve)
                        oev[dt % 2] = c_ve.cnt
                    oevt[gi] = c_ve.cnt
                for gi, tl in enumerate(grp):
                    Ac.wait(c_ve, oevt[gi])
                    V.wait(c_pe, lffree2[sl])
                    st = p_stat + 32 * (gi % 2)
                    xap = ar(Y_2[sl] + 2048 * gi, 2048)
                    V.wait(c_ve, oevt[gi])
                    V.drain()
                    for j in range(4):
                        V.bn_stats(ar(st + 6 * j, 6),
                                   xap[:, 512 * j:512 * (j + 1)])
                    V.drain()
                    V.bn_aggr(ar(st + 24, 2), ar(st, 24), inc=c_ve)
                    Ac.wait(c_ve)
                    Ac.drain()
                    Ac.activation(ar(st + 26, 1), ar(st + 25, 1), AF.Ln,
                                  bias=eps_t, scale=1.0)
                    Ac.drain()
                    Ac.activation(ar(st + 26, 1), ar(st + 26, 1), AF.Exp,
                                  scale=-0.5, inc=c_ac)
                    V.wait(c_ac)
                    V.drain()
                    V.tensor_scalar(ar(st + 26, 1), ar(st + 26, 1), C1, None,
                                    op0=OP.mult, inc=c_ve)
                    V.drain()
                    V.tensor_scalar(ar(st + 27, 1), ar(st + 24, 1),
                                    ar(st + 26, 1), -1.0,
                                    op0=OP.mult, op1=OP.mult, inc=c_ve)
                    V.drain()
                    V.tensor_scalar(ar(s4_LT, 2048), xap,
                                    ar(st + 26, 1), ar(st + 27, 1),
                                    op0=OP.mult, op1=OP.add, inc=c_ve)
                    ln_ve = c_ve.cnt
                    Tn.wait(c_ve, ln_ve)
                    transpose_tile(s4_LT, LF_2[sl] + 2048 * gi, 128, 0)
                lf_ve2 = c_ve.cnt
                w_issue2(0)
                w_issue2(1)
                ffn2_half(sl, grp, [0, 1], lf_ve2)
                ffn2_half(sl, grp, [2, 3], lf_ve2)
                lffree2[sl] = c_pe.cnt
                for gi, tl in enumerate(grp):
                    G.wait(c_ve)
                    G.wait(c_ve, hbb_ve)
                    yt = ar(Y_2[sl] + 2048 * gi, 2048)
                    G.tensor_tensor(yt, yt, ar(s4_hbb, 2048), op=OP.add,
                                    inc=c_gp)
                    G.drain()
                    G.wait(c_gd)
                    G.dma_start(out=yout[128 * tl:128 * (tl + 1), :],
                                in_=yt, inc=c_gd, k=16)
                yfree[sl] = c_gd.cnt
            out_gd = c_gd.cnt
            Sx.wait(c_gd, out_gd)

        @block.sync
        def _(eng):
            for c in ctrs:
                c.cnt = 0
            Sx.real = eng
            schedule()
            Sx.real = None

        @block.vector
        def _(eng):
            for c in ctrs:
                c.cnt = 0
            V.real = eng
            schedule()
            V.real = None

        @block.tensor
        def _(eng):
            for c in ctrs:
                c.cnt = 0
            Tn.real = eng
            schedule()
            Tn.real = None

        @block.scalar
        def _(eng):
            for c in ctrs:
                c.cnt = 0
            Ac.real = eng
            schedule()
            Ac.real = None

        @block.gpsimd
        def _(eng):
            for c in ctrs:
                c.cnt = 0
            G.real = eng
            schedule()
            G.real = None

    return nc


_NC_CACHE = [None]


def kernel(L0_attn_hidden_states, L0_attn_conv_w, L0_attn_conv_b,
           L0_attn_in_proj_w, L0_attn_ch_gate_w0, L0_attn_ch_gate_w1,
           L0_attn_ch_gate_b1, L0_attn_mem_gate_w, L0_attn_mem_gate_b,
           L0_attn_residual_weight, L0_ffn_factors, L0_ffn_bias):
    f32 = np.float32
    x = np.ascontiguousarray(np.asarray(L0_attn_hidden_states, f32)[0])
    factors = np.ascontiguousarray(np.asarray(L0_ffn_factors, f32))
    ipT = np.ascontiguousarray(np.asarray(L0_attn_in_proj_w, f32).T)
    mgT = np.ascontiguousarray(np.asarray(L0_attn_mem_gate_w, f32).T)
    ch0T = np.ascontiguousarray(np.asarray(L0_attn_ch_gate_w0, f32).T)
    ch1rT = np.ascontiguousarray(np.asarray(L0_attn_ch_gate_w1, f32).T)
    convw = np.ascontiguousarray(np.asarray(L0_attn_conv_w, f32))
    convb = np.ascontiguousarray(np.asarray(L0_attn_conv_b, f32))
    mgb = np.ascontiguousarray(np.asarray(L0_attn_mem_gate_b, f32).reshape(2, 64).T)
    resw = np.ascontiguousarray(np.asarray(L0_attn_residual_weight, f32))
    ch1b = np.ascontiguousarray(np.asarray(L0_attn_ch_gate_b1, f32))
    fbias = np.ascontiguousarray(np.asarray(L0_ffn_bias, f32))

    ident = np.eye(128, dtype=f32)
    mask32 = np.tile(np.triu(np.ones((32, 32), f32)), (4, 1))
    ones_t = np.ones((128, 512), f32)

    if _NC_CACHE[0] is None:
        _NC_CACHE[0] = build_nc()
    nc = _NC_CACHE[0]

    in_maps = []
    for c in range(NCORE):
        s0 = c * SEG
        xp = np.zeros(((NT + 1) * 128, D), f32)
        if c > 0:
            xp[0:3] = x[s0 - 3:s0]
        xp[128:128 + SEG] = x[s0:s0 + SEG]
        fsl = np.ascontiguousarray(factors[:, :, ESL * c:ESL * (c + 1)])
        mj = np.zeros((8, 8), f32)
        sel = np.zeros((8, 1), f32)
        for i in range(c):
            sel[i, 0] = 1.0
            for l in range(i + 1, c):
                mj[l, i] = 1.0
        in_maps.append(dict(
            xin=xp, fsl=fsl, ipT=ipT, mgT=mgT, ch0T=ch0T, ch1rT=ch1rT,
            convw=convw, convb=convb, mgb=mgb, resw=resw, ch1b=ch1b,
            fbias=fbias, identi=ident, mask32i=mask32, onesi=ones_t,
            mji=mj, seli=sel))

    res = run_bass_kernel_spmd(nc, in_maps, core_ids=list(range(NCORE)))
    out = np.concatenate([r["yout"] for r in res.results], axis=0)
    return out[None, :, :]


# version 27
# speedup vs baseline: 1.0538x; 1.0059x over previous
import contextlib
import os as _os
import numpy as np
import concourse.bass as bass
import concourse.mybir as mybir
from concourse.bass_utils import run_bass_kernel_spmd
from concourse.alu_op_type import AluOpType

F32 = mybir.dt.float32
AF = mybir.ActivationFunctionType
OP = AluOpType
AX = mybir.AxisListType

NCORE = 8
T_TOT, D, M = 16384, 2048, 64
if _os.environ.get("KERNEL_SMALL") == "1":
    T_TOT = 8192
SEG = T_TOT // NCORE
NT = SEG // 128
NB = SEG // 512
ESL = D // NCORE
SCALE = float(1.0 / np.sqrt(M))
C1 = float(0.5 / np.sqrt(8.0))
EPS = 1e-5
HFS = 260


class Ctr:
    def __init__(self, handle):
        self.h = handle
        self.cnt = 0


class _FakeIns:
    def then_inc(self, h, k):
        return self


class _Prox:

    def __init__(self, name):
        self.name = name
        self.real = None

    def wait(self, ctr, val=None):
        v = ctr.cnt if val is None else val
        if self.real is not None and v > 0:
            self.real.wait_ge(ctr.h, v)

    def __getattr__(self, attr):
        def call(*a, _attr=attr, inc=None, k=1, **kw):
            if self.real is not None:
                ins = getattr(self.real, _attr)(*a, **kw)
            else:
                ins = _FakeIns()
            if inc is not None:
                ins.then_inc(inc.h, k)
                inc.cnt += k
            return ins
        return call


def build_nc():
    nc = bass.Bass("TRN2", target_bir_lowering=False)

    xin = nc.dram_tensor("xin", [(NT + 1) * 128, D], F32, kind="ExternalInput").ap()
    fsl = nc.dram_tensor("fsl", [8, D, ESL], F32, kind="ExternalInput").ap()
    ipT = nc.dram_tensor("ipT", [D, 128], F32, kind="ExternalInput").ap()
    mgT = nc.dram_tensor("mgT", [D, 128], F32, kind="ExternalInput").ap()
    ch0T = nc.dram_tensor("ch0T", [D, 128], F32, kind="ExternalInput").ap()
    ch1rT = nc.dram_tensor("ch1rT", [128, D], F32, kind="ExternalInput").ap()
    convw = nc.dram_tensor("convw", [D, 4], F32, kind="ExternalInput").ap()
    convb = nc.dram_tensor("convb", [D], F32, kind="ExternalInput").ap()
    mgb = nc.dram_tensor("mgb", [64, 2], F32, kind="ExternalInput").ap()
    resw = nc.dram_tensor("resw", [D], F32, kind="ExternalInput").ap()
    ch1b = nc.dram_tensor("ch1b", [D], F32, kind="ExternalInput").ap()
    fbias = nc.dram_tensor("fbias", [D], F32, kind="ExternalInput").ap()
    ident_i = nc.dram_tensor("identi", [128, 128], F32, kind="ExternalInput").ap()
    mask32_i = nc.dram_tensor("mask32i", [128, 32], F32, kind="ExternalInput").ap()
    ones_i = nc.dram_tensor("onesi", [128, 512], F32, kind="ExternalInput").ap()
    mj_i = nc.dram_tensor("mji", [8, 8], F32, kind="ExternalInput").ap()
    sel_i = nc.dram_tensor("seli", [8, 1], F32, kind="ExternalInput").ap()

    yout = nc.dram_tensor("yout", [SEG, D], F32, kind="ExternalOutput").ap()

    cc_in = nc.dram_tensor("cc_in", [D, ESL], F32).ap()
    cc_out = nc.dram_tensor("cc_out", [NCORE * D, ESL], F32).ap()
    w_hbm = nc.dram_tensor("w_hbm", [64 * 128, 512], F32).ap()
    xmid_hbm = nc.dram_tensor("xmid_hbm", [(NT + 1) * 128, D], F32).ap()
    ypart_hbm = nc.dram_tensor("ypart_hbm", [SEG, D], F32).ap()
    u_hbm = nc.dram_tensor("u_hbm", [SEG, D], F32).ap()
    z_hbm = nc.dram_tensor("z_hbm", [SEG, D], F32).ap()
    r_hbm = nc.dram_tensor("r_hbm", [M, SEG], F32).ap()
    k_hbm = nc.dram_tensor("k_hbm", [M, SEG], F32).ap()
    g_hbm = nc.dram_tensor("g_hbm", [M, SEG], F32).ap()
    sc_in = nc.dram_tensor("sc_in", [M, 2049], F32).ap()
    sc_out = nc.dram_tensor("sc_out", [NCORE * M, 2049], F32).ap()

    es = contextlib.ExitStack()
    with es:
        AR = 47616
        arena = es.enter_context(nc.sbuf_tensor("arena", [128, AR], F32)).ap()

        pos = [0]

        def A_(n):
            p = pos[0]
            pos[0] += n
            assert pos[0] <= AR, f"arena overflow {pos[0]}"
            return p

        p_ipT = A_(2048); p_mgT = A_(2048); p_ch0T = A_(2048); p_ch1rT = A_(2048)
        p_convw = A_(64); p_convb = A_(16); p_resw = A_(16); p_mgb = A_(2)
        p_ident = A_(128); p_mask = A_(32); p_ones = A_(512)
        p_acar = A_(4); p_zero = A_(1); p_eps = A_(1)
        p_stat = A_(64); p_small = A_(64)

        stage0 = pos[0]

        def stage_alloc(sizes):
            out, q = [], stage0
            for n in sizes:
                out.append(q)
                q += n
            assert q <= AR, f"stage arena overflow {q}"
            return out

        (s1_Xa, s1_Xb, s1_LFa, s1_LFb, s1_LT, s1_W, s1_hbb,
         s1_HF, s1_SH, s1_U,
         s1_k0, s1_sel, s1_tau, s1_it, s1_dif, s1_an, s1_ch, s1_srw,
         s1_gacc, s1_ch1bb) = stage_alloc(
            [2 * 2048, 2 * 2048, 2 * 2048, 2 * 2048, 2048, 8 * 512, 2048,
             16 * 132, 3 * 2048, 2048,
             128, 128, 128, 128, 128, 128, 128, 4 * 128,
             128, 2048])
        s1_WS = s1_LFa
        (s2_U, s2_Z, s2_a, s2_r, s2_k, s2_g,
         s2_T1, s2_T2, s2_T3, s2_T4, s2_T5, s2_dif,
         s2_P, s2_kE, s2_kp, s2_anch,
         s2_S, s2_Sb) = stage_alloc(
            [4 * 2048, 4 * 2048, 2 * 512, 2 * 512, 2 * 512, 2 * 512,
             2 * 512, 2 * 512, 2 * 512, 2 * 512, 2 * 512, 2 * 512,
             128, 64, 2 * 192, 2 * 384,
             2048, 2048])
        (s4_Y, s4_LF, s4_LT, s4_W, s4_L, s4_A, s4_w8, s4_wT,
         s4_Sst, s4_Sst2, _s4pad, s4_hbb, p_rhat) = stage_alloc(
            [4 * 2048, 4 * 2048, 2048, 8 * 512, 2048, 64, 64, 8,
             2048, 2048, 3256, 2048, 2048])
        assert s4_hbb >= s2_Sb + 2048

        globals()["LAYOUT"] = {k: v for k, v in locals().items()
                               if isinstance(v, int)}
        pst = [es.enter_context(nc.psum_tensor(f"ps{i}", [128, 512], F32)).ap()
               for i in range(7)]
        psPf = es.enter_context(nc.psum_tensor("psP", [128, 512], F32)).ap()
        psP = psPf[:, 0:128]
        psP2 = psPf[:, 128:256]

        c_ld = Ctr(es.enter_context(nc.semaphore("c_ld")))
        c_pre = Ctr(es.enter_context(nc.semaphore("c_pre")))
        c_x = Ctr(es.enter_context(nc.semaphore("c_x")))
        c_wa = Ctr(es.enter_context(nc.semaphore("c_wa")))
        c_wb = Ctr(es.enter_context(nc.semaphore("c_wb")))
        c_xm = Ctr(es.enter_context(nc.semaphore("c_xm")))
        c_misc = Ctr(es.enter_context(nc.semaphore("c_misc")))
        c_sc = Ctr(es.enter_context(nc.semaphore("c_sc")))
        c_y = Ctr(es.enter_context(nc.semaphore("c_y")))
        c_sp = Ctr(es.enter_context(nc.semaphore("c_sp")))
        c_gp = Ctr(es.enter_context(nc.semaphore("c_gp")))
        c_gd = Ctr(es.enter_context(nc.semaphore("c_gd")))
        c_cc = Ctr(es.enter_context(nc.semaphore("c_cc")))
        c_pe = Ctr(es.enter_context(nc.semaphore("c_pe")))
        c_ve = Ctr(es.enter_context(nc.semaphore("c_ve")))
        c_ac = Ctr(es.enter_context(nc.semaphore("c_ac")))
        c_st = Ctr(es.enter_context(nc.semaphore("c_st")))
        ctrs = [c_ld, c_pre, c_x, c_wa, c_wb, c_xm, c_misc, c_sc, c_y, c_sp, c_gp, c_gd, c_cc, c_pe, c_ve, c_ac, c_st]
        block = es.enter_context(nc.Block())

        def ar(col, ncol, p0=0, np_=128):
            return arena[p0:p0 + np_, col:col + ncol]

        Sx = _Prox("sync"); V = _Prox("vector"); Tn = _Prox("tensor")
        Ac = _Prox("scalar"); G = _Prox("gpsimd")

        def schedule():
            Sx.dma_start(out=ar(p_ipT, 2048).rearrange("p (dc r) -> p dc r", r=128),
                         in_=ipT.rearrange("(dc p) r -> p dc r", p=128),
                         inc=c_ld, k=16)
            Sx.dma_start(out=ar(p_mgT, 2048).rearrange("p (dc r) -> p dc r", r=128),
                         in_=mgT.rearrange("(dc p) r -> p dc r", p=128),
                         inc=c_ld, k=16)
            Sx.dma_start(out=ar(p_ch0T, 2048).rearrange("p (dc r) -> p dc r", r=128),
                         in_=ch0T.rearrange("(dc p) r -> p dc r", p=128),
                         inc=c_ld, k=16)
            Sx.dma_start(out=ar(p_ch1rT, 2048), in_=ch1rT, inc=c_ld, k=16)
            Sx.dma_start(out=ar(p_convw, 64).rearrange("p (dc w) -> p dc w", w=4),
                         in_=convw.rearrange("(dc p) w -> p dc w", p=128),
                         inc=c_ld, k=16)
            Sx.dma_start(out=ar(p_convb, 16),
                         in_=convb.rearrange("(dc p) -> p dc", p=128),
                         allow_slow_non_contiguous=True, inc=c_ld, k=16)
            Sx.dma_start(out=ar(p_resw, 16),
                         in_=resw.rearrange("(dc p) -> p dc", p=128),
                         allow_slow_non_contiguous=True, inc=c_ld, k=16)
            Sx.dma_start(out=ar(p_mgb, 2, np_=64), in_=mgb, inc=c_ld, k=16)
            Sx.dma_start(out=ar(p_ident, 128), in_=ident_i, inc=c_ld, k=16)
            Sx.dma_start(out=ar(p_mask, 32), in_=mask32_i, inc=c_ld, k=16)
            Sx.dma_start(out=ar(p_ones, 512), in_=ones_i, inc=c_ld, k=16)
            Sx.dma_start(out=ar(p_small, 8, np_=8), in_=mj_i, inc=c_ld, k=16)
            Sx.dma_start(out=ar(p_small + 8, 1, np_=8), in_=sel_i, inc=c_ld, k=16)
            hb_row = ar(s1_LT, 2048, np_=1)
            cb_row = hb_row
            Sx.dma_start(out=hb_row, in_=fbias.rearrange("(one x) -> one x", one=1),
                         inc=c_ld, k=16)
            ld_all = c_ld.cnt

            Ac.wait(c_ld, ld_all)
            Ac.activation(hb_row, hb_row, AF.Copy, scale=0.5, inc=c_ac)
            Tn.wait(c_ac)
            Tn.wait(c_ld, ld_all)

            def bcast_row(src_row, dst):
                for j in range(4):
                    Tn.wait(c_ve)
                    Tn.matmul(pst[0], lhsT=ar(p_ones, 128, np_=1),
                              rhs=src_row[:, 512 * j:512 * (j + 1)],
                              start=True, stop=True, inc=c_pe)
                    V.wait(c_pe)
                    V.tensor_copy(ar(dst + 512 * j, 512), pst[0], inc=c_ve)

            bcast_row(hb_row, s1_hbb)
            V.memset(ar(p_zero, 1), 0.0, inc=c_ve)
            V.memset(ar(p_eps, 1), EPS, inc=c_ve)
            V.memset(ar(p_acar, 1, np_=64), 0.0, inc=c_ve)
            consts_done = c_ve.cnt
            eps_t = ar(p_eps, 1)
            zero_t = ar(p_zero, 1)

            for rr in range(8):
                G.wait(c_gd)
                kw = {} if rr == 0 else dict(accum_op=OP.add)
                G.dma_start(out=ar(s1_WS, 4096).rearrange(
                                "p (dc x) -> p dc x", x=ESL),
                            in_=fsl[rr].rearrange("(dc p) x -> p dc x", p=128),
                            inc=c_gd, k=16, **kw)
            G.wait(c_gd)
            G.dma_start(out=cc_in.rearrange("(dc p) x -> p dc x", p=128),
                        in_=ar(s1_WS, 4096).rearrange(
                            "p (dc x) -> p dc x", x=ESL),
                        inc=c_gd, k=16)
            G.wait(c_gd)
            ws_gd = c_gd.cnt
            G.collective_compute("AllGather", OP.bypass,
                                 replica_groups=[list(range(NCORE))],
                                 ins=[cc_in.opt()], outs=[cc_out.opt()],
                                 inc=c_cc, k=1)
            ag_cc = c_cc.cnt



            def ln_stats_apply(xap, out_ap, cfold, st):
                V.drain()
                for j in range(4):
                    V.bn_stats(ar(st + 6 * j, 6), xap[:, 512 * j:512 * (j + 1)])
                V.drain()
                V.bn_aggr(ar(st + 24, 2), ar(st, 24), inc=c_ve)
                Ac.wait(c_ve)
                Ac.drain()
                Ac.activation(ar(st + 26, 1), ar(st + 25, 1), AF.Ln,
                              bias=eps_t, scale=1.0)
                Ac.drain()
                Ac.activation(ar(st + 26, 1), ar(st + 26, 1), AF.Exp,
                              scale=-0.5, inc=c_ac)
                V.wait(c_ac)
                V.drain()
                V.tensor_scalar(ar(st + 26, 1), ar(st + 26, 1), cfold, None,
                                op0=OP.mult, inc=c_ve)
                V.drain()
                V.tensor_scalar(ar(st + 27, 1), ar(st + 24, 1),
                                ar(st + 26, 1), -1.0,
                                op0=OP.mult, op1=OP.mult, inc=c_ve)
                V.drain()
                V.tensor_scalar(out_ap, xap, ar(st + 26, 1), ar(st + 27, 1),
                                op0=OP.mult, op1=OP.add, inc=c_ve)

            def transpose_tile(src_col, dst_col, dst_stride, dst_off, nsub=16):
                Tn.wait(c_ac)
                evac_done = [0, 0]
                for dq in range(nsub // 4):
                    pt = pst[4] if dq % 2 == 0 else pst[6]
                    Tn.wait(c_ve, evac_done[dq % 2])
                    for dj in range(4):
                        dc = 4 * dq + dj
                        Tn.transpose(pt[:, 128 * dj:128 * (dj + 1)],
                                     ar(src_col + 128 * dc, 128),
                                     ar(p_ident, 128),
                                     inc=c_pe if dj == 3 else None)
                    V.wait(c_pe)
                    base = dst_col + dst_stride * 4 * dq + dst_off
                    if dst_stride == 128:
                        V.tensor_copy(ar(base, 512), pt, inc=c_ve)
                    else:
                        lo = dst_col + dst_stride * 4 * dq
                        V.tensor_copy(
                            arena[:, lo:lo + 4 * dst_stride].rearrange(
                                "p (g x) -> p g x", x=dst_stride)[
                                :, :, dst_off:dst_off + 128],
                            pt.rearrange("p (g x) -> p g x", x=128),
                            inc=c_ve)
                    evac_done[dq % 2] = c_ve.cnt

            ones64 = ar(p_ones, 512, np_=64)
            Sx.wait(c_pe)
            Sx.wait(c_misc)
            Sx.dma_start(out=cb_row,
                         in_=ch1b.rearrange("(one x) -> one x", one=1),
                         inc=c_misc, k=16)
            Tn.wait(c_misc)
            bcast_row(cb_row, s1_ch1bb)

            srw_last = [0]
            z_evac = [0, 0]
            pst_v_free = [0]
            pspfree = [0]
            psp2free = [0]
            X_ = [s1_Xa, s1_Xb]
            LF_ = [s1_LFa, s1_LFb]
            xfree = [[0, 0], [0, 0]]
            lffree = [0, 0]
            wfree = [0, 0]
            ffn_evac = [0, 0]
            wdma = {}
            tst = {}
            wq = cc_out.rearrange("(e2 i2 dc p) xi -> e2 p dc i2 xi",
                                  e2=4, i2=2, p=128)

            def w_issue(d):
                if d >= 16:
                    return
                Sx.wait(c_cc, ag_cc)
                Sx.wait(c_pe, wfree[d % 2])
                cwx = c_wa if d % 2 == 0 else c_wb
                e = d // 4
                dc0 = (4 * d) % 16
                for i2 in range(2):
                    Sx.dma_start(
                        out=ar(s1_W + 2048 * (d % 2), 2048).rearrange(
                            "p (c xi) -> p c xi", xi=512
                            )[:, :, 256 * i2:256 * i2 + 256],
                        in_=wq[e][:, dc0:dc0 + 4, i2],
                        inc=cwx, k=16)
                wdma[d] = (cwx, cwx.cnt)

            def post_a(sl, gi, tl):
                rt = tl - 1
                xm = ar(X_[sl] + 2048 * gi, 2048)
                G.wait(c_ve)
                G.tensor_tensor(xm, xm, ar(s1_hbb, 2048), op=OP.add, inc=c_gp)
                V.wait(c_gp)
                Ac.wait(c_gp)
                Ac.wait(c_pe)
                ln_stats_apply(xm, ar(s1_LT, 2048), 1.0, p_stat)
                V.wait(c_ac)
                V.wait(c_gp)
                V.drain()
                if tl == 1:
                    V.tensor_copy(
                        arena[:, s1_HF:s1_HF + 16 * 132].rearrange(
                            "p (g x) -> p g x", x=132)[:, :, 1:4],
                        arena[:, s1_HF:s1_HF + 16 * 132].rearrange(
                            "p (g x) -> p g x", x=132)[:, :, 4:7], inc=c_ve)
                elif tl > 1:
                    V.tensor_copy(
                        arena[:, s1_HF:s1_HF + 16 * 132].rearrange(
                            "p (g x) -> p g x", x=132)[:, :, 1:4],
                        arena[:, s1_HF:s1_HF + 16 * 132].rearrange(
                            "p (g x) -> p g x", x=132)[:, :, 129:132],
                        inc=c_ve)
                carry_ve = c_ve.cnt
                Tn.wait(c_ve, carry_ve)
                transpose_tile(s1_LT, s1_HF, 132, 4)
                if tl == 0:
                    return
                hf_ve = c_ve.cnt
                Tn.wait(c_ve, hf_ve)
                for dc in range(16):
                    Tn.matmul(pst[5][:, 0:128],
                              lhsT=ar(p_ch0T + 128 * dc, 128),
                              rhs=ar(s1_HF + 132 * dc + 4, 128),
                              start=(dc == 0), stop=(dc == 15),
                              inc=c_pe if dc == 15 else None)
                V.wait(c_pe)
                V.drain()
                V.tensor_copy(ar(s1_ch, 128), pst[5][:, 0:128], inc=c_ve)
                ch_ve = c_ve.cnt
                Tn.wait(c_ve, ch_ve)
                pr_ve = ffn_evac[1]
                for pr in range(2):
                    Tn.wait(c_ve, pr_ve)
                    for j in range(2):
                        e = 2 * pr + j
                        Tn.matmul(pst[2 + j], lhsT=ar(s1_ch, 128),
                                  rhs=ar(p_ch1rT + 512 * e, 512),
                                  start=True, stop=True,
                                  inc=c_pe if j == 1 else None)
                    V.wait(c_pe)
                    if pr == 0:
                        V.wait(c_sp)
                    V.drain()
                    for j in range(2):
                        e = 2 * pr + j
                        V.scalar_tensor_tensor(
                            ar(s1_U + 512 * e, 512), pst[2 + j], 1.0,
                            ar(s1_ch1bb + 512 * e, 512),
                            op0=OP.mult, op1=OP.add,
                            inc=c_ve if j == 1 else None)
                    pr_ve = c_ve.cnt
                pst_v_free[0] = c_ve.cnt
                Ac.wait(c_ve)
                Ac.drain()
                for e in range(4):
                    Ac.activation(ar(s1_U + 512 * e, 512),
                                  ar(s1_U + 512 * e, 512), AF.Sigmoid,
                                  inc=c_ac if e == 3 else None)
                V.wait(c_ac)
                V.drain()
                for e in range(4):
                    V.tensor_tensor(ar(s1_U + 512 * e, 512),
                                    ar(s1_U + 512 * e, 512),
                                    ar(s1_LT + 512 * e, 512),
                                    op=OP.mult, inc=c_ve if e == 3 else None)
                u_ve = c_ve.cnt
                G.wait(c_ve, u_ve)
                G.wait(c_sp)
                G.dma_start(out=u_hbm[128 * rt:128 * (rt + 1), :],
                            in_=ar(s1_U, 2048), inc=c_sp, k=16)
                u_sp = c_sp.cnt
                sh = s1_SH + 2048 * (tl % 3)
                V.wait(c_ac)
                silu_done = [0, 0]
                for dc in range(8):
                    hb_ = s1_HF + 132 * dc
                    pc = pst[4] if dc % 2 == 0 else pst[6]
                    V.wait(c_ac, silu_done[dc % 2])
                    V.wait(c_pe)
                    V.drain()
                    V.tensor_scalar(pc[:, 0:128], ar(hb_ + 1, 128),
                                    ar(p_convw + 4 * dc, 1), None,
                                    op0=OP.mult, inc=c_ve)
                    for i in range(1, 4):
                        V.drain()
                        V.scalar_tensor_tensor(
                            pc[:, 0:128], ar(hb_ + 1 + i, 128),
                            ar(p_convw + 4 * dc + i, 1), pc[:, 0:128],
                            op0=OP.mult, op1=OP.add, inc=c_ve)
                    Ac.wait(c_ve)
                    Ac.activation(ar(sh + 128 * dc, 128), pc[:, 0:128],
                                  AF.Silu, bias=ar(p_convb + dc, 1), scale=1.0,
                                  inc=c_ac)
                    silu_done[dc % 2] = c_ac.cnt
                G.wait(c_pe)
                for dc in range(8, 16):
                    hb_ = s1_HF + 132 * dc
                    G.wait(c_ac)
                    G.drain()
                    G.tensor_scalar(ar(s1_gacc, 128), ar(hb_ + 1, 128),
                                    ar(p_convw + 4 * dc, 1), zero_t,
                                    op0=OP.mult, op1=OP.add, inc=c_gp)
                    for i in range(1, 4):
                        G.drain()
                        G.tensor_scalar(ar(s1_an, 128), ar(hb_ + 1 + i, 128),
                                        ar(p_convw + 4 * dc + i, 1), zero_t,
                                        op0=OP.mult, op1=OP.add, inc=c_gp)
                        G.drain()
                        G.tensor_tensor(ar(s1_gacc, 128), ar(s1_gacc, 128),
                                        ar(s1_an, 128), op=OP.add, inc=c_gp)
                    Ac.wait(c_gp)
                    Ac.drain()
                    Ac.activation(ar(sh + 128 * dc, 128), ar(s1_gacc, 128),
                                  AF.Silu, bias=ar(p_convb + dc, 1), scale=1.0,
                                  inc=c_ac)
                tst[(sl, gi)] = (c_ac.cnt, u_sp)

            def post_b(sl, gi, tl):
                if tl == 0:
                    return
                rt = tl - 1
                silu_ac, u_sp = tst[(sl, gi)]
                sh = s1_SH + 2048 * (tl % 3)
                Tn.wait(c_ac, max(silu_ac, psp2free[0]))
                Tn.wait(c_ve, pspfree[0])

                def gate_mm(ps, wcol, mov_base):
                    for dc in range(16):
                        Tn.matmul(ps, lhsT=ar(wcol + 128 * dc, 128),
                                  rhs=ar(mov_base + 128 * dc, 128),
                                  start=(dc == 0), stop=(dc == 15),
                                  inc=c_pe if dc == 15 else None)

                gate_mm(psP[0:128, 0:128], p_ipT, sh)
                gate_mm(psP2[0:128, 0:128], p_mgT, sh)
                gates_pe = c_pe.cnt
                V.wait(c_pe, gates_pe)
                V.wait(c_sp)
                V.drain()
                V.tensor_copy(ar(s1_dif, 128, np_=64), psP[0:64, 0:128],
                              inc=c_ve)
                V.tensor_copy(ar(s1_k0, 128, np_=64), psP[64:128, 0:128],
                              inc=c_ve)
                r_ve = c_ve.cnt
                G.wait(c_ve, r_ve)
                G.dma_start(out=r_hbm[:, 128 * rt:128 * (rt + 1)],
                            in_=ar(s1_dif, 128, np_=64), inc=c_sp, k=16)
                sb_ = [s1_srw, s1_U]
                scale_gp = [0, 0, 0, 0]
                tpe = [0, 0, 0, 0]

                def srw_scales(dq):
                    G.wait(c_ac, silu_ac)
                    if dq < 2:
                        G.wait(c_pe, srw_last[0])
                    else:
                        G.wait(c_pe, tpe[dq - 2])
                    if dq % 2 == 1:
                        G.wait(c_sp, u_sp)
                    G.drain()
                    for dj in range(4):
                        dc = 4 * dq + dj
                        G.tensor_scalar(ar(sb_[dq % 2] + 128 * dj, 128),
                                        ar(sh + 128 * dc, 128),
                                        ar(p_resw + dc, 1), zero_t,
                                        op0=OP.mult, op1=OP.add, inc=c_gp)
                    scale_gp[dq] = c_gp.cnt

                srw_scales(0)
                srw_scales(1)
                Ac.wait(c_pe, gates_pe)
                Ac.wait(c_ve, r_ve)
                Ac.drain()
                Ac.activation(ar(s1_sel, 128, np_=64), psP2[0:64, 0:128],
                              AF.Exp, bias=ar(p_mgb, 1, np_=64), scale=1.0)
                Ac.drain()
                Ac.activation(ar(s1_sel, 128, np_=64), ar(s1_sel, 128, np_=64),
                              AF.Ln, bias=ones64[:, 0:1], scale=1.0)
                Ac.drain()
                Ac.activation(ar(s1_it, 128, np_=64), ar(s1_sel, 128, np_=64),
                              AF.Ln)
                Ac.activation(ar(s1_tau, 128, np_=64), psP2[64:128, 0:128],
                              AF.Sigmoid, bias=ar(p_mgb + 1, 1, np_=64),
                              scale=1.0, inc=c_ac)
                psp2free[0] = c_ac.cnt
                selit_ac = c_ac.cnt
                for dq in range(2):
                    pz = pst[4] if dq % 2 == 0 else pst[6]
                    Tn.wait(c_gp, scale_gp[dq])
                    Tn.wait(c_ve, z_evac[dq % 2])
                    for dj in range(4):
                        Tn.transpose(pz[:, 128 * dj:128 * (dj + 1)],
                                     ar(sb_[dq % 2] + 128 * dj, 128),
                                     ar(p_ident, 128),
                                     inc=c_pe if dj == 3 else None)
                    tpe[dq] = c_pe.cnt
                    srw_scales(dq + 2)
                V.wait(c_ac, selit_ac)
                V.drain()
                V.tensor_tensor(ar(s1_it, 128, np_=64), ar(s1_it, 128, np_=64),
                                ar(s1_tau, 128, np_=64), op=OP.mult, inc=c_ve)
                V.wait(c_sp)
                V.drain()
                V.scalar_tensor_tensor(ar(s1_dif, 128, np_=64),
                                       ar(s1_sel, 128, np_=64), -1.0,
                                       ar(s1_tau, 128, np_=64),
                                       op0=OP.mult, op1=OP.mult, inc=c_ve)
                it_ve = c_ve.cnt
                g_ve = c_ve.cnt
                G.wait(c_ve, g_ve)
                G.dma_start(out=g_hbm[:, 128 * rt:128 * (rt + 1)],
                            in_=ar(s1_dif, 128, np_=64), inc=c_sp, k=16)
                gsp = c_sp.cnt

                def z_stt(dq):
                    pz = pst[4] if dq % 2 == 0 else pst[6]
                    V.wait(c_pe, tpe[dq])
                    V.drain()
                    V.scalar_tensor_tensor(
                        ar(X_[sl] + 2048 * gi + 512 * dq, 512), pz, 1.0,
                        ar(X_[sl] + 2048 * gi + 512 * dq, 512),
                        op0=OP.mult, op1=OP.add, inc=c_ve)
                    z_evac[dq % 2] = c_ve.cnt

                def transp(dq):
                    pz = pst[4] if dq % 2 == 0 else pst[6]
                    Tn.wait(c_gp, scale_gp[dq])
                    Tn.wait(c_ve, z_evac[dq % 2])
                    for dj in range(4):
                        Tn.transpose(pz[:, 128 * dj:128 * (dj + 1)],
                                     ar(sb_[dq % 2] + 128 * dj, 128),
                                     ar(p_ident, 128),
                                     inc=c_pe if dj == 3 else None)
                    tpe[dq] = c_pe.cnt

                z_stt(0)
                transp(2)
                z_stt(1)
                transp(3)
                z_stt(2)
                z_stt(3)
                srw_last[0] = tpe[3]
                G.wait(c_ve, z_evac[1])
                G.wait(c_ve, z_evac[0])
                G.wait(c_sp)
                G.dma_start(out=z_hbm[128 * rt:128 * (rt + 1), :],
                            in_=ar(X_[sl] + 2048 * gi, 2048),
                            inc=c_sp, k=16)
                xfree[sl][gi] = c_sp.cnt
                Ac.wait(c_ve, it_ve)
                Ac.drain()
                Ac.activation(ar(s1_it, 128, np_=64), ar(s1_it, 128, np_=64),
                              AF.Exp)
                Ac.wait(c_sp, gsp)
                Ac.drain()
                Ac.activation(ar(s1_dif, 128, np_=64), ar(s1_k0, 128, np_=64),
                              AF.Square, inc=c_ac)
                Tn.wait(c_ac)
                Tn.matmul(psPf[0:1, 256:384], lhsT=ones64[:, 0:1],
                          rhs=ar(s1_dif, 128, np_=64), start=True, stop=True,
                          inc=c_pe)
                V.wait(c_pe)
                V.wait(c_gp)
                V.drain()
                V.tensor_scalar(ar(s1_an, 128, np_=1),
                                psPf[0:1, 256:384], 1e-19, None,
                                op0=OP.max, inc=c_ve)
                Ac.wait(c_ve)
                Ac.drain()
                Ac.activation(ar(s1_an, 128, np_=1),
                              ar(s1_an, 128, np_=1), AF.Ln)
                Ac.drain()
                Ac.activation(ar(s1_an, 128, np_=1),
                              ar(s1_an, 128, np_=1), AF.Exp,
                              scale=-0.5, inc=c_ac)
                Tn.wait(c_ac)
                Tn.matmul(psPf[64:128, 256:384], lhsT=ones64[0:1, 0:64],
                          rhs=ar(s1_an, 128, np_=1),
                          start=True, stop=True, inc=c_pe)
                V.wait(c_pe)
                V.drain()
                V.tensor_tensor(ar(s1_k0, 128, np_=64),
                                ar(s1_k0, 128, np_=64),
                                psPf[64:128, 256:384], op=OP.mult, inc=c_ve)
                V.drain()
                V.tensor_tensor(ar(s1_k0, 128, np_=64),
                                ar(s1_k0, 128, np_=64),
                                ar(s1_it, 128, np_=64), op=OP.mult, inc=c_ve)
                k_ve = c_ve.cnt
                pspfree[0] = k_ve
                G.wait(c_ve, k_ve)
                G.wait(c_sp)
                G.dma_start(out=k_hbm[:, 128 * rt:128 * (rt + 1)],
                            in_=ar(s1_k0, 128, np_=64), inc=c_sp, k=16)

            def ffn_half(sl, grp, es, lf_ve):
                for e in es:
                    for dc in range(16):
                        q = 16 * e + dc
                        d = q // 4
                        slot = q % 8
                        wc, wv = wdma[d]
                        Tn.wait(wc, wv)
                        if dc == 0:
                            Tn.wait(c_ve, lf_ve)
                            if e % 2 == 0:
                                Tn.wait(c_ve, ffn_evac[0])
                            else:
                                Tn.wait(c_ve, max(pst_v_free[0], ffn_evac[1]))
                        for gi in range(len(grp)):
                            Tn.matmul(pst[2 * (e % 2) + gi],
                                      lhsT=ar(LF_[sl] + 2048 * gi + 128 * dc,
                                              128),
                                      rhs=ar(s1_W + 512 * slot, 512),
                                      start=(dc == 0), stop=(dc == 15),
                                      inc=c_pe if gi == len(grp) - 1 else None)
                        if slot == 7 or slot == 3:
                            wfree[(q // 4) % 2] = c_pe.cnt
                            w_issue(q // 4 + 2)
                    for gi in range(len(grp)):
                        V.wait(c_pe)
                        V.drain()
                        V.scalar_tensor_tensor(
                            ar(X_[sl] + 2048 * gi + 512 * e, 512),
                            pst[2 * (e % 2) + gi], 1.0,
                            ar(X_[sl] + 2048 * gi + 512 * e, 512),
                            op0=OP.mult, op1=OP.add, inc=c_ve)
                    ffn_evac[e % 2] = c_ve.cnt

            grps = [list(range(2 * g, min(2 * g + 2, NT + 1)))
                    for g in range((NT + 2) // 2)]
            aq = []
            bq = []
            first = [True]
            for gidx, grp in enumerate(grps):
                sl = gidx % 2
                Sx.wait(c_ve, consts_done)
                Sx.wait(c_sp, xfree[sl][0])
                Sx.dma_start(out=ar(X_[sl], 2048),
                             in_=xin[128 * grp[0]:128 * (grp[0] + 1), :],
                             inc=c_x, k=16)
                if bq:
                    post_b(*bq.pop())
                for gi, tl in enumerate(grp):
                    if gi == 0:
                        continue
                    Sx.wait(c_sp, xfree[sl][gi])
                    Sx.dma_start(out=ar(X_[sl] + 2048 * gi, 2048),
                                 in_=xin[128 * tl:128 * (tl + 1), :],
                                 inc=c_x, k=16)
                x_ld = c_x.cnt
                for gi, tl in enumerate(grp):
                    V.wait(c_x, x_ld)
                    Ac.wait(c_x, x_ld)
                    if first[0]:
                        V.wait(c_gd, ws_gd)
                        first[0] = False
                    V.wait(c_pe, lffree[sl])
                    ln_stats_apply(ar(X_[sl] + 2048 * gi, 2048),
                                   ar(s1_LT, 2048), C1, p_stat + 32 * (gi % 2))
                    Tn.wait(c_ve)
                    transpose_tile(s1_LT, LF_[sl] + 2048 * gi, 128, 0)
                lf_ve = c_ve.cnt
                w_issue(0)
                w_issue(1)
                if aq:
                    post_a(*aq[0])
                ffn_half(sl, grp, [0, 1], lf_ve)
                if aq:
                    post_b(*aq.pop(0))
                if aq:
                    post_a(*aq[0])
                ffn_half(sl, grp, [2, 3], lf_ve)
                lffree[sl] = c_pe.cnt
                bq = aq
                aq = [(sl, gi, tl) for gi, tl in enumerate(grp)]
            for t in bq + aq:
                if t in aq:
                    post_a(*t)
                post_b(*t)
            s1_gd = c_sp.cnt

            V.wait(c_sp, s1_gd)
            V.drain()
            V.memset(ar(s2_S, 2048, np_=64), 0.0, inc=c_ve)
            expst = {"i": 0, "subs": [0, 0]}
            rhat_gp = [0]
            tgp = [{}, {}]
            psc_acs = [0, 0]
            kpv = [0, 0]
            anv = [0, 0]
            rkgv = [0, 0]
            uzv = [(0, 0)] * 4
            stc = [0] * 4
            dspe = [0] * 4
            yev = [0] * 4
            oev2 = [0, 0]
            sup2 = [0, 0]
            ke_st = [0]

            def rkg_loads(b):
                pb = b % 2
                Sx.wait(c_gp)
                if b == 0:
                    Sx.wait(c_sp, s1_gd)
                Sx.wait(c_xm)
                Sx.dma_start(out=ar(s2_r + 512 * pb, 512, np_=64),
                             in_=r_hbm[:, 512 * b:512 * (b + 1)],
                             inc=c_xm, k=16)
                Sx.wait(c_xm)
                Sx.dma_start(out=ar(s2_k + 512 * pb, 512, np_=64),
                             in_=k_hbm[:, 512 * b:512 * (b + 1)],
                             inc=c_xm, k=16)
                Sx.wait(c_xm)
                Sx.dma_start(out=ar(s2_g + 512 * pb, 512, np_=64),
                             in_=g_hbm[:, 512 * b:512 * (b + 1)],
                             inc=c_xm, k=16)
                rkgv[pb] = c_xm.cnt

            def exp_tile(b, dst, anch_off, n_g, neg, mul_src):
                pb = b % 2
                aa = ar(s2_a + 512 * pb, 512, np_=64)
                i = expst["i"]
                expst["i"] += 1
                dbuf = s2_dif + 512 * (i % 2)
                gsz = 512 // n_g
                V.wait(c_ac, expst["subs"][i % 2])
                V.drain()
                V.tensor_tensor(
                    ar(dbuf, 512, np_=64).rearrange(
                        "p (g x) -> p g x", x=gsz),
                    aa.rearrange("p (g x) -> p g x", x=gsz),
                    ar(s2_anch + 384 * pb + anch_off, n_g,
                       np_=64).to_broadcast((64, n_g, gsz)),
                    op=OP.subtract, inc=c_ve)
                Ac.wait(c_ve)
                Ac.drain()
                Ac.activation(ar(dst, 512, np_=64),
                              ar(dbuf, 512, np_=64), AF.Exp,
                              scale=(-1.0 if neg else 1.0), inc=c_ac)
                expst["subs"][i % 2] = c_ac.cnt
                G.wait(c_ac)
                G.drain()
                G.tensor_tensor(ar(dst, 512, np_=64),
                                ar(dst, 512, np_=64), mul_src,
                                op=OP.mult, inc=c_gp)
                return c_gp.cnt

            def prep_piece(b, piece):
                pb = b % 2
                an = s2_anch + 384 * pb
                aa = ar(s2_a + 512 * pb, 512, np_=64)
                rr = ar(s2_r + 512 * pb, 512, np_=64)
                kk_ = ar(s2_k + 512 * pb, 512, np_=64)
                gg = ar(s2_g + 512 * pb, 512, np_=64)
                if piece == 0:
                    V.wait(c_xm, rkgv[pb])
                    V.drain()
                    V.tensor_tensor_scan(aa, ones64[:, 0:512], gg,
                                         initial=ar(p_acar, 1, np_=64),
                                         op0=OP.mult, op1=OP.add, inc=c_ve)
                    V.drain()
                    V.tensor_copy(ar(an, 16, np_=64), aa[:, 15:512:32],
                                  inc=c_ve)
                    V.tensor_copy(ar(an + 16, 16, np_=64), aa[:, 31:512:32],
                                  inc=c_ve)
                    V.tensor_copy(ar(an + 32, 1, np_=64),
                                  ar(p_acar, 1, np_=64), inc=c_ve)
                    V.drain()
                    V.tensor_copy(ar(an + 33, 15, np_=64),
                                  ar(an + 16, 15, np_=64), inc=c_ve)
                    V.tensor_copy(ar(an + 48, 1, np_=64),
                                  ar(p_acar, 1, np_=64), inc=c_ve)
                    V.tensor_copy(ar(an + 49, 3, np_=64), aa[:, 127:384:128],
                                  inc=c_ve)
                    V.tensor_copy(ar(an + 52, 4, np_=64), aa[:, 127:512:128],
                                  inc=c_ve)
                    V.memset(ar(an + 63, 1, np_=64), 0.0, inc=c_ve)
                    V.drain()
                    V.tensor_copy(ar(p_acar, 1, np_=64), aa[:, 511:512],
                                  inc=c_ve)
                    V.drain()
                    for ou in range(4):
                        for pi, (t_, s_) in enumerate(pairs):
                            V.tensor_tensor(
                                ar(an + 64 + 6 * ou + pi, 1, np_=64),
                                ar(an + 32 + 4 * ou + t_, 1, np_=64),
                                ar(an + 4 * ou + s_, 1, np_=64),
                                op=OP.subtract)
                        V.tensor_tensor(ar(an + 88 + ou, 1, np_=64),
                                        ar(an + 52 + ou, 1, np_=64),
                                        ar(an + 48 + ou, 1, np_=64),
                                        op=OP.subtract,
                                        inc=c_ve if ou == 3 else None)
                    Ac.wait(c_ve)
                    Ac.drain()
                    Ac.activation(ar(an + 64, 28, np_=64),
                                  ar(an + 64, 28, np_=64), AF.Exp, inc=c_ac)
                    anv[pb] = c_ve.cnt
                    psc_acs[pb] = c_ac.cnt
                elif piece == 1:
                    tgp[pb][1] = exp_tile(b, s2_T1 + 512 * pb, 0, 16,
                                          False, rr)
                    tgp[pb][2] = exp_tile(b, s2_T2 + 512 * pb, 0, 16,
                                          True, kk_)
                elif piece == 2:
                    tgp[pb][3] = exp_tile(b, s2_T3 + 512 * pb, 32, 16,
                                          False, rr)
                    tgp[pb][4] = exp_tile(b, s2_T4 + 512 * pb, 48, 4,
                                          False, rr)
                else:
                    tgp[pb][5] = exp_tile(b, s2_T5 + 512 * pb, 52, 4,
                                          True, kk_)
                    rhat_gp[0] = exp_tile(b, p_rhat + 512 * b, 63, 1,
                                          False, rr)

            def uz_load0():
                Sx.wait(c_sp, s1_gd)
                for tt in range(4):
                    Sx.wait(c_xm)
                    Sx.dma_start(out=ar(s2_U + 2048 * tt, 2048),
                                 in_=u_hbm[128 * tt:128 * (tt + 1), :],
                                 inc=c_xm, k=16)
                    Sx.wait(c_y)
                    Sx.dma_start(out=ar(s2_Z + 2048 * tt, 2048),
                                 in_=z_hbm[128 * tt:128 * (tt + 1), :],
                                 inc=c_y, k=16)
                    uzv[tt] = (c_xm.cnt, c_y.cnt)

            def uz_load(b1, ou):
                Sx.wait(c_st, stc[ou])
                Sx.wait(c_pe, dspe[ou])
                Sx.wait(c_xm)
                Sx.dma_start(out=ar(s2_U + 2048 * ou, 2048),
                             in_=u_hbm[512 * b1 + 128 * ou:
                                       512 * b1 + 128 * (ou + 1), :],
                             inc=c_xm, k=16)
                Sx.wait(c_y)
                Sx.dma_start(out=ar(s2_Z + 2048 * ou, 2048),
                             in_=z_hbm[512 * b1 + 128 * ou:
                                       512 * b1 + 128 * (ou + 1), :],
                             inc=c_y, k=16)
                uzv[ou] = (c_xm.cnt, c_y.cnt)

            pairs = [(t_, s_) for t_ in range(1, 4) for s_ in range(t_)]

            def kp_scale(b, ou):
                pb = b % 2
                an = s2_anch + 384 * pb
                V.wait(c_ac, psc_acs[pb])
                V.wait(c_gp, tgp[pb][2])
                V.wait(c_pe)
                V.drain()
                for pi, (t_, s_) in enumerate(pairs):
                    V.tensor_scalar(ar(s2_kp + 192 * (ou % 2) + 32 * pi, 32,
                                       np_=64),
                                    ar(s2_T2 + 512 * pb + 128 * ou + 32 * s_,
                                       32, np_=64),
                                    ar(an + 64 + 6 * ou + pi, 1, np_=64),
                                    None, op0=OP.mult,
                                    inc=c_ve if pi == 5 else None)
                kpv[ou % 2] = c_ve.cnt

            def comp_ou(b, ou):
                pb = b % 2
                an = s2_anch + 384 * pb
                ob = 128 * ou
                if ou == 0:
                    kp_scale(b, 0)
                kp_ve = kpv[ou % 2]
                Tn.wait(c_ve, kp_ve)
                Tn.wait(c_gp, tgp[pb][3])
                for s_ in range(4):
                    sc = 512 * pb + ob + 32 * s_
                    Tn.matmul(psP[32 * s_:32 * s_ + 32,
                                  32 * s_:32 * s_ + 32],
                              lhsT=ar(s2_T2 + sc, 32, np_=64),
                              rhs=ar(s2_T1 + sc, 32, np_=64),
                              start=True, stop=(s_ == 3),
                              tile_position=(0, 32 * s_), inc=c_pe)
                    for t_ in range(s_ + 1, 4):
                        pi = pairs.index((t_, s_))
                        Tn.matmul(psP[32 * s_:32 * s_ + 32,
                                      32 * t_:32 * t_ + 32],
                                  lhsT=ar(s2_kp + 192 * (ou % 2) + 32 * pi,
                                          32, np_=64),
                                  rhs=ar(s2_T3 + 512 * pb + ob + 32 * t_, 32,
                                         np_=64),
                                  start=False, stop=(t_ == 3),
                                  tile_position=(0, 32 * s_), inc=c_pe)
                if ou < 3:
                    kp_scale(b, ou + 1)
                V.wait(c_pe)
                V.drain()
                V.memset(ar(s2_P, 128), 0.0, inc=c_ve)
                V.drain()
                for s_ in range(4):
                    V.tensor_tensor(
                        arena[32 * s_:32 * s_ + 32,
                              s2_P + 32 * s_:s2_P + 32 * s_ + 32],
                        psP[32 * s_:32 * s_ + 32, 32 * s_:32 * s_ + 32],
                        arena[32 * s_:32 * s_ + 32, p_mask:p_mask + 32],
                        op=OP.mult, inc=c_ve)
                    for t_ in range(s_ + 1, 4):
                        V.tensor_copy(
                            arena[32 * s_:32 * s_ + 32,
                                  s2_P + 32 * t_:s2_P + 32 * t_ + 32],
                            psP[32 * s_:32 * s_ + 32,
                                32 * t_:32 * t_ + 32], inc=c_ve)
                p_ve = c_ve.cnt
                Tn.wait(c_ve, p_ve)
                Tn.wait(c_ve, ke_st[0])
                Tn.wait(c_gp, tgp[pb][5])
                Tn.transpose(pst[6][:, 0:64],
                             ar(s2_T5 + 512 * pb + ob, 128, np_=64),
                             ar(p_ident, 128)[0:64, 0:64], inc=c_pe)
                V.wait(c_pe)
                V.drain()
                V.tensor_copy(ar(s2_kE, 64), pst[6][:, 0:64], inc=c_ve)
                ke_st[0] = c_ve.cnt
                Tn.wait(c_xm, uzv[ou][0])
                Tn.wait(c_gp, tgp[pb][4])
                V.wait(c_y, uzv[ou][1])
                for dt in range(4):
                    po = pst[dt % 2]
                    Tn.wait(c_ve, oev2[dt % 2])
                    Tn.matmul(po, lhsT=ar(s2_T4 + 512 * pb + ob, 128,
                                          np_=64),
                              rhs=ar(s2_S + 512 * dt, 512, np_=64),
                              start=True, stop=False)
                    Tn.matmul(po, lhsT=ar(s2_P, 128),
                              rhs=ar(s2_U + 2048 * ou + 512 * dt, 512),
                              start=False, stop=True, inc=c_pe)
                    V.wait(c_pe)
                    V.drain()
                    V.scalar_tensor_tensor(
                        ar(s2_Z + 2048 * ou + 512 * dt, 512), po, SCALE,
                        ar(s2_Z + 2048 * ou + 512 * dt, 512),
                        op0=OP.mult, op1=OP.add, inc=c_ve)
                    oev2[dt % 2] = c_ve.cnt
                yev[ou] = c_ve.cnt
                G.wait(c_ve, sup2[0])
                G.wait(c_ve, sup2[1])
                G.wait(c_ac, psc_acs[pb])
                G.drain()
                G.tensor_scalar(ar(s2_Sb, 2048, np_=64),
                                ar(s2_S, 2048, np_=64),
                                ar(an + 88 + ou, 1, np_=64), zero_t[0:64, :],
                                op0=OP.mult, op1=OP.add, inc=c_gp)
                sb_gp = c_gp.cnt
                Tn.wait(c_ve, ke_st[0])
                for dt in range(4):
                    pd = pst[2 + dt % 2]
                    Tn.wait(c_ve, sup2[dt % 2])
                    Tn.matmul(pd[0:64, :], lhsT=ar(s2_kE, 64),
                              rhs=ar(s2_U + 2048 * ou + 512 * dt, 512),
                              start=True, stop=True, inc=c_pe)
                    V.wait(c_pe)
                    V.wait(c_gp, sb_gp)
                    V.drain()
                    V.tensor_tensor(ar(s2_S + 512 * dt, 512, np_=64),
                                    ar(s2_Sb + 512 * dt, 512, np_=64),
                                    pd[0:64, :], op=OP.add, inc=c_ve)
                    sup2[dt % 2] = c_ve.cnt
                dspe[ou] = c_pe.cnt

            def y_spill(b, ou):
                Ac.wait(c_ve, yev[ou])
                Ac.wait(c_st)
                Ac.dma_start(out=ypart_hbm[512 * b + 128 * ou:
                                           512 * b + 128 * (ou + 1), :],
                             in_=ar(s2_Z + 2048 * ou, 2048),
                             inc=c_st, k=16)
                stc[ou] = c_st.cnt

            rkg_loads(0)
            uz_load0()
            for piece in range(4):
                prep_piece(0, piece)
            for b in range(NB):
                if b + 1 < NB:
                    rkg_loads(b + 1)
                for ou in range(4):
                    comp_ou(b, ou)
                    y_spill(b, ou)
                    if b + 1 < NB:
                        uz_load(b + 1, ou)
                        prep_piece(b + 1, ou)


            if _os.environ.get("KERNEL_NOS3") == "1":
                Sx.wait(c_st)
                Sx.wait(c_sp)
                return
            hb_row4 = ar(s4_LT, 2048, np_=1)
            Sx.wait(c_misc)
            Sx.wait(c_gp)
            Sx.wait(c_ve)
            Sx.wait(c_ac)
            Sx.wait(c_pe)
            Sx.dma_start(out=hb_row4,
                         in_=fbias.rearrange("(one x) -> one x", one=1),
                         inc=c_misc, k=16)
            Ac.wait(c_misc)
            Ac.wait(c_ve)
            Ac.activation(hb_row4, hb_row4, AF.Copy, scale=0.5, inc=c_ac)
            Tn.wait(c_ac)
            bcast_row(hb_row4, s4_hbb)
            hbb_ve = c_ve.cnt
            G.wait(c_ve)
            G.dma_start(out=sc_in[:, 0:2048], in_=ar(s2_S, 2048, np_=64),
                        inc=c_gd, k=16)
            G.wait(c_gd)
            G.dma_start(out=sc_in[:, 2048:2049], in_=ar(p_acar, 1, np_=64),
                        allow_slow_non_contiguous=True, inc=c_gd, k=16)
            G.wait(c_gd)
            G.collective_compute("AllGather", OP.bypass,
                                 replica_groups=[list(range(NCORE))],
                                 ins=[sc_in.opt()], outs=[sc_out.opt()],
                                 inc=c_cc, k=1)
            Sx.wait(c_cc)
            Sx.dma_start(out=ar(s4_A, 64, np_=8),
                         in_=sc_out[:, 2048:2049].rearrange(
                             "(i m) one -> i (m one)", m=M),
                         allow_slow_non_contiguous=True, inc=c_sc, k=16)
            Tn.wait(c_sc)
            Tn.wait(c_ve)
            Tn.matmul(pst[0][0:8, 0:64], lhsT=ar(p_small, 8, np_=8),
                      rhs=ar(s4_A, 64, np_=8), start=True, stop=True, inc=c_pe)
            Ac.wait(c_pe)
            Ac.activation(ar(s4_w8, 64, np_=8), pst[0][0:8, 0:64], AF.Exp,
                          inc=c_ac)
            V.wait(c_ac)
            V.tensor_scalar(ar(s4_w8, 64, np_=8), ar(s4_w8, 64, np_=8),
                            ar(p_small + 8, 1, np_=8), None, op0=OP.mult,
                            inc=c_ve)
            Tn.wait(c_ve)
            Tn.transpose(pst[1][0:64, 0:8], ar(s4_w8, 64, np_=8),
                         ar(p_ident, 128)[0:8, 0:8], inc=c_pe)
            V.wait(c_pe)
            V.tensor_copy(ar(s4_wT, 8, np_=64), pst[1][0:64, 0:8], inc=c_ve)
            V.memset(ar(s4_Sst, 2048, np_=64), 0.0, inc=c_ve)
            Sx.wait(c_ve)
            Sx.wait(c_cc)
            Sx.wait(c_st)
            Sx.wait(c_pe)
            Sx.dma_start(out=ar(s4_Y, 8 * 2048, np_=64).rearrange(
                             "m (i x) -> m i x", x=2048),
                         in_=sc_out[:, 0:2048].rearrange(
                             "(i m) x -> m i x", m=M),
                         inc=c_sc, k=16)
            l_sc = c_sc.cnt
            sst = [s4_Sst, s4_Sst2]
            V.wait(c_sc, l_sc)
            for i in range(8):
                V.drain()
                V.scalar_tensor_tensor(ar(sst[(i + 1) % 2], 2048, np_=64),
                                       ar(s4_Y + 2048 * i, 2048, np_=64),
                                       ar(s4_wT + i, 1, np_=64),
                                       ar(sst[i % 2], 2048, np_=64),
                                       op0=OP.mult, op1=OP.add, inc=c_ve)
            sstart_col = sst[8 % 2]
            sst_ve = c_ve.cnt

            if _os.environ.get("KERNEL_NOS4") == "1":
                Sx.wait(c_st)
                Sx.wait(c_gd)
                return
            wfree2 = [0, 0]
            wdma2 = {}
            ffn_evac2 = [0, 0]
            oev = [0, 0]
            yfree = [0, 0]
            lffree2 = [0, 0]
            Y_2 = [s4_Y, s4_Y + 4096]
            LF_2 = [s4_LF, s4_LF + 4096]
            def w_issue2(d):
                if d >= 16:
                    return
                Sx.wait(c_pe, wfree2[d % 2])
                cwx = c_wa if d % 2 == 0 else c_wb
                e = d // 4
                dc0 = (4 * d) % 16
                for i2 in range(2):
                    Sx.dma_start(
                        out=ar(s4_W + 2048 * (d % 2), 2048).rearrange(
                            "p (c xi) -> p c xi", xi=512
                            )[:, :, 256 * i2:256 * i2 + 256],
                        in_=wq[e][:, dc0:dc0 + 4, i2],
                        inc=cwx, k=16)
                wdma2[d] = (cwx, cwx.cnt)

            def ffn2_half(sl, grp, es, lf_ve2):
                for e in es:
                    for dc in range(16):
                        q = 16 * e + dc
                        d = q // 4
                        slot = q % 8
                        wc, wv = wdma2[d]
                        Tn.wait(wc, wv)
                        if dc == 0:
                            Tn.wait(c_ve, lf_ve2)
                            Tn.wait(c_ve, ffn_evac2[e % 2])
                        for gi in range(len(grp)):
                            Tn.matmul(pst[2 * (e % 2) + gi],
                                      lhsT=ar(LF_2[sl] + 2048 * gi + 128 * dc,
                                              128),
                                      rhs=ar(s4_W + 512 * slot, 512),
                                      start=(dc == 0), stop=(dc == 15),
                                      inc=c_pe if gi == len(grp) - 1 else None)
                        if slot == 7 or slot == 3:
                            wfree2[(q // 4) % 2] = c_pe.cnt
                            w_issue2(q // 4 + 2)
                    for gi in range(len(grp)):
                        V.wait(c_pe)
                        V.drain()
                        V.scalar_tensor_tensor(
                            ar(Y_2[sl] + 2048 * gi + 512 * e, 512),
                            pst[2 * (e % 2) + gi], 1.0,
                            ar(Y_2[sl] + 2048 * gi + 512 * e, 512),
                            op0=OP.mult, op1=OP.add, inc=c_ve)
                    ffn_evac2[e % 2] = c_ve.cnt

            grps2 = [[2 * g, 2 * g + 1] for g in range(NT // 2)]
            Tn.wait(c_gp, rhat_gp[0])
            Sx.wait(c_st)
            Sx.wait(c_ve, sst_ve)
            for gidx, grp in enumerate(grps2):
                sl = gidx % 2
                Sx.wait(c_gd, yfree[sl])
                for gi, tl in enumerate(grp):
                    Sx.dma_start(out=ar(Y_2[sl] + 2048 * gi, 2048),
                                 in_=ypart_hbm[128 * tl:128 * (tl + 1), :],
                                 inc=c_y, k=16)
                y_ld = c_y.cnt
                oevt = [0, 0]
                for gi, tl in enumerate(grp):
                    Tn.wait(c_y, y_ld)
                    Tn.wait(c_ve)
                    for dt in range(4):
                        po = pst[4] if dt % 2 == 0 else pst[6]
                        Tn.wait(c_ve, oev[dt % 2])
                        Tn.matmul(po, lhsT=ar(p_rhat + 128 * tl, 128, np_=64),
                                  rhs=ar(sstart_col + 512 * dt, 512, np_=64),
                                  start=True, stop=True, inc=c_pe)
                        V.wait(c_pe)
                        V.wait(c_ve, sst_ve)
                        V.wait(c_y, y_ld)
                        V.drain()
                        V.scalar_tensor_tensor(
                            ar(Y_2[sl] + 2048 * gi + 512 * dt, 512), po, SCALE,
                            ar(Y_2[sl] + 2048 * gi + 512 * dt, 512),
                            op0=OP.mult, op1=OP.add, inc=c_ve)
                        oev[dt % 2] = c_ve.cnt
                    oevt[gi] = c_ve.cnt
                for gi, tl in enumerate(grp):
                    st = p_stat + 32 * (gi % 2)
                    xap = ar(Y_2[sl] + 2048 * gi, 2048)
                    V.wait(c_ve, oevt[gi])
                    V.drain()
                    for j in range(4):
                        V.bn_stats(ar(st + 6 * j, 6),
                                   xap[:, 512 * j:512 * (j + 1)])
                    V.drain()
                    V.bn_aggr(ar(st + 24, 2), ar(st, 24), inc=c_ve)
                st0 = p_stat
                st1 = p_stat + 32
                Ac.wait(c_ve)
                Ac.drain()
                Ac.activation(ar(st0 + 26, 1), ar(st0 + 25, 1), AF.Ln,
                              bias=eps_t, scale=1.0)
                Ac.activation(ar(st1 + 26, 1), ar(st1 + 25, 1), AF.Ln,
                              bias=eps_t, scale=1.0)
                Ac.drain()
                Ac.activation(ar(st0 + 26, 1), ar(st0 + 26, 1), AF.Exp,
                              scale=-0.5)
                Ac.activation(ar(st1 + 26, 1), ar(st1 + 26, 1), AF.Exp,
                              scale=-0.5, inc=c_ac)
                V.wait(c_ac)
                for gi, tl in enumerate(grp):
                    st = p_stat + 32 * (gi % 2)
                    xap = ar(Y_2[sl] + 2048 * gi, 2048)
                    V.wait(c_pe, lffree2[sl])
                    V.drain()
                    V.tensor_scalar(ar(st + 26, 1), ar(st + 26, 1), C1, None,
                                    op0=OP.mult, inc=c_ve)
                    V.drain()
                    V.tensor_scalar(ar(st + 27, 1), ar(st + 24, 1),
                                    ar(st + 26, 1), -1.0,
                                    op0=OP.mult, op1=OP.mult, inc=c_ve)
                    V.drain()
                    V.tensor_scalar(ar(s4_LT, 2048), xap,
                                    ar(st + 26, 1), ar(st + 27, 1),
                                    op0=OP.mult, op1=OP.add, inc=c_ve)
                    ln_ve = c_ve.cnt
                    Tn.wait(c_ve, ln_ve)
                    transpose_tile(s4_LT, LF_2[sl] + 2048 * gi, 128, 0)
                lf_ve2 = c_ve.cnt
                w_issue2(0)
                w_issue2(1)
                ffn2_half(sl, grp, [0, 1], lf_ve2)
                ffn2_half(sl, grp, [2, 3], lf_ve2)
                lffree2[sl] = c_pe.cnt
                for gi, tl in enumerate(grp):
                    G.wait(c_ve)
                    G.wait(c_ve, hbb_ve)
                    yt = ar(Y_2[sl] + 2048 * gi, 2048)
                    G.tensor_tensor(yt, yt, ar(s4_hbb, 2048), op=OP.add,
                                    inc=c_gp)
                    G.drain()
                    G.wait(c_gd)
                    G.dma_start(out=yout[128 * tl:128 * (tl + 1), :],
                                in_=yt, inc=c_gd, k=16)
                yfree[sl] = c_gd.cnt
            out_gd = c_gd.cnt
            Sx.wait(c_gd, out_gd)

        @block.sync
        def _(eng):
            for c in ctrs:
                c.cnt = 0
            Sx.real = eng
            schedule()
            Sx.real = None

        @block.vector
        def _(eng):
            for c in ctrs:
                c.cnt = 0
            V.real = eng
            schedule()
            V.real = None

        @block.tensor
        def _(eng):
            for c in ctrs:
                c.cnt = 0
            Tn.real = eng
            schedule()
            Tn.real = None

        @block.scalar
        def _(eng):
            for c in ctrs:
                c.cnt = 0
            Ac.real = eng
            schedule()
            Ac.real = None

        @block.gpsimd
        def _(eng):
            for c in ctrs:
                c.cnt = 0
            G.real = eng
            schedule()
            G.real = None

    return nc


_NC_CACHE = [None]


def kernel(L0_attn_hidden_states, L0_attn_conv_w, L0_attn_conv_b,
           L0_attn_in_proj_w, L0_attn_ch_gate_w0, L0_attn_ch_gate_w1,
           L0_attn_ch_gate_b1, L0_attn_mem_gate_w, L0_attn_mem_gate_b,
           L0_attn_residual_weight, L0_ffn_factors, L0_ffn_bias):
    f32 = np.float32
    x = np.ascontiguousarray(np.asarray(L0_attn_hidden_states, f32)[0])
    factors = np.ascontiguousarray(np.asarray(L0_ffn_factors, f32))
    ipT = np.ascontiguousarray(np.asarray(L0_attn_in_proj_w, f32).T)
    mgT = np.ascontiguousarray(np.asarray(L0_attn_mem_gate_w, f32).T)
    ch0T = np.ascontiguousarray(np.asarray(L0_attn_ch_gate_w0, f32).T)
    ch1rT = np.ascontiguousarray(np.asarray(L0_attn_ch_gate_w1, f32).T)
    convw = np.ascontiguousarray(np.asarray(L0_attn_conv_w, f32))
    convb = np.ascontiguousarray(np.asarray(L0_attn_conv_b, f32))
    mgb = np.ascontiguousarray(np.asarray(L0_attn_mem_gate_b, f32).reshape(2, 64).T)
    resw = np.ascontiguousarray(np.asarray(L0_attn_residual_weight, f32))
    ch1b = np.ascontiguousarray(np.asarray(L0_attn_ch_gate_b1, f32))
    fbias = np.ascontiguousarray(np.asarray(L0_ffn_bias, f32))

    ident = np.eye(128, dtype=f32)
    mask32 = np.tile(np.triu(np.ones((32, 32), f32)), (4, 1))
    ones_t = np.ones((128, 512), f32)

    if _NC_CACHE[0] is None:
        _NC_CACHE[0] = build_nc()
    nc = _NC_CACHE[0]

    in_maps = []
    for c in range(NCORE):
        s0 = c * SEG
        xp = np.zeros(((NT + 1) * 128, D), f32)
        if c > 0:
            xp[0:3] = x[s0 - 3:s0]
        xp[128:128 + SEG] = x[s0:s0 + SEG]
        fsl = np.ascontiguousarray(factors[:, :, ESL * c:ESL * (c + 1)])
        mj = np.zeros((8, 8), f32)
        sel = np.zeros((8, 1), f32)
        for i in range(c):
            sel[i, 0] = 1.0
            for l in range(i + 1, c):
                mj[l, i] = 1.0
        in_maps.append(dict(
            xin=xp, fsl=fsl, ipT=ipT, mgT=mgT, ch0T=ch0T, ch1rT=ch1rT,
            convw=convw, convb=convb, mgb=mgb, resw=resw, ch1b=ch1b,
            fbias=fbias, identi=ident, mask32i=mask32, onesi=ones_t,
            mji=mj, seli=sel))

    res = run_bass_kernel_spmd(nc, in_maps, core_ids=list(range(NCORE)))
    out = np.concatenate([r["yout"] for r in res.results], axis=0)
    return out[None, :, :]
